# version 19
# baseline (speedup 1.0000x reference)
import numpy as np
import ml_dtypes
from contextlib import ExitStack
import concourse.bass as bass
import concourse.mybir as mybir
from concourse.bass_utils import run_bass_kernel_spmd

BF = ml_dtypes.bfloat16
F32 = mybir.dt.float32
BF16 = mybir.dt.bfloat16
AF = mybir.ActivationFunctionType
OP = mybir.AluOpType
AX = mybir.AxisListType

D = 2048
H, DK, DV, M = 16, 64, 128, 64
HF = 8192
RND = 12582912.0
SCALE = 0.125


class Bld:
    ENGS = ("t", "v", "s", "g", "d")

    def __init__(self):
        self.q = {e: [] for e in self.ENGS}
        self.cnt = {e: 0 for e in self.ENGS}
        self.lastwait = {e: {} for e in self.ENGS}
        self.pend = {e: [] for e in self.ENGS}

    def _need(self, e, deps):
        need = {}
        for d_ in deps:
            if d_ is None:
                continue
            se, v = d_
            if se == e and v >= self.cnt[e]:
                v = self.cnt[e]
            if v > self.lastwait[e].get(se, 0) and v > need.get(se, 0):
                need[se] = v
        for se, v in need.items():
            self.lastwait[e][se] = v
        return sorted(need.items())

    def op(self, e, fn, deps=(), dma=False):
        deps = list(deps) + self.pend[e]
        self.pend[e] = []
        waits = self._need(e, deps)
        inc = 16 if dma else 1
        self.cnt[e] += inc
        self.q[e].append((waits, fn, inc))
        return (e, self.cnt[e])

    def dma(self, qe, out, in_, deps=()):
        def fn(eng, out=out, in_=in_):
            return eng.dma_start(out=out, in_=in_)
        return self.op(qe, fn, deps, dma=True)

    def waitop(self, e, deps):
        deps = list(deps) + self.pend[e]
        self.pend[e] = []
        waits = self._need(e, deps)
        self.q[e].append((waits, None, 0))

    def barrier(self):
        evs = [(e, self.cnt[e]) for e in self.ENGS if self.cnt[e] > 0]
        for e in self.ENGS:
            self.pend[e] = list({(se, v) for (se, v) in (self.pend[e] + evs)})


class Arena:
    def __init__(self, bf_ap, f32_ap, nbytes):
        self.bf = bf_ap
        self.f32 = f32_ap
        self.nbytes = nbytes
        self.off = 0

    def reset(self):
        self.off = 0

    def _take(self, nbytes):
        o = self.off
        self.off = o + ((nbytes + 3) // 4) * 4
        assert self.off <= self.nbytes, f"arena overflow {self.off} > {self.nbytes}"
        return o

    def abf(self, parts, n):
        o = self._take(2 * n)
        return self.bf[0:parts, o // 2: o // 2 + n]

    def af32(self, parts, n):
        o = self._take(4 * n)
        return self.f32[0:parts, o // 4: o // 4 + n]


def build(TOK=4096, PRE=4096, dbg=()):
    nc = bass.Bass("TRN2", target_bir_lowering=False, debug=False)

    def din(name, shape, dt):
        return nc.dram_tensor(name, shape, dt, kind="ExternalInput").ap()

    def dscr(name, shape, dt):
        kind = "ExternalOutput" if name in dbg else "Internal"
        return nc.dram_tensor(name, shape, dt, kind=kind).ap()

    x_own = din("x_own", [TOK, D], F32)
    x_pre = din("x_pre", [PRE, D], F32)
    pbias = din("pbias", [128, 1], F32)
    wq = din("wq", [D, 1024], F32)
    wk = din("wk", [D, 1024], F32)
    wv = din("wv", [D, 2048], F32)
    wg = din("wg", [D, 2048], F32)
    ws = din("ws", [D, 1024], F32)
    wo = din("wo", [2048, D], F32)
    wgu = din("wgu", [2 * HF, D], F32)
    wp = din("wp", [D, HF], F32)
    gnw = din("gnw", [1, 128], F32)
    umask_in = din("umask", [128, 128], BF16)
    eye_in = din("eye", [128, 128], BF16)

    out = nc.dram_tensor("out", [TOK, D], F32, kind="ExternalOutput").ap()

    wq_b = dscr("wq_b", [D, 1024], BF16)
    wk_b = dscr("wk_b", [D, 1024], BF16)
    ws_b = dscr("ws_b", [D, 1024], BF16)
    wv_b = dscr("wv_b", [D, 2048], BF16)
    wg_b = dscr("wg_b", [D, 2048], BF16)
    wo_b = dscr("wo_b", [2048, D], BF16)
    wgu_q = dscr("wgu_q", [2 * HF, D], BF16)
    wp_q = dscr("wp_q", [D, HF], BF16)
    xn_own = dscr("xn_own", [D, TOK], BF16)
    xn_pre = dscr("xn_pre", [D, PRE], BF16)
    h_scr = dscr("h_scr", [HF, TOK], BF16)
    att_scr = dscr("att_scr", [TOK, D], F32)

    bld = Bld()
    st_own = TOK // 512
    st_pre = PRE // 512

    with ExitStack() as stack:
        sb = stack.enter_context
        cU = sb(nc.sbuf_tensor("cU", [128, 128], BF16))
        cEye = sb(nc.sbuf_tensor("cEye", [128, 128], BF16))
        cU4 = sb(nc.sbuf_tensor("cU4", [128, 512], BF16))
        cOnes1 = sb(nc.sbuf_tensor("cOnes1", [1, 128], F32))
        cOne11 = sb(nc.sbuf_tensor("cOne11", [1, 1], F32))
        cGnw = sb(nc.sbuf_tensor("cGnw", [128, 2048], BF16))
        cBias32 = sb(nc.sbuf_tensor("cBias32", [128, 1], F32))
        cOnesColBf = sb(nc.sbuf_tensor("cOnesColBf", [128, 1], BF16))
        cRND = sb(nc.sbuf_tensor("cRND", [128, 1], F32))
        cRNDn = sb(nc.sbuf_tensor("cRNDn", [128, 1], F32))
        cPBias = sb(nc.sbuf_tensor("cPBias", [128, 1], F32))
        hk_sb = sb(nc.sbuf_tensor("hk_sb", [64, 1024], F32))
        hk_bf = sb(nc.sbuf_tensor("hk_bf", [64, 1024], BF16))
        hv_sb = sb(nc.sbuf_tensor("hv_sb", [64, 2048], F32))
        hv_bf = sb(nc.sbuf_tensor("hv_bf", [64, 2048], BF16))
        s_prev = sb(nc.sbuf_tensor("s_prev", [1, 1024], F32))
        rmsacc = sb(nc.sbuf_tensor("rmsacc", [128, TOK // 128], F32))
        rstdt = sb(nc.sbuf_tensor("rstdt", [128, TOK // 128], F32))
        ARN = 178 * 1024
        arena_t = sb(nc.sbuf_tensor("arena", [128, ARN // 2], BF16))
        ar = Arena(arena_t[:], arena_t[:].bitcast(F32), ARN)
        ps = [sb(nc.psum_tensor(f"ps{i}", [128, 512], F32)) for i in range(6)]
        psT = [sb(nc.psum_tensor(f"psT{i}", [128, 1024], BF16)) for i in range(2)]
        sems = {e: sb(nc.semaphore(f"sem_{e}")) for e in Bld.ENGS}

        ev_u = bld.dma("d", cU[:], umask_in)
        ev_e = bld.dma("d", cEye[:], eye_in)
        gn_row = ar.af32(1, 128)
        ev_gn = bld.dma("d", gn_row, gnw)
        ev_pb = bld.dma("d", cPBias[:], pbias)
        bld.op("v", lambda v: v.memset(cOnes1[:], 1.0))
        bld.op("v", lambda v: v.memset(cOne11[:], 1.0))
        bld.op("v", lambda v: v.memset(cBias32[:], -32.0))
        bld.op("v", lambda v: v.memset(cOnesColBf[:], 1.0))
        bld.op("v", lambda v: v.memset(cRND[:], RND))
        bld.op("v", lambda v: v.memset(cRNDn[:], -RND))
        bld.op("v", lambda v: v.memset(s_prev[:], 0.0))
        bld.op("v", lambda v: v.memset(hk_sb[:], 0.0))
        bld.op("v", lambda v: v.memset(hk_bf[:], 0.0))
        bld.op("v", lambda v: v.memset(hv_sb[:], 0.0))
        bld.op("v", lambda v: v.memset(hv_bf[:], 0.0))
        bld.op("v", lambda v: v.memset(rmsacc[:], 0.0))
        ev_u4 = bld.op("v", lambda v: v.tensor_copy(
            cU4[:].rearrange("p (r c) -> p r c", c=128),
            cU[:].rearrange("p c -> p () c").broadcast_to([128, 4, 128])), deps=[ev_u])
        ev_g1 = bld.op("t", lambda t: t.matmul(ps[0][:, 0:128], cOnes1[0:1, :], gn_row,
                                               start=True, stop=True), deps=[ev_gn])
        gtmp = ar.abf(128, 128)
        ev_g2 = bld.op("v", lambda v: v.tensor_copy(gtmp, ps[0][:, 0:128]), deps=[ev_g1])
        ev_g3 = bld.op("v", lambda v: v.tensor_copy(
            cGnw[:].rearrange("p (r c) -> p r c", c=128),
            gtmp.rearrange("p c -> p () c").broadcast_to([128, 16, 128])), deps=[ev_g2])
        bld.barrier()
        ar.reset()

        wf = [ar.af32(128, 1024) for _ in range(2)]
        wbb = [ar.abf(128, 1024) for _ in range(2)]
        cast_jobs = []
        for (src, dst, width) in ((wq, wq_b, 1024), (wk, wk_b, 1024), (ws, ws_b, 1024),
                                  (wv, wv_b, 2048), (wg, wg_b, 2048), (wo, wo_b, 2048)):
            for r in range(16):
                for cc in range(width // 1024):
                    cast_jobs.append((src, dst, r, cc))
        prev_read = [None, None]
        prev_st = [None, None]
        for i, (src, dst, r, cc) in enumerate(cast_jobs):
            sl = i % 2
            evl = bld.dma("d", wf[sl], src[r * 128:(r + 1) * 128, cc * 1024:(cc + 1) * 1024],
                          deps=[prev_read[sl]])
            evc = bld.op("s", lambda s, sl=sl: s.copy(wbb[sl], wf[sl]), deps=[evl, prev_st[sl]])
            prev_read[sl] = evc
            evs_ = bld.dma("g", dst[r * 128:(r + 1) * 128, cc * 1024:(cc + 1) * 1024], wbb[sl],
                           deps=[evc])
            prev_st[sl] = evs_
        bld.barrier()
        ar.reset()

        def mk_qslots(width, nsl):
            out = []
            for _ in range(nsl):
                out.append(dict(
                    wb=ar.abf(128, width), sc=ar.af32(128, width // 64),
                    scb=ar.abf(128, width // 64), scf2=ar.af32(128, width // 64),
                    rin=ar.af32(128, width // 64), q1=ar.abf(128, width),
                    wh=ar.abf(128, width)))
            qtmp = ar.af32(128, width)
            for o in out:
                o["qtmp"] = qtmp
            return out

        def quant_front(wf32, width, bufs, deps):
            ng = width // 64
            e1 = bld.op("s", lambda s: s.copy(bufs["wb"], wf32), deps=deps)
            e2 = bld.op("v", lambda v: v.tensor_reduce(
                bufs["sc"], bufs["wb"].rearrange("p (g k) -> p g k", k=64), AX.X, OP.add,
                apply_absolute_value=True), deps=[e1])
            e3 = bld.op("v", lambda v: v.tensor_scalar(bufs["scb"], bufs["sc"], 1.0 / 64, 1e-8,
                                                       OP.mult, OP.max), deps=[e2])
            e3b = bld.op("s", lambda s: s.copy(bufs["scf2"], bufs["scb"]), deps=[e3])
            e4 = bld.op("v", lambda v: v.reciprocal(bufs["rin"], bufs["scf2"]), deps=[e3b])
            return e4

        def q_stage_quo(width, bufs, front_ev):
            ng = width // 64
            return bld.op("v", lambda v: v.tensor_tensor(
                bufs["q1"].rearrange("p (g k) -> p g k", k=64),
                bufs["wb"].rearrange("p (g k) -> p g k", k=64),
                bufs["rin"].rearrange("p g -> p g ()").broadcast_to([128, ng, 64]), OP.mult),
                deps=[front_ev])

        def q_stage_round(bufs, quo_ev, qtmp_war):
            e6a = bld.op("s", lambda s: s.activation(bufs["qtmp"], bufs["q1"], AF.Identity,
                                                     bias=cRND[:], scale=1.0),
                         deps=[quo_ev, qtmp_war])
            e6b = bld.op("s", lambda s: s.activation(bufs["q1"], bufs["qtmp"], AF.Identity,
                                                     bias=cRNDn[:], scale=1.0), deps=[e6a])
            return e6b

        def q_stage_fin(width, bufs, round_ev, war_wh):
            ng = width // 64
            e7 = bld.op("v", lambda v: v.tensor_scalar(bufs["q1"], bufs["q1"], 1.0, -1.0,
                                                       OP.min, OP.max), deps=[round_ev])
            e8 = bld.op("v", lambda v: v.tensor_tensor(
                bufs["wh"].rearrange("p (g k) -> p g k", k=64),
                bufs["q1"].rearrange("p (g k) -> p g k", k=64),
                bufs["scb"].rearrange("p g -> p g ()").broadcast_to([128, ng, 64]), OP.mult),
                deps=[e7, war_wh])
            return e8

        NSL = 4
        wfq = [ar.af32(128, 2048) for _ in range(NSL)]
        qs = mk_qslots(2048, NSL)
        jobs = [("gu", i) for i in range(128)] + [("wp", i) for i in range(64)]
        NJ = len(jobs)

        def q_src(job):
            kind, idx = job
            if kind == "gu":
                return wgu[idx * 128:(idx + 1) * 128, :]
            r, cc = idx // 4, idx % 4
            return wp[r * 128:(r + 1) * 128, cc * 2048:(cc + 1) * 2048]

        def q_dst(job):
            kind, idx = job
            if kind == "gu":
                return wgu_q[idx * 128:(idx + 1) * 128, :]
            r, cc = idx // 4, idx % 4
            return wp_q[r * 128:(r + 1) * 128, cc * 2048:(cc + 1) * 2048]

        ld_ev = {}
        fr_ev = {}
        quo_ev = {}
        rnd_ev = {}
        st_ev = {}
        qtmp_last = [None]
        qn = [0]

        def prep_step():
            n = qn[0]
            if n >= NJ + 4:
                return False
            qn[0] += 1
            if n < NJ:
                sl = n % NSL
                ld_ev[n] = bld.dma("d", wfq[sl], q_src(jobs[n]),
                                   deps=[fr_ev.get(n - NSL)])
            if n >= 1 and n - 1 < NJ:
                k = n - 1
                fr_ev[k] = quant_front(wfq[k % NSL], 2048, qs[k % NSL], [ld_ev[k]])
            if n >= 2 and n - 2 < NJ:
                k = n - 2
                quo_ev[k] = q_stage_quo(2048, qs[k % NSL], fr_ev[k])
            if n >= 3 and n - 3 < NJ:
                k = n - 3
                rnd_ev[k] = q_stage_round(qs[k % NSL], quo_ev[k], qtmp_last[0])
                qtmp_last[0] = rnd_ev[k]
            if n >= 4 and n - 4 < NJ:
                k = n - 4
                e8 = q_stage_fin(2048, qs[k % NSL], rnd_ev[k], st_ev.get(k - NSL))
                st_ev[k] = bld.dma("g", q_dst(jobs[k]), qs[k % NSL]["wh"], deps=[e8])
            return True

        _lnb = dict(
            xb=[ar.af32(128, 2048) for _ in range(2)],
            sqs=ar.af32(128, 2048), sqacc=ar.af32(128, 2), mcol=ar.af32(128, 2),
            m2c=ar.af32(128, 1), vv=ar.af32(128, 1), rr=ar.af32(128, 1),
            xnt=[ar.abf(128, 2048) for _ in range(2)], xnasm=ar.abf(128, 16 * 512))

        def ln_pass(x_src, xn_dst, ntok, filler=None):
            xb = _lnb["xb"]
            sqs = _lnb["sqs"]
            sqacc = _lnb["sqacc"]
            mcol = _lnb["mcol"]
            m2c = _lnb["m2c"]
            vv = _lnb["vv"]
            rr = _lnb["rr"]
            xnt = _lnb["xnt"]
            xnasm = _lnb["xnasm"]
            prev_rd = [ln_pass.carry_rd[0], ln_pass.carry_rd[1]]
            prev_tr = [ln_pass.carry_tr[0], ln_pass.carry_tr[1]]
            asm_dma = [ln_pass.carry_asm]
            tevl = [ln_pass.carry_tev[0], ln_pass.carry_tev[1]]
            fr = {}

            def ln_front(i):
                sl = i % 2
                evl = bld.dma("d", xb[sl], x_src[i * 128:(i + 1) * 128, :], deps=[prev_rd[sl]])
                e1 = bld.op("s", lambda s, sl=sl: s.activation(sqs, xb[sl], AF.Square,
                                                               accum_out=sqacc[:, sl:sl + 1]),
                            deps=[evl, fr.get(("sq", i - 2))])
                e2 = bld.op("v", lambda v, sl=sl: v.tensor_reduce(mcol[:, sl:sl + 1], xb[sl],
                                                                  AX.X, OP.add),
                            deps=[evl, fr.get(("mc", i - 2))])
                fr[("e1", i)] = e1
                fr[("e2", i)] = e2

            def ln_back(i):
                sl = i % 2
                e1 = fr.pop(("e1", i))
                e2 = fr.pop(("e2", i))
                e3 = bld.op("v", lambda v, sl=sl: v.tensor_scalar(mcol[:, sl:sl + 1], mcol[:, sl:sl + 1],
                                                                  1.0 / 2048, None, OP.mult), deps=[e2])
                e4 = bld.op("v", lambda v, sl=sl: v.tensor_scalar(vv, sqacc[:, sl:sl + 1],
                                                                  1.0 / 2048, None, OP.mult), deps=[e1])
                e5 = bld.op("v", lambda v, sl=sl: v.tensor_tensor(m2c, mcol[:, sl:sl + 1],
                                                                  mcol[:, sl:sl + 1], OP.mult), deps=[e3])
                e6 = bld.op("v", lambda v: v.tensor_tensor(vv, vv, m2c, OP.subtract), deps=[e4, e5])
                e7 = bld.op("v", lambda v: v.tensor_scalar(vv, vv, 1e-5, None, OP.add), deps=[e6])
                e8 = bld.op("s", lambda s: s.activation(vv, vv, AF.Sqrt), deps=[e7])
                e9v = bld.op("v", lambda v: v.reciprocal(rr, vv), deps=[e8])
                e9 = bld.op("v", lambda v, sl=sl: v.tensor_scalar(
                    xnt[sl], xb[sl], mcol[:, sl:sl + 1], rr, OP.subtract, OP.mult),
                    deps=[e9v, e3, prev_tr[sl]])
                fr[("sq", i)] = e9
                fr[("mc", i)] = e9
                prev_rd[sl] = e9
                tcol = (i % 4) * 128
                evs16 = []
                for j in range(16):
                    tb = (i * 16 + j) % 2
                    evt = bld.op("t", lambda t, j=j, tb=tb, sl=sl: t.transpose(
                        psT[tb][:, (j % 8) * 128:(j % 8) * 128 + 128],
                        xnt[sl][:, j * 128:(j + 1) * 128], cEye[:]),
                        deps=[e9, tevl[tb]])
                    eng = "v" if j % 2 == 0 else "s"
                    def evac(e, j=j, tb=tb, tcol=tcol):
                        dstv = xnasm[:, j * 512 + tcol: j * 512 + tcol + 128]
                        srcv = psT[tb][:, (j % 8) * 128:(j % 8) * 128 + 128]
                        return e.tensor_copy(dstv, srcv) if j % 2 == 0 else e.copy(dstv, srcv)
                    eve = bld.op(eng, evac, deps=[evt, asm_dma[0]])
                    tevl[tb] = eve
                    evs16.append(eve)
                prev_tr[sl] = evs16[-1]
                if i % 4 == 3:
                    g0 = (i // 4) * 512
                    devs = []
                    for j in range(16):
                        devs.append(bld.dma("g", xn_dst[j * 128:(j + 1) * 128, g0:g0 + 512],
                                            xnasm[:, j * 512:(j + 1) * 512], deps=evs16))
                    asm_dma[0] = devs[-1]

            nt = ntok // 128
            for i in range(nt + 1):
                if i < nt:
                    ln_front(i)
                if i >= 1:
                    ln_back(i - 1)
                    if filler is not None:
                        for _ in range(6):
                            filler()
            ln_pass.carry_rd = prev_rd
            ln_pass.carry_tr = prev_tr
            ln_pass.carry_asm = asm_dma[0]
            ln_pass.carry_tev = tevl

        ln_pass.carry_rd = [None, None]
        ln_pass.carry_tr = [None, None]
        ln_pass.carry_asm = None
        ln_pass.carry_tev = [None, None]
        ln_pass(x_own, xn_own, TOK, prep_step)
        ln_pass(x_pre, xn_pre, PRE, prep_step)
        while prep_step():
            pass
        bld.barrier()
        ar.reset()

        pbank_ev = {i: None for i in range(6)}
        pbank_ev["psT0"] = None
        pbank_ev["psT1"] = None

        def proj_tm(xnb_ap, xnb_ev, wsrc, wwidth, wbufs, wbuf_ev, evac_fn):
            evs = []
            for ng in range(wwidth // 512):
                sl = ng % 2
                rr_ = wsrc[:, ng * 512:(ng + 1) * 512].rearrange("(dc p) n -> p dc n", p=128)
                weva = bld.dma("d", wbufs[sl][0].rearrange("p (dc n) -> p dc n", n=512),
                               rr_[:, 0:8, :], deps=[wbuf_ev[sl]])
                wevb = bld.dma("d", wbufs[sl][1].rearrange("p (dc n) -> p dc n", n=512),
                               rr_[:, 8:16, :], deps=[wbuf_ev[sl]])
                for tc in range(4):
                    bi = 4 + (ng * 4 + tc) % 2
                    bank = ps[bi]
                    def mmfn(t, tc=tc, sl=sl, bank=bank):
                        mm = None
                        for dc in range(16):
                            wv_ = wbufs[sl][dc // 8][:, (dc % 8) * 512:(dc % 8 + 1) * 512]
                            mm = t.matmul(bank[:],
                                          xnb_ap[:, dc * 512 + tc * 128: dc * 512 + tc * 128 + 128],
                                          wv_, start=(dc == 0), stop=(dc == 15))
                        return mm
                    evm = bld.op("t", mmfn, deps=[weva, wevb, xnb_ev, pbank_ev[bi]])
                    eve = evac_fn(tc, ng, bank, evm)
                    pbank_ev[bi] = eve
                    evs.append(eve)
                wbuf_ev[sl] = evs[-1]
            return evs

        def proj_fm(xnb_ap, xnb_ev, wsrc, fwidth, wbufs, wbuf_ev, evac_fn):
            evs = []
            for fh in range(fwidth // 512):
                sl = fh % 2
                rr_ = wsrc[:, fh * 512:(fh + 1) * 512].rearrange("(dc p) n -> p dc n", p=128)
                weva = bld.dma("d", wbufs[sl][0].rearrange("p (dc n) -> p dc n", n=512),
                               rr_[:, 0:8, :], deps=[wbuf_ev[sl]])
                wevb = bld.dma("d", wbufs[sl][1].rearrange("p (dc n) -> p dc n", n=512),
                               rr_[:, 8:16, :], deps=[wbuf_ev[sl]])
                for f128 in range(4):
                    bi = 4 + (fh * 4 + f128) % 2
                    bank = ps[bi]
                    def mmfn(t, f128=f128, sl=sl, bank=bank):
                        mm = None
                        for dc in range(16):
                            wv_ = wbufs[sl][dc // 8][:, (dc % 8) * 512 + f128 * 128:
                                                     (dc % 8) * 512 + f128 * 128 + 128]
                            mm = t.matmul(bank[:], wv_,
                                          xnb_ap[:, dc * 512:(dc + 1) * 512],
                                          start=(dc == 0), stop=(dc == 15))
                        return mm
                    evm = bld.op("t", mmfn, deps=[weva, wevb, xnb_ev, pbank_ev[bi]])
                    eve = evac_fn(fh, f128, bank, evm)
                    pbank_ev[bi] = eve
                    evs.append(eve)
                wbuf_ev[sl] = evs[-1]
            return evs

        ar.reset()
        xnb = ar.abf(128, 16 * 512)
        wbufs = [(ar.abf(128, 4096), ar.abf(128, 4096)) for _ in range(2)]
        a_tm = ar.abf(128, 4 * 1024)
        v_st = ar.abf(128, 4 * 2048)
        k_tm_st = ar.abf(128, 4 * 1024)
        stmp = [ar.af32(128, 512) for _ in range(2)]
        gs_tmp = ar.af32(1, 1024)
        wbuf_ev = [None, None]
        xnb_last = [None]
        hk_ev = [None]
        hv_ev = [None]
        sprev_ev = [None]
        stmp_ev = [None, None]
        a_war = {tc: None for tc in range(4)}
        a_wr = {}
        v_wr = {}
        k_wr = {}

        def emit_prefix_st(st, xn_src, bias_ap, bias_dep):
            devs = []
            for dc in range(16):
                devs.append(bld.dma("d", xnb[:, dc * 512:(dc + 1) * 512],
                                    xn_src[dc * 128:(dc + 1) * 128, st * 512:(st + 1) * 512],
                                    deps=[xnb_last[0]]))
            xnb_ev = devs[-1]

            def evac_s(tc, ng, bank, evm):
                sl2 = (tc * 2 + ng) % 2
                e1 = bld.op("v", lambda v, sl2=sl2, bank=bank: v.tensor_scalar(
                    stmp[sl2], bank[:], 32.0, -32.0, OP.min, OP.max),
                    deps=[evm, stmp_ev[sl2]])
                e2 = bld.op("s", lambda s, tc=tc, ng=ng, sl2=sl2: s.activation(
                    a_tm[:, tc * 1024 + ng * 512: tc * 1024 + ng * 512 + 512],
                    stmp[sl2], AF.Exp, bias=bias_ap, scale=1.0),
                    deps=[e1, bias_dep, a_war[tc]])
                stmp_ev[sl2] = e2
                a_wr[(tc, ng)] = e2
                return e2
            proj_tm(xnb, xnb_ev, ws_b, 1024, wbufs, wbuf_ev, evac_s)

            def evac_v(tc, ng, bank, evm):
                e = bld.op("v", lambda v, tc=tc, ng=ng, bank=bank: v.tensor_copy(
                    v_st[:, tc * 2048 + ng * 512: tc * 2048 + ng * 512 + 512], bank[:]),
                    deps=[evm, v_wr.get((tc, ng))])
                v_wr[(tc, ng)] = e
                return e
            proj_tm(xnb, xnb_ev, wv_b, 2048, wbufs, wbuf_ev, evac_v)

            def evac_k(tc, ng, bank, evm):
                e = bld.op("v", lambda v, tc=tc, ng=ng, bank=bank: v.tensor_copy(
                    k_tm_st[:, tc * 1024 + ng * 512: tc * 1024 + ng * 512 + 512], bank[:]),
                    deps=[evm, k_wr.get((tc, ng))])
                k_wr[(tc, ng)] = e
                return e
            evk = proj_tm(xnb, xnb_ev, wk_b, 1024, wbufs, wbuf_ev, evac_k)
            xnb_last[0] = evk[-1]

            sa_mm = [None, None]
            hk_mm = {}
            hv_mm = {}
            for tc in range(4):
                adep = [a_wr[(tc, 0)], a_wr[(tc, 1)]]
                vdep = [v_wr[(tc, ng)] for ng in range(4)]
                kdep = [k_wr[(tc, 0)], k_wr[(tc, 1)]]
                for half in range(2):
                    e = bld.op("t", lambda t, tc=tc, half=half: t.matmul(
                        psT[half][:].bitcast(F32)[0:1, 0:512], cOnesColBf[:],
                        a_tm[:, tc * 1024 + half * 512: tc * 1024 + (half + 1) * 512],
                        start=(tc == 0), stop=(tc == 3)),
                        deps=adep + ([pbank_ev[f"psT{half}"]] if tc == 0 else []))
                    sa_mm[half] = e
                for half in range(2):
                    bank = ps[4 + half]
                    def mmhk(t, tc=tc, half=half, bank=bank):
                        mm = None
                        for hh in range(8):
                            h_ = half * 8 + hh
                            mm = t.matmul(bank[0:64, hh * 64:(hh + 1) * 64],
                                          k_tm_st[:, tc * 1024 + h_ * 64: tc * 1024 + (h_ + 1) * 64],
                                          a_tm[:, tc * 1024 + h_ * 64: tc * 1024 + (h_ + 1) * 64],
                                          start=(tc == 0 and hh == 0), stop=(tc == 3 and hh == 7),
                                          skip_group_check=True)
                        return mm
                    hk_mm[half] = bld.op("t", mmhk, deps=adep + kdep +
                                         ([pbank_ev[4 + half]] if tc == 0 else []))
                for qtr in range(4):
                    bank = ps[qtr]
                    def mmhv(t, tc=tc, qtr=qtr, bank=bank):
                        mm = None
                        for hh in range(4):
                            h_ = qtr * 4 + hh
                            mm = t.matmul(bank[0:64, hh * 128:(hh + 1) * 128],
                                          a_tm[:, tc * 1024 + h_ * 64: tc * 1024 + (h_ + 1) * 64],
                                          v_st[:, tc * 2048 + h_ * 128: tc * 2048 + (h_ + 1) * 128],
                                          start=(tc == 0 and hh == 0), stop=(tc == 3 and hh == 3),
                                          skip_group_check=True)
                        return mm
                    hv_mm[qtr] = bld.op("t", mmhv, deps=adep + vdep +
                                        ([pbank_ev[qtr]] if tc == 0 else []))
                a_war[tc] = None
            for half in range(2):
                e = bld.op("v", lambda v, half=half: v.tensor_tensor(
                    s_prev[0:1, half * 512:(half + 1) * 512],
                    s_prev[0:1, half * 512:(half + 1) * 512],
                    psT[half][:].bitcast(F32)[0:1, 0:512], OP.add),
                    deps=[sa_mm[half], sprev_ev[0]])
                pbank_ev[f"psT{half}"] = e
                sprev_ev[0] = e
            for half in range(2):
                e1 = bld.op("v", lambda v, half=half: v.tensor_tensor(
                    hk_sb[:, half * 512:(half + 1) * 512], hk_sb[:, half * 512:(half + 1) * 512],
                    ps[4 + half][0:64, :], OP.add), deps=[hk_mm[half], hk_ev[0]])
                pbank_ev[4 + half] = e1
                hk_ev[0] = e1
            for qtr in range(4):
                e1 = bld.op("v", lambda v, qtr=qtr: v.tensor_tensor(
                    hv_sb[:, qtr * 512:(qtr + 1) * 512], hv_sb[:, qtr * 512:(qtr + 1) * 512],
                    ps[qtr][0:64, :], OP.add), deps=[hv_mm[qtr], hv_ev[0]])
                pbank_ev[qtr] = e1
                hv_ev[0] = e1

        for st in range(st_pre):
            emit_prefix_st(st, xn_pre, cPBias[:], ev_pb)
        hk_ev[0] = bld.op("v", lambda v: v.tensor_copy(hk_bf[:], hk_sb[:]), deps=[hk_ev[0]])
        hv_ev[0] = bld.op("v", lambda v: v.tensor_copy(hv_bf[:], hv_sb[:]), deps=[hv_ev[0]])
        bld.barrier()
        ar.reset()

        xnb = ar.abf(128, 16 * 512)
        wbufs = [(ar.abf(128, 4096), ar.abf(128, 4096)) for _ in range(2)]
        a_tm = ar.abf(128, 4 * 1024)
        v_st = ar.abf(128, 4 * 2048)
        o_st = ar.abf(128, 4 * 2048)
        og_fm = ar.abf(128, 16 * 512)
        qh = [ar.abf(64, 512) for _ in range(16)]
        kh = [ar.abf(64, 512) for _ in range(16)]
        stmp = [ar.af32(128, 512) for _ in range(2)]
        gs_tmp = ar.af32(1, 1024)
        srep_sb = ar.af32(128, 1024)
        esb = srep_sb
        sfull = ar.af32(128, 1024)
        sinv = ar.af32(128, 1024)
        kqc = ar.abf(128, 1024)
        okt = ar.af32(128, 1024)
        u_tm = ar.abf(128, 1024)
        ptc = kqc
        o2 = u_tm
        mx = ar.af32(128, 16)
        sm = ar.af32(128, 16)
        smr = ar.af32(128, 16)
        rr_st = ar.af32(128, 64)
        ogt = ar.abf(128, 512)
        sgt = stmp[0]
        sggt = kqc[:, 512:1024]
        afm = [ar.abf(64, 128) for _ in range(16)]
        ufm = [ar.abf(64, 128) for _ in range(16)]
        ktm = [ar.abf(128, 64) for _ in range(16)]
        attb = stmp

        ptbank_ev = [None, None]
        wbuf_ev = [None, None]
        xnb_last = [None]
        stmp_ev = [None, None]
        a_war = {tc: None for tc in range(4)}
        a_wr, v_wr, q_wr, k_wr = {}, {}, {}, {}
        og_wr = {}
        attb_ev = [None, None]
        cb = {"pk0": None, "pk1": None, "okt": None, "u": None,
              "sfull": None, "sinv": None, "srep": None, "mx": None, "sm": None, "smr": None,
              "ogt": None}
        afm_ev = [None] * 16
        ufm_ev = [None] * 16
        ktm_ev = [None] * 16
        rr_war = [None]
        o_wr = {}

        def emit_main_st(st):
            devs = []
            for dc in range(16):
                devs.append(bld.dma("d", xnb[:, dc * 512:(dc + 1) * 512],
                                    xn_own[dc * 128:(dc + 1) * 128, st * 512:(st + 1) * 512],
                                    deps=[xnb_last[0]]))
            xnb_ev = devs[-1]

            def evac_q(fh, f128, bank, evm):
                h0 = fh * 8 + f128 * 2
                bld.op("v", lambda v, h0=h0, bank=bank: v.tensor_copy(qh[h0][:], bank[0:64, :]),
                       deps=[evm, q_wr.get(h0)])
                e2 = bld.op("v", lambda v, h0=h0, bank=bank: v.tensor_copy(qh[h0 + 1][:], bank[64:128, :]),
                            deps=[evm, q_wr.get(h0 + 1)])
                q_wr[h0] = e2
                q_wr[h0 + 1] = e2
                return e2
            proj_fm(xnb, xnb_ev, wq_b, 1024, wbufs, wbuf_ev, evac_q)

            def evac_k2(fh, f128, bank, evm):
                h0 = fh * 8 + f128 * 2
                bld.op("v", lambda v, h0=h0, bank=bank: v.tensor_copy(kh[h0][:], bank[0:64, :]),
                       deps=[evm, k_wr.get(h0)])
                e2 = bld.op("v", lambda v, h0=h0, bank=bank: v.tensor_copy(kh[h0 + 1][:], bank[64:128, :]),
                            deps=[evm, k_wr.get(h0 + 1)])
                k_wr[h0] = e2
                k_wr[h0 + 1] = e2
                return e2
            proj_fm(xnb, xnb_ev, wk_b, 1024, wbufs, wbuf_ev, evac_k2)

            def evac_s(tc, ng, bank, evm):
                sl2 = (tc * 2 + ng) % 2
                e1 = bld.op("v", lambda v, sl2=sl2, bank=bank: v.tensor_scalar(
                    stmp[sl2], bank[:], 32.0, -32.0, OP.min, OP.max),
                    deps=[evm, stmp_ev[sl2]])
                e2 = bld.op("s", lambda s, tc=tc, ng=ng, sl2=sl2: s.activation(
                    a_tm[:, tc * 1024 + ng * 512: tc * 1024 + ng * 512 + 512],
                    stmp[sl2], AF.Exp, bias=cBias32[:], scale=1.0),
                    deps=[e1, a_war[tc]])
                stmp_ev[sl2] = e2
                a_wr[(tc, ng)] = e2
                return e2
            proj_tm(xnb, xnb_ev, ws_b, 1024, wbufs, wbuf_ev, evac_s)

            def evac_v(tc, ng, bank, evm):
                e = bld.op("v", lambda v, tc=tc, ng=ng, bank=bank: v.tensor_copy(
                    v_st[:, tc * 2048 + ng * 512: tc * 2048 + ng * 512 + 512], bank[:]),
                    deps=[evm, v_wr.get((tc, ng))])
                v_wr[(tc, ng)] = e
                return e
            evg = proj_tm(xnb, xnb_ev, wv_b, 2048, wbufs, wbuf_ev, evac_v)
            xnb_last[0] = evg[-1]

            for tc in range(4):
                adep = [a_wr[(tc, 0)], a_wr[(tc, 1)]]
                vdep = [v_wr[(tc, ng)] for ng in range(4)]
                evS = []
                for half in range(2):
                    e = bld.op("t", lambda t, tc=tc, half=half: t.matmul(
                        ps[half][:], cU[:],
                        a_tm[:, tc * 1024 + half * 512: tc * 1024 + (half + 1) * 512],
                        start=True, stop=True),
                        deps=adep + [ev_u, pbank_ev[half]])
                    evS.append(e)
                evR = []
                for half in range(2):
                    e = bld.op("t", lambda t, half=half: t.matmul(
                        ps[2 + half][:], cOnes1[0:1, :], s_prev[0:1, half * 512:(half + 1) * 512],
                        start=True, stop=True),
                        deps=[sprev_ev[0], pbank_ev[2 + half]])
                    evR.append(e)
                sA = []
                for half in range(2):
                    e = bld.op("t", lambda t, tc=tc, half=half: t.matmul(
                        ps[4 + half][0:1, 0:512], cOnesColBf[:],
                        a_tm[:, tc * 1024 + half * 512: tc * 1024 + (half + 1) * 512],
                        start=True, stop=True),
                        deps=adep + [pbank_ev[4 + half]])
                    sA.append(e)
                for half in range(2):
                    e = bld.op("v", lambda v, half=half: v.tensor_tensor(
                        s_prev[0:1, half * 512:(half + 1) * 512],
                        s_prev[0:1, half * 512:(half + 1) * 512],
                        ps[4 + half][0:1, 0:512], OP.add),
                        deps=[sA[half], evR[0], evR[1], sprev_ev[0]])
                    pbank_ev[4 + half] = e
                    sprev_ev[0] = e
                a_war[tc] = None
                e1 = bld.op("v", lambda v: v.tensor_copy(srep_sb[:, 0:512], ps[2][:]),
                            deps=[evR[0], cb["srep"]])
                e2 = bld.op("v", lambda v: v.tensor_copy(srep_sb[:, 512:1024], ps[3][:]),
                            deps=[evR[1]])
                cb["srep"] = e2
                pbank_ev[2] = e1
                pbank_ev[3] = e2
                f1 = bld.op("v", lambda v: v.tensor_tensor(sfull[:, 0:512], ps[0][:],
                                                           srep_sb[:, 0:512], OP.add),
                            deps=[evS[0], e1, cb["sfull"]])
                f2 = bld.op("v", lambda v: v.tensor_tensor(sfull[:, 512:1024], ps[1][:],
                                                           srep_sb[:, 512:1024], OP.add),
                            deps=[evS[1], e2])
                cb["sfull"] = f2
                pbank_ev[0] = f1
                pbank_ev[1] = f2
                evsi = bld.op("v", lambda v: v.reciprocal(sinv[:], sfull[:]),
                              deps=[f1, f2, cb["sinv"]])
                cb["sinv"] = evsi
                kq_evac = {}
                for grp in range(4):
                    bi = 4 + grp % 2
                    bank = ps[bi]
                    def mmkq(t, tc=tc, grp=grp, bank=bank):
                        mm = None
                        for hh in range(4):
                            h_ = grp * 4 + hh
                            mm = t.matmul(bank[:, hh * 128:(hh + 1) * 128],
                                          kh[h_][:, tc * 128:(tc + 1) * 128],
                                          qh[h_][:, tc * 128:(tc + 1) * 128],
                                          start=True, stop=True)
                        return mm
                    hdeps = [k_wr[grp * 4], k_wr[grp * 4 + 2], q_wr[grp * 4], q_wr[grp * 4 + 2]]
                    e = bld.op("t", mmkq, deps=hdeps + [pbank_ev[bi]])
                    slot = grp % 2
                    ee = bld.op("v", lambda v, slot=slot, bank=bank: v.tensor_tensor(
                        kqc[:, slot * 512:(slot + 1) * 512], bank[:], cU4[:], OP.mult),
                        deps=[e, ev_u4, cb[f"pk{slot}"]])
                    pbank_ev[bi] = ee
                    cb[f"pk{grp % 2}"] = ee
                    kq_evac[grp] = ee
                ok_ev = []
                for half in range(2):
                    bank = ps[half]
                    def mmok(t, tc=tc, half=half, bank=bank):
                        mm = None
                        for hh in range(8):
                            h_ = half * 8 + hh
                            slot = (h_ // 4) % 2
                            mm = t.matmul(bank[:, hh * 64:(hh + 1) * 64],
                                          kqc[:, slot * 512 + (h_ % 4) * 128: slot * 512 + (h_ % 4 + 1) * 128],
                                          a_tm[:, tc * 1024 + h_ * 64: tc * 1024 + (h_ + 1) * 64],
                                          start=True, stop=False)
                            mm = t.matmul(bank[:, hh * 64:(hh + 1) * 64],
                                          qh[h_][:, tc * 128:(tc + 1) * 128],
                                          hk_bf[:, h_ * 64:(h_ + 1) * 64],
                                          start=False, stop=True)
                        return mm
                    e = bld.op("t", mmok, deps=[kq_evac[half * 2], kq_evac[half * 2 + 1],
                                                hk_ev[0], pbank_ev[half]] + adep)
                    ok_ev.append(e)
                g1 = bld.op("v", lambda v: v.scalar_tensor_tensor(
                    okt[:, 0:512], ps[0][:], SCALE, sinv[:, 0:512], OP.mult, OP.mult),
                    deps=[ok_ev[0], evsi, cb["okt"]])
                g2 = bld.op("v", lambda v: v.scalar_tensor_tensor(
                    okt[:, 512:1024], ps[1][:], SCALE, sinv[:, 512:1024], OP.mult, OP.mult),
                    deps=[ok_ev[1], evsi])
                cb["okt"] = g2
                pbank_ev[0] = g1
                pbank_ev[1] = g2
                m1 = bld.op("v", lambda v: v.tensor_reduce(
                    mx, okt.rearrange("p (h m) -> p h m", m=64), AX.X, OP.max, negate=True),
                    deps=[g1, g2, cb["mx"]])
                cb["mx"] = m1
                m2 = bld.op("v", lambda v: v.tensor_tensor(
                    okt.rearrange("p (h m) -> p h m", m=64),
                    okt.rearrange("p (h m) -> p h m", m=64),
                    mx.rearrange("p h -> p h ()").broadcast_to([128, 16, 64]), OP.add),
                    deps=[m1])
                m3 = bld.op("s", lambda s: s.activation(esb[:], okt[:], AF.Exp), deps=[m2])
                m4 = bld.op("v", lambda v: v.tensor_reduce(
                    sm, esb.rearrange("p (h m) -> p h m", m=64), AX.X, OP.add),
                    deps=[m3, cb["sm"]])
                cb["sm"] = m4
                m5 = bld.op("v", lambda v: v.reciprocal(smr, sm), deps=[m4, cb["smr"]])
                cb["smr"] = m5
                m6 = bld.op("v", lambda v: v.tensor_tensor(
                    esb.rearrange("p (h m) -> p h m", m=64),
                    esb.rearrange("p (h m) -> p h m", m=64),
                    smr.rearrange("p h -> p h ()").broadcast_to([128, 16, 64]), OP.mult),
                    deps=[m5])
                m7 = bld.op("v", lambda v: v.tensor_tensor(u_tm[:], esb[:], sinv[:], OP.mult),
                            deps=[m6, cb["u"]])
                cb["u"] = m7
                tr = {}
                for kind, srcfn, dstlist, evlist, osz in (
                    ("a", lambda h_: a_tm[:, tc * 1024 + h_ * 64: tc * 1024 + (h_ + 1) * 64], afm, afm_ev, 128),
                    ("u", lambda h_: u_tm[:, h_ * 64:(h_ + 1) * 64], ufm, ufm_ev, 128),
                    ("k", lambda h_: kh[h_][:, tc * 128:(tc + 1) * 128], ktm, ktm_ev, 64),
                ):
                    if kind == "a":
                        srcdep = adep
                    elif kind == "u":
                        srcdep = [m7]
                    else:
                        srcdep = [k_wr[0], k_wr[2], k_wr[4], k_wr[6], k_wr[8], k_wr[10], k_wr[12], k_wr[14]]
                    oparts = 64 if kind != "k" else 128
                    for h_ in range(16):
                        tb = h_ % 2
                        slot = (h_ // 2) % 4
                        idap = cEye[0:64, 0:64] if kind == "k" else cEye[:]
                        def trf(t, h_=h_, tb=tb, slot=slot, srcfn=srcfn, oparts=oparts, osz=osz, idap=idap):
                            return t.transpose(psT[tb][0:oparts, slot * 256: slot * 256 + osz],
                                               srcfn(h_), idap)
                        evt = bld.op("t", trf, deps=srcdep + [ptbank_ev[tb]])
                        eng = "v" if h_ % 2 == 0 else "s"
                        def evf(e, h_=h_, tb=tb, slot=slot, dstlist=dstlist, oparts=oparts, osz=osz):
                            srcv = psT[tb][0:oparts, slot * 256: slot * 256 + osz]
                            dstv = dstlist[h_][:]
                            return e.tensor_copy(dstv, srcv) if h_ % 2 == 0 else e.copy(dstv, srcv)
                        eve = bld.op(eng, evf, deps=[evt, evlist[h_]])
                        evlist[h_] = eve
                        ptbank_ev[tb] = eve
                        tr[(kind, h_)] = eve
                pt_evac = {}
                ovi_ev = {}
                o_evs = [None] * 4
                banksel = [0, 1, 4, 5]

                def emit_pt(grp):
                    bi = 2 + grp % 2
                    bank = ps[bi]
                    def mmpt(t, grp=grp, bank=bank):
                        mm = None
                        for hh in range(4):
                            h_ = grp * 4 + hh
                            mm = t.matmul(bank[:, hh * 128:(hh + 1) * 128],
                                          afm[h_][:], ufm[h_][:], start=True, stop=True)
                        return mm
                    hdeps = [tr[("a", grp * 4 + hh)] for hh in range(4)] + \
                            [tr[("u", grp * 4 + hh)] for hh in range(4)]
                    e = bld.op("t", mmpt, deps=hdeps + [pbank_ev[bi]])
                    slot = grp % 2
                    ee = bld.op("v", lambda v, slot=slot, bank=bank: v.tensor_tensor(
                        ptc[:, slot * 512:(slot + 1) * 512], bank[:], cU4[:], OP.mult),
                        deps=[e, cb[f"pk{slot}"]])
                    pbank_ev[bi] = ee
                    cb[f"pk{slot}"] = ee
                    pt_evac[grp] = ee

                def emit_ov(qtr):
                    bi = banksel[qtr]
                    bank = ps[bi]
                    slot = qtr % 2
                    def mmov(t, tc=tc, qtr=qtr, bank=bank, slot=slot):
                        mm = None
                        for hh in range(4):
                            h_ = qtr * 4 + hh
                            mm = t.matmul(bank[:, hh * 128:(hh + 1) * 128],
                                          ptc[:, slot * 512 + hh * 128: slot * 512 + (hh + 1) * 128],
                                          v_st[:, tc * 2048 + h_ * 128: tc * 2048 + (h_ + 1) * 128],
                                          start=True, stop=True)
                        return mm
                    e = bld.op("t", mmov, deps=[pt_evac[qtr]] + vdep + [pbank_ev[bi]])
                    ovi_ev[qtr] = e
                    eo = bld.op("v", lambda v, tc=tc, qtr=qtr, bank=bank: v.tensor_copy(
                        o_st[:, tc * 2048 + qtr * 512: tc * 2048 + (qtr + 1) * 512], bank[:]),
                        deps=[e, o_wr.get((tc, qtr))])
                    pbank_ev[bi] = eo
                    o_evs[qtr] = eo

                emit_pt(0)
                emit_pt(1)
                emit_ov(0)
                emit_pt(2)
                emit_ov(1)
                emit_pt(3)
                emit_ov(2)
                emit_ov(3)
                for qtr in range(4):
                    bi = banksel[qtr]
                    bank = ps[bi]
                    def mmoj(t, qtr=qtr, bank=bank):
                        mm = None
                        for hh in range(4):
                            h_ = qtr * 4 + hh
                            mm = t.matmul(bank[:, hh * 128:(hh + 1) * 128],
                                          ufm[h_][:], hv_bf[:, h_ * 128:(h_ + 1) * 128],
                                          start=True, stop=True)
                        return mm
                    e = bld.op("t", mmoj, deps=[tr[("u", qtr * 4 + hh)] for hh in range(4)] +
                               [hv_ev[0], o_evs[qtr], pbank_ev[bi]])
                    eo = bld.op("v", lambda v, tc=tc, qtr=qtr, bank=bank: v.tensor_tensor(
                        o_st[:, tc * 2048 + qtr * 512: tc * 2048 + (qtr + 1) * 512],
                        o_st[:, tc * 2048 + qtr * 512: tc * 2048 + (qtr + 1) * 512],
                        bank[:], OP.add), deps=[e, o_evs[qtr]])
                    pbank_ev[bi] = eo
                    o_wr[(tc, qtr)] = eo
                odep = [o_wr[(tc, qtr)] for qtr in range(4)]
                hkmm = {}
                for half in range(2):
                    bi = 2 + half
                    bank = ps[bi]
                    def mmhk(t, tc=tc, half=half, bank=bank):
                        mm = None
                        for hh in range(8):
                            h_ = half * 8 + hh
                            mm = t.matmul(bank[0:64, hh * 64:(hh + 1) * 64],
                                          ktm[h_][:],
                                          a_tm[:, tc * 1024 + h_ * 64: tc * 1024 + (h_ + 1) * 64],
                                          start=True, stop=True)
                        return mm
                    hkmm[half] = bld.op("t", mmhk, deps=[tr[("k", half * 8 + hh)] for hh in range(8)] +
                                        adep + [pbank_ev[bi]])
                e_add = []
                for half in range(2):
                    e1_ = bld.op("v", lambda v, half=half: v.tensor_tensor(
                        hk_sb[:, half * 512:(half + 1) * 512], hk_sb[:, half * 512:(half + 1) * 512],
                        ps[2 + half][0:64, :], OP.add), deps=[hkmm[half], hk_ev[0]])
                    e_add.append(e1_)
                    pbank_ev[2 + half] = e1_
                hk_ev[0] = bld.op("v", lambda v: v.tensor_copy(hk_bf[:], hk_sb[:]), deps=e_add)
                hvmm = {}
                for qtr in range(4):
                    bi = banksel[qtr]
                    bank = ps[bi]
                    def mmhv(t, tc=tc, qtr=qtr, bank=bank):
                        mm = None
                        for hh in range(4):
                            h_ = qtr * 4 + hh
                            mm = t.matmul(bank[0:64, hh * 128:(hh + 1) * 128],
                                          a_tm[:, tc * 1024 + h_ * 64: tc * 1024 + (h_ + 1) * 64],
                                          v_st[:, tc * 2048 + h_ * 128: tc * 2048 + (h_ + 1) * 128],
                                          start=True, stop=True)
                        return mm
                    hvmm[qtr] = bld.op("t", mmhv, deps=adep + vdep + [pbank_ev[bi]])
                e_add = []
                for qtr in range(4):
                    e1_ = bld.op("v", lambda v, qtr=qtr: v.tensor_tensor(
                        hv_sb[:, qtr * 512:(qtr + 1) * 512], hv_sb[:, qtr * 512:(qtr + 1) * 512],
                        ps[banksel[qtr]][0:64, :], OP.add), deps=[hvmm[qtr], hv_ev[0]])
                    e_add.append(e1_)
                    pbank_ev[banksel[qtr]] = e1_
                hv_ev[0] = bld.op("v", lambda v: v.tensor_copy(hv_bf[:], hv_sb[:]), deps=e_add)
                p1a = bld.op("v", lambda v, tc=tc: v.tensor_tensor(
                    o2[:], o_st[:, tc * 2048: tc * 2048 + 1024],
                    o_st[:, tc * 2048: tc * 2048 + 1024], OP.mult), deps=odep[0:2] + [cb["u"]])
                p2a = bld.op("v", lambda v, tc=tc: v.tensor_reduce(
                    rr_st[:, tc * 16: tc * 16 + 8], o2.rearrange("p (h d) -> p h d", d=128),
                    AX.X, OP.add), deps=[p1a, rr_war[0]])
                p1b = bld.op("v", lambda v, tc=tc: v.tensor_tensor(
                    o2[:], o_st[:, tc * 2048 + 1024: tc * 2048 + 2048],
                    o_st[:, tc * 2048 + 1024: tc * 2048 + 2048], OP.mult), deps=odep[2:4] + [p2a])
                cb["u"] = p1b
                p2b = bld.op("v", lambda v, tc=tc: v.tensor_reduce(
                    rr_st[:, tc * 16 + 8: tc * 16 + 16], o2.rearrange("p (h d) -> p h d", d=128),
                    AX.X, OP.add), deps=[p1b])
                p3 = bld.op("v", lambda v, tc=tc: v.tensor_scalar(
                    rr_st[:, tc * 16:(tc + 1) * 16], rr_st[:, tc * 16:(tc + 1) * 16],
                    1.0 / 128, 1e-5, OP.mult, OP.add), deps=[p2a, p2b])
                p4 = bld.op("s", lambda s, tc=tc: s.activation(
                    rr_st[:, tc * 16:(tc + 1) * 16], rr_st[:, tc * 16:(tc + 1) * 16], AF.Sqrt),
                    deps=[p3])
                p5 = bld.op("v", lambda v, tc=tc: v.reciprocal(
                    rr_st[:, tc * 16:(tc + 1) * 16], rr_st[:, tc * 16:(tc + 1) * 16]), deps=[p4])
                rr_war[0] = p5

            def evac_g(tc, ng, bank, evm):
                e1 = bld.op("s", lambda s, bank=bank: s.activation(sgt, bank[:], AF.Silu),
                            deps=[evm, stmp_ev[0]])
                e2 = bld.op("v", lambda v, ng=ng: v.tensor_tensor(
                    sggt, sgt, cGnw[:, ng * 512:(ng + 1) * 512], OP.mult),
                    deps=[e1, ev_g3, cb["pk1"]])
                stmp_ev[0] = e2
                cb["pk1"] = e2
                e3 = bld.op("v", lambda v, tc=tc, ng=ng: v.tensor_tensor(
                    ogt, o_st[:, tc * 2048 + ng * 512: tc * 2048 + ng * 512 + 512], sggt, OP.mult),
                    deps=[e2, o_wr[(tc, ng)], cb["ogt"]])
                cb["sggt"] = e3
                e4 = bld.op("v", lambda v, tc=tc, ng=ng: v.tensor_tensor(
                    ogt.rearrange("p (h d) -> p h d", d=128),
                    ogt.rearrange("p (h d) -> p h d", d=128),
                    rr_st[:, tc * 16 + ng * 4: tc * 16 + (ng + 1) * 4]
                    .rearrange("p h -> p h ()").broadcast_to([128, 4, 128]), OP.mult),
                    deps=[e3, rr_war[0]])
                last = e4
                for jj in range(4):
                    j = ng * 4 + jj
                    tb = jj % 2
                    def trf(t, jj=jj, tb=tb):
                        return t.transpose(psT[tb][:, 512 * (jj // 2): 512 * (jj // 2) + 128],
                                           ogt[:, jj * 128:(jj + 1) * 128], cEye[:])
                    evt = bld.op("t", trf, deps=[e4, ptbank_ev[tb]])
                    eng = "v" if jj % 2 == 0 else "s"
                    def evf(e, j=j, jj=jj, tb=tb, tc=tc):
                        srcv = psT[tb][:, 512 * (jj // 2): 512 * (jj // 2) + 128]
                        dstv = og_fm[:, j * 512 + tc * 128: j * 512 + tc * 128 + 128]
                        return e.tensor_copy(dstv, srcv) if jj % 2 == 0 else e.copy(dstv, srcv)
                    eve = bld.op(eng, evf, deps=[evt, og_wr.get((tc, j))])
                    og_wr[(tc, j)] = eve
                    ptbank_ev[tb] = eve
                    last = eve
                cb["ogt"] = last
                return last
            proj_tm(xnb, xnb_ev, wg_b, 2048, wbufs, wbuf_ev, evac_g)

            ogdep = [og_wr[(tc, j)] for tc in range(4) for j in range(16)]
            for ng in range(4):
                sl = ng % 2
                rr_ = wo_b[:, ng * 512:(ng + 1) * 512].rearrange("(dc p) n -> p dc n", p=128)
                weva = bld.dma("d", wbufs[sl][0].rearrange("p (dc n) -> p dc n", n=512),
                               rr_[:, 0:8, :], deps=[wbuf_ev[sl]])
                wevb = bld.dma("d", wbufs[sl][1].rearrange("p (dc n) -> p dc n", n=512),
                               rr_[:, 8:16, :], deps=[wbuf_ev[sl]])
                for tc in range(4):
                    bi = 4 + (ng * 4 + tc) % 2
                    bank = ps[bi]
                    def mmwo(t, tc=tc, sl=sl, bank=bank):
                        mm = None
                        for dc in range(16):
                            wv_ = wbufs[sl][dc // 8][:, (dc % 8) * 512:(dc % 8 + 1) * 512]
                            mm = t.matmul(bank[:],
                                          og_fm[:, dc * 512 + tc * 128: dc * 512 + tc * 128 + 128],
                                          wv_, start=(dc == 0), stop=(dc == 15))
                        return mm
                    evm = bld.op("t", mmwo, deps=[weva, wevb, pbank_ev[bi]] + ogdep)
                    asl = tc % 2
                    eve = bld.op("v", lambda v, bank=bank, asl=asl: v.tensor_copy(attb[asl], bank[:]),
                                 deps=[evm, attb_ev[asl]])
                    pbank_ev[bi] = eve
                    evd = bld.dma("g", att_scr[st * 512 + tc * 128: st * 512 + (tc + 1) * 128,
                                              ng * 512:(ng + 1) * 512], attb[asl], deps=[eve])
                    attb_ev[asl] = evd
                    stmp_ev[asl] = evd
                wbuf_ev[sl] = evm

        for st in range(st_own):
            emit_main_st(st)
        bld.barrier()
        ar.reset()

        wgb2 = [ar.abf(128, 16 * 1024) for _ in range(2)]
        wub2 = [ar.abf(128, 16 * 1024) for _ in range(2)]
        xnb2 = [ar.abf(128, 16 * 512) for _ in range(2)]
        hbuf = [ar.abf(128, 512) for _ in range(2)]
        sgb = [ar.af32(128, 512) for _ in range(2)]
        hsq = [ar.abf(128, 512) for _ in range(2)]
        rtmp = ar.af32(1, 512)
        ones_bf = ar.abf(128, 1)
        ev_ob = bld.op("v", lambda v: v.memset(ones_bf, 1.0))
        pbank_ev = {i: None for i in range(6)}
        pbank_ev["rms"] = None
        pbank_ev["rms2"] = None
        hbuf_ev = [None, None]
        sgb_ev = [None, None]
        hsq_ev = [None, None]
        rtmp_ev = [None]
        rmsacc_ev = [None]
        xnb2_ev = [None, None]
        xnb2_dma = [None, None]
        wslab_ev = [None, None]
        wslab_dma = [None, None]
        def load_whalf(hs, slot, deps):
            evs = []
            for dc in range(16):
                evs.append(bld.op("d", (lambda out, in_: (lambda eng: eng.dma_start(
                    out=out, in_=in_, transpose=True)))(
                    wgb2[slot][:, dc * 1024:(dc + 1) * 1024],
                    wgu_q[hs * 1024:(hs + 1) * 1024, dc * 128:(dc + 1) * 128]),
                    deps=deps, dma=True))
                evs.append(bld.op("d", (lambda out, in_: (lambda eng: eng.dma_start(
                    out=out, in_=in_, transpose=True)))(
                    wub2[slot][:, dc * 1024:(dc + 1) * 1024],
                    wgu_q[HF + hs * 1024: HF + (hs + 1) * 1024, dc * 128:(dc + 1) * 128]),
                    deps=deps, dma=True))
            return evs[-1]

        def load_xnb2(st, slot, deps):
            evs = []
            for dc in range(16):
                evs.append(bld.dma("d", xnb2[slot][:, dc * 512:(dc + 1) * 512],
                                   xn_own[dc * 128:(dc + 1) * 128, st * 512:(st + 1) * 512],
                                   deps=deps))
            return evs[-1]

        wslab_dma[0] = load_whalf(0, 0, [None])
        for hs in range(8):
            wsl = hs % 2
            wev = wslab_dma[wsl]
            if hs + 1 < 8:
                wslab_dma[(hs + 1) % 2] = load_whalf(hs + 1, (hs + 1) % 2, [wslab_ev[(hs + 1) % 2]])
            for st in range(st_own):
                xsl = st % 2
                if hs == 0 and st == 0:
                    xnb2_dma[0] = load_xnb2(0, 0, [None])
                xnb_ev = xnb2_dma[xsl]
                if st + 1 < st_own:
                    xnb2_dma[(st + 1) % 2] = load_xnb2(st + 1, (st + 1) % 2,
                                                       [xnb2_ev[(st + 1) % 2]])
                elif hs + 1 < 8:
                    xnb2_dma[(st_own) % 2] = load_xnb2(0, (st_own) % 2, [xnb2_ev[(st_own) % 2]])
                for hc in range(8):
                    gb = ps[(hc % 3) * 2]
                    ub = ps[(hc % 3) * 2 + 1]
                    def mmg(t, hc=hc, gb=gb, wsl=wsl, xsl=xsl):
                        mm = None
                        for dc in range(16):
                            mm = t.matmul(gb[:],
                                          wgb2[wsl][:, dc * 1024 + hc * 128: dc * 1024 + hc * 128 + 128],
                                          xnb2[xsl][:, dc * 512:(dc + 1) * 512],
                                          start=(dc == 0), stop=(dc == 15))
                        return mm
                    def mmu(t, hc=hc, ub=ub, wsl=wsl, xsl=xsl):
                        mm = None
                        for dc in range(16):
                            mm = t.matmul(ub[:],
                                          wub2[wsl][:, dc * 1024 + hc * 128: dc * 1024 + hc * 128 + 128],
                                          xnb2[xsl][:, dc * 512:(dc + 1) * 512],
                                          start=(dc == 0), stop=(dc == 15))
                        return mm
                    eg_ = bld.op("t", mmg, deps=[wev, xnb_ev, pbank_ev[(hc % 3) * 2]])
                    eu_ = bld.op("t", mmu, deps=[wev, xnb_ev, pbank_ev[(hc % 3) * 2 + 1]])
                    sl = hc % 2
                    e1 = bld.op("s", lambda s, sl=sl, gb=gb: s.activation(sgb[sl], gb[:], AF.Silu),
                                deps=[eg_, sgb_ev[sl]])
                    pbank_ev[(hc % 3) * 2] = e1
                    e2 = bld.op("v", lambda v, sl=sl, ub=ub: v.tensor_tensor(
                        hbuf[sl], sgb[sl], ub[:], OP.mult), deps=[e1, eu_, hbuf_ev[sl]])
                    sgb_ev[sl] = e2
                    pbank_ev[(hc % 3) * 2 + 1] = e2
                    e3 = bld.op("s", lambda s, sl=sl: s.activation(hsq[sl], hbuf[sl], AF.Square),
                                deps=[e2, hsq_ev[sl]])
                    e4 = bld.op("t", lambda t, sl=sl, hc=hc: t.matmul(
                        psT[0][:].bitcast(F32)[0:1, :], ones_bf, hsq[sl],
                        start=(hc == 0), stop=(hc == 7)),
                        deps=[e3, ev_ob] + ([pbank_ev["rms"]] if hc == 0 else []))
                    hsq_ev[sl] = e4
                    rms_mm_ev = e4
                    evd = bld.dma("g", h_scr[hs * 1024 + hc * 128: hs * 1024 + (hc + 1) * 128,
                                             st * 512:(st + 1) * 512], hbuf[sl], deps=[e2])
                    hbuf_ev[sl] = evd
                r1 = bld.op("v", lambda v: v.tensor_copy(rtmp, psT[0][:].bitcast(F32)[0:1, :]),
                            deps=[rms_mm_ev, rtmp_ev[0]])
                pbank_ev["rms"] = r1
                r2s = []
                for q4 in range(4):
                    e = bld.op("t", lambda t, q4=q4: t.matmul(
                        psT[1][:].bitcast(F32)[:, q4:q4 + 1], rtmp[0:1, q4 * 128:(q4 + 1) * 128],
                        cOne11[0:1, :], start=True, stop=True),
                        deps=[r1] + ([pbank_ev["rms2"]] if q4 == 0 else []))
                    r2s.append(e)
                rtmp_ev[0] = r2s[-1]
                r3 = bld.op("v", lambda v, st=st: v.tensor_tensor(
                    rmsacc[:, st * 4:(st + 1) * 4], rmsacc[:, st * 4:(st + 1) * 4],
                    psT[1][:].bitcast(F32)[:, 0:4], OP.add), deps=r2s + [rmsacc_ev[0]])
                rmsacc_ev[0] = r3
                pbank_ev["rms2"] = r3
                xnb2_ev[xsl] = r3
            wslab_ev[wsl] = rmsacc_ev[0]
        bld.barrier()
        ar.reset()

        EPS32 = float(np.finfo(np.float32).eps)
        q1_ = bld.op("v", lambda v: v.tensor_scalar(rstdt[:], rmsacc[:], 1.0 / HF, EPS32,
                                                    OP.mult, OP.add))
        q2_ = bld.op("s", lambda s: s.activation(rstdt[:], rstdt[:], AF.Sqrt), deps=[q1_])
        q3_ = bld.op("v", lambda v: v.reciprocal(rstdt[:], rstdt[:]), deps=[q2_])

        wpb = ar.abf(128, 64 * 1024)
        hstr = [ar.abf(128, 512) for _ in range(16)]
        xb2 = [ar.af32(128, 1024) for _ in range(2)]
        ab2 = [ar.af32(128, 1024) for _ in range(2)]
        ob2 = [ar.af32(128, 1024) for _ in range(2)]
        p4banks = [p[:] for p in ps] + [psT[0][:].bitcast(F32), psT[1][:].bitcast(F32)]
        hstr_ev = [None] * 16
        xb2_ev = [None, None]
        ab2_ev = [None, None]
        ob2_ev = [None, None]
        wpb_ev = [None]
        bank_ev = [None] * 8
        for dh in range(2):
            wevs = []
            for hc in range(64):
                wevs.append(bld.op("d", (lambda out, in_: (lambda eng: eng.dma_start(
                    out=out, in_=in_, transpose=True)))(
                    wpb[:, hc * 1024:(hc + 1) * 1024],
                    wp_q[dh * 1024:(dh + 1) * 1024, hc * 128:(hc + 1) * 128]),
                    deps=[wpb_ev[0]], dma=True))
            wev = wevs[-1]
            for st in range(st_own):
                mm_last = None
                for wave in range(8):
                    wp_ = wave % 2
                    hevs = []
                    for sl in range(8):
                        hc = wave * 8 + sl
                        hevs.append(bld.dma("d", hstr[wp_ * 8 + sl],
                                            h_scr[hc * 128:(hc + 1) * 128, st * 512:(st + 1) * 512],
                                            deps=[hstr_ev[wp_ * 8 + sl]]))
                    for tc in range(4):
                        def mmp(t, tc=tc, wave=wave, wp_=wp_):
                            mm = None
                            for sl in range(8):
                                hc = wave * 8 + sl
                                for nh in range(2):
                                    bank = p4banks[tc * 2 + nh]
                                    mm = t.matmul(bank,
                                                  hstr[wp_ * 8 + sl][:, tc * 128:(tc + 1) * 128],
                                                  wpb[:, hc * 1024 + nh * 512: hc * 1024 + nh * 512 + 512],
                                                  start=(hc == 0), stop=(hc == 63))
                            return mm
                        mm_last = bld.op("t", mmp, deps=hevs + [wevs[wave * 8 + 7]] +
                                         ([bank_ev[tc * 2], bank_ev[tc * 2 + 1]] if wave == 0 else []))
                    for sl in range(8):
                        hstr_ev[wp_ * 8 + sl] = mm_last
                for tc in range(4):
                    row0 = st * 512 + tc * 128
                    sl2 = tc % 2
                    evx = bld.dma("d", xb2[sl2], x_own[row0:row0 + 128, dh * 1024:(dh + 1) * 1024],
                                  deps=[xb2_ev[sl2]])
                    eva = bld.dma("d", ab2[sl2], att_scr[row0:row0 + 128, dh * 1024:(dh + 1) * 1024],
                                  deps=[ab2_ev[sl2]])
                    b0 = p4banks[tc * 2]
                    b1 = p4banks[tc * 2 + 1]
                    e1 = bld.op("v", lambda v, sl2=sl2, b0=b0, st=st, tc=tc: v.scalar_tensor_tensor(
                        ob2[sl2][:, 0:512], b0,
                        rstdt[:, st * 4 + tc: st * 4 + tc + 1],
                        ab2[sl2][:, 0:512], OP.mult, OP.add),
                        deps=[mm_last, eva, q3_, ob2_ev[sl2]])
                    e2 = bld.op("v", lambda v, sl2=sl2, b1=b1, st=st, tc=tc: v.scalar_tensor_tensor(
                        ob2[sl2][:, 512:1024], b1,
                        rstdt[:, st * 4 + tc: st * 4 + tc + 1],
                        ab2[sl2][:, 512:1024], OP.mult, OP.add),
                        deps=[mm_last, eva, e1])
                    e3 = bld.op("v", lambda v, sl2=sl2: v.tensor_tensor(
                        ob2[sl2], ob2[sl2], xb2[sl2], OP.add), deps=[e1, e2, evx])
                    xb2_ev[sl2] = e3
                    ab2_ev[sl2] = e3
                    bank_ev[tc * 2] = e3
                    bank_ev[tc * 2 + 1] = e3
                    evd = bld.dma("g", out[row0:row0 + 128, dh * 1024:(dh + 1) * 1024],
                                  ob2[sl2], deps=[e3])
                    ob2_ev[sl2] = evd
            wpb_ev[0] = mm_last
        bld.waitop("v", [ob2_ev[0], ob2_ev[1], ("g", bld.cnt["g"]), ("d", bld.cnt["d"])])
        bld.op("v", lambda v: v.engine_nop())

        block = stack.enter_context(nc.Block())

        def mkbody(qe):
            def body(eng):
                for waits, fn, inc in bld.q[qe]:
                    for (se, v) in waits:
                        eng.wait_ge(sems[se], v)
                    if fn is not None:
                        instr = fn(eng)
                        instr.then_inc(sems[qe], inc)
            return body

        block.sync(mkbody("d"))
        block.tensor(mkbody("t"))
        block.vector(mkbody("v"))
        block.scalar(mkbody("s"))
        block.gpsimd(mkbody("g"))

    return nc


def _consts():
    u = np.triu(np.ones((128, 128), np.float32)).astype(BF)
    eye = np.eye(128, dtype=BF)
    return u, eye


def kernel(**inputs):
    ins = {k: np.asarray(v) for k, v in inputs.items()}
    x = np.ascontiguousarray(ins["L0_attn_hidden_states"].reshape(32768, 2048))
    u, eye = _consts()
    nc = build(TOK=4096, PRE=4096)
    in_maps = []
    for c in range(8):
        b, half = c // 2, c % 2
        in_maps.append({
            "x_own": np.ascontiguousarray(x[c * 4096:(c + 1) * 4096]),
            "x_pre": np.ascontiguousarray(x[b * 8192: b * 8192 + 4096]),
            "pbias": np.full((128, 1), -32.0 if half == 1 else -100000.0, np.float32),
            "wq": ins["L0_attn_Wq"], "wk": ins["L0_attn_Wk"], "wv": ins["L0_attn_Wv"],
            "wg": ins["L0_attn_Wg"], "ws": ins["L0_attn_Ws"], "wo": ins["L0_attn_Wo"],
            "wgu": ins["L0_ffn_w_gate_up"], "wp": ins["L0_ffn_w_proj"],
            "gnw": np.ascontiguousarray(ins["L0_attn_g_norm_weight"].reshape(1, 128)),
            "umask": u, "eye": eye,
        })
    res = run_bass_kernel_spmd(nc, in_maps, core_ids=list(range(8)))
    out = np.concatenate([np.asarray(r["out"]) for r in res.results], axis=0)
    return out.reshape(4, 8192, 2048)


# version 21
# speedup vs baseline: 1.0276x; 1.0276x over previous
import numpy as np
import ml_dtypes
from contextlib import ExitStack
import concourse.bass as bass
import concourse.mybir as mybir
from concourse.bass_utils import run_bass_kernel_spmd

BF = ml_dtypes.bfloat16
F32 = mybir.dt.float32
BF16 = mybir.dt.bfloat16
AF = mybir.ActivationFunctionType
OP = mybir.AluOpType
AX = mybir.AxisListType

D = 2048
H, DK, DV, M = 16, 64, 128, 64
HF = 8192
RND = 12582912.0
SCALE = 0.125


class Bld:
    ENGS = ("t", "v", "s", "g", "d")

    def __init__(self):
        self.q = {e: [] for e in self.ENGS}
        self.cnt = {e: 0 for e in self.ENGS}
        self.lastwait = {e: {} for e in self.ENGS}
        self.pend = {e: [] for e in self.ENGS}

    def _need(self, e, deps):
        need = {}
        for d_ in deps:
            if d_ is None:
                continue
            se, v = d_
            if se == e and v >= self.cnt[e]:
                v = self.cnt[e]
            if v > self.lastwait[e].get(se, 0) and v > need.get(se, 0):
                need[se] = v
        for se, v in need.items():
            self.lastwait[e][se] = v
        return sorted(need.items())

    def op(self, e, fn, deps=(), dma=False):
        deps = list(deps) + self.pend[e]
        self.pend[e] = []
        waits = self._need(e, deps)
        inc = 16 if dma else 1
        self.cnt[e] += inc
        self.q[e].append((waits, fn, inc))
        return (e, self.cnt[e])

    def dma(self, qe, out, in_, deps=()):
        def fn(eng, out=out, in_=in_):
            return eng.dma_start(out=out, in_=in_)
        return self.op(qe, fn, deps, dma=True)

    def waitop(self, e, deps):
        deps = list(deps) + self.pend[e]
        self.pend[e] = []
        waits = self._need(e, deps)
        self.q[e].append((waits, None, 0))

    def barrier(self):
        evs = [(e, self.cnt[e]) for e in self.ENGS if self.cnt[e] > 0]
        for e in self.ENGS:
            self.pend[e] = list({(se, v) for (se, v) in (self.pend[e] + evs)})


class Arena:
    def __init__(self, bf_ap, f32_ap, nbytes):
        self.bf = bf_ap
        self.f32 = f32_ap
        self.nbytes = nbytes
        self.off = 0

    def reset(self):
        self.off = 0

    def _take(self, nbytes):
        o = self.off
        self.off = o + ((nbytes + 3) // 4) * 4
        assert self.off <= self.nbytes, f"arena overflow {self.off} > {self.nbytes}"
        return o

    def abf(self, parts, n):
        o = self._take(2 * n)
        return self.bf[0:parts, o // 2: o // 2 + n]

    def af32(self, parts, n):
        o = self._take(4 * n)
        return self.f32[0:parts, o // 4: o // 4 + n]


def build(TOK=4096, PRE=4096, dbg=()):
    nc = bass.Bass("TRN2", target_bir_lowering=False, debug=False)

    def din(name, shape, dt):
        return nc.dram_tensor(name, shape, dt, kind="ExternalInput").ap()

    def dscr(name, shape, dt):
        kind = "ExternalOutput" if name in dbg else "Internal"
        return nc.dram_tensor(name, shape, dt, kind=kind).ap()

    x_own = din("x_own", [TOK, D], F32)
    x_pre = din("x_pre", [PRE, D], F32)
    pbias = din("pbias", [128, 1], F32)
    wq = din("wq", [D, 1024], F32)
    wk = din("wk", [D, 1024], F32)
    wv = din("wv", [D, 2048], F32)
    wg = din("wg", [D, 2048], F32)
    ws = din("ws", [D, 1024], F32)
    wo = din("wo", [2048, D], F32)
    wgu = din("wgu", [2 * HF, D], F32)
    wp = din("wp", [D, HF], F32)
    gnw = din("gnw", [1, 128], F32)
    umask_in = din("umask", [128, 128], BF16)
    eye_in = din("eye", [128, 128], BF16)

    out = nc.dram_tensor("out", [TOK, D], F32, kind="ExternalOutput").ap()

    wq_b = dscr("wq_b", [D, 1024], BF16)
    wk_b = dscr("wk_b", [D, 1024], BF16)
    ws_b = dscr("ws_b", [D, 1024], BF16)
    wv_b = dscr("wv_b", [D, 2048], BF16)
    wg_b = dscr("wg_b", [D, 2048], BF16)
    wo_b = dscr("wo_b", [2048, D], BF16)
    wgu_q = dscr("wgu_q", [2 * HF, D], BF16)
    wp_q = dscr("wp_q", [D, HF], BF16)
    xn_own = dscr("xn_own", [D, TOK], BF16)
    xn_pre = dscr("xn_pre", [D, PRE], BF16)
    h_scr = dscr("h_scr", [HF, TOK], BF16)
    att_scr = dscr("att_scr", [TOK, D], F32)

    bld = Bld()
    st_own = TOK // 512
    st_pre = PRE // 512

    with ExitStack() as stack:
        sb = stack.enter_context
        cU = sb(nc.sbuf_tensor("cU", [128, 128], BF16))
        cEye = sb(nc.sbuf_tensor("cEye", [128, 128], BF16))
        cU4 = sb(nc.sbuf_tensor("cU4", [128, 512], BF16))
        cOne11 = sb(nc.sbuf_tensor("cOne11", [1, 1], F32))
        cGnw = sb(nc.sbuf_tensor("cGnw", [128, 2048], BF16))
        cBias32 = sb(nc.sbuf_tensor("cBias32", [128, 1], F32))
        cOnesColBf = sb(nc.sbuf_tensor("cOnesColBf", [128, 1], BF16))
        cRND = sb(nc.sbuf_tensor("cRND", [128, 1], F32))
        cRNDn = sb(nc.sbuf_tensor("cRNDn", [128, 1], F32))
        cPBias = sb(nc.sbuf_tensor("cPBias", [128, 1], F32))
        hk_sb = sb(nc.sbuf_tensor("hk_sb", [64, 1024], F32))[:]
        hk_bfs = [sb(nc.sbuf_tensor(f"hk_bf{i}", [64, 1024], BF16)) for i in range(2)]
        hv_big = sb(nc.sbuf_tensor("hv_sb", [128, 2048], F32))
        hv_sb = hv_big[0:64, :]
        s_prev = hv_big[64:65, 0:1024]
        cOnes1 = hv_big[64:65, 1024:1152]
        gn_row = hv_big[64:65, 1152:1280]
        hv_bfs = [sb(nc.sbuf_tensor(f"hv_bf{i}", [64, 2048], BF16)) for i in range(2)]
        rmsacc = sb(nc.sbuf_tensor("rmsacc", [128, TOK // 128], F32))
        rstdt = sb(nc.sbuf_tensor("rstdt", [128, TOK // 128], F32))
        ARN = 178 * 1024 - 128
        arena_t = sb(nc.sbuf_tensor("arena", [128, ARN // 2], BF16))
        ar = Arena(arena_t[:], arena_t[:].bitcast(F32), ARN)
        ps = [sb(nc.psum_tensor(f"ps{i}", [128, 512], F32)) for i in range(6)]
        psT = [sb(nc.psum_tensor(f"psT{i}", [128, 1024], BF16)) for i in range(2)]
        sems = {e: sb(nc.semaphore(f"sem_{e}")) for e in Bld.ENGS}

        ev_u = bld.dma("d", cU[:], umask_in)
        ev_e = bld.dma("d", cEye[:], eye_in)
        ev_gn = bld.dma("d", gn_row, gnw)
        ev_pb = bld.dma("d", cPBias[:], pbias)
        bld.op("v", lambda v: v.memset(cOnes1[:], 1.0))
        bld.op("v", lambda v: v.memset(cOne11[:], 1.0))
        bld.op("v", lambda v: v.memset(cBias32[:], -32.0))
        bld.op("v", lambda v: v.memset(cOnesColBf[:], 1.0))
        bld.op("v", lambda v: v.memset(cRND[:], RND))
        bld.op("v", lambda v: v.memset(cRNDn[:], -RND))
        bld.op("v", lambda v: v.memset(s_prev[:], 0.0))
        bld.op("v", lambda v: v.memset(hk_sb[:], 0.0))
        bld.op("v", lambda v: v.memset(hk_bfs[0][:], 0.0))
        bld.op("v", lambda v: v.memset(hk_bfs[1][:], 0.0))
        bld.op("v", lambda v: v.memset(hv_sb[:], 0.0))
        bld.op("v", lambda v: v.memset(hv_bfs[0][:], 0.0))
        bld.op("v", lambda v: v.memset(hv_bfs[1][:], 0.0))
        bld.op("v", lambda v: v.memset(rmsacc[:], 0.0))
        ev_u4 = bld.op("v", lambda v: v.tensor_copy(
            cU4[:].rearrange("p (r c) -> p r c", c=128),
            cU[:].rearrange("p c -> p () c").broadcast_to([128, 4, 128])), deps=[ev_u])
        ev_g1 = bld.op("t", lambda t: t.matmul(ps[0][:, 0:128], cOnes1[0:1, :], gn_row,
                                               start=True, stop=True), deps=[ev_gn])
        gtmp = ar.abf(128, 128)
        ev_g2 = bld.op("v", lambda v: v.tensor_copy(gtmp, ps[0][:, 0:128]), deps=[ev_g1])
        ev_g3 = bld.op("v", lambda v: v.tensor_copy(
            cGnw[:].rearrange("p (r c) -> p r c", c=128),
            gtmp.rearrange("p c -> p () c").broadcast_to([128, 16, 128])), deps=[ev_g2])
        bld.barrier()
        ar.reset()

        wf = [ar.af32(128, 1024) for _ in range(2)]
        wbb = [ar.abf(128, 1024) for _ in range(2)]
        cast_jobs = []
        for (src, dst, width) in ((wq, wq_b, 1024), (wk, wk_b, 1024), (ws, ws_b, 1024),
                                  (wv, wv_b, 2048), (wg, wg_b, 2048), (wo, wo_b, 2048)):
            for r in range(16):
                for cc in range(width // 1024):
                    cast_jobs.append((src, dst, r, cc))
        prev_read = [None, None]
        prev_st = [None, None]
        for i, (src, dst, r, cc) in enumerate(cast_jobs):
            sl = i % 2
            evl = bld.dma("d", wf[sl], src[r * 128:(r + 1) * 128, cc * 1024:(cc + 1) * 1024],
                          deps=[prev_read[sl]])
            evc = bld.op("s", lambda s, sl=sl: s.copy(wbb[sl], wf[sl]), deps=[evl, prev_st[sl]])
            prev_read[sl] = evc
            evs_ = bld.dma("g", dst[r * 128:(r + 1) * 128, cc * 1024:(cc + 1) * 1024], wbb[sl],
                           deps=[evc])
            prev_st[sl] = evs_
        bld.barrier()
        ar.reset()

        def mk_qslots(width, nsl):
            out = []
            for _ in range(nsl):
                out.append(dict(
                    wb=ar.abf(128, width), sc=ar.af32(128, width // 64),
                    scb=ar.abf(128, width // 64), scf2=ar.af32(128, width // 64),
                    rin=ar.af32(128, width // 64), q1=ar.abf(128, width),
                    wh=ar.abf(128, width)))
            qtmp = ar.af32(128, width)
            for o in out:
                o["qtmp"] = qtmp
            return out

        def quant_front(wf32, width, bufs, deps):
            ng = width // 64
            e1 = bld.op("g", lambda g: g.tensor_copy(bufs["wb"], wf32), deps=deps)
            e2 = bld.op("v", lambda v: v.tensor_reduce(
                bufs["sc"], bufs["wb"].rearrange("p (g k) -> p g k", k=64), AX.X, OP.add,
                apply_absolute_value=True), deps=[e1])
            e3 = bld.op("v", lambda v: v.tensor_scalar(bufs["scb"], bufs["sc"], 1.0 / 64, 1e-8,
                                                       OP.mult, OP.max), deps=[e2])
            e3b = bld.op("s", lambda s: s.copy(bufs["scf2"], bufs["scb"]), deps=[e3])
            e4 = bld.op("v", lambda v: v.reciprocal(bufs["rin"], bufs["scf2"]), deps=[e3b])
            return e4

        def q_stage_quo(width, bufs, front_ev):
            ng = width // 64
            return bld.op("g", lambda g: g.tensor_tensor(
                bufs["q1"].rearrange("p (g k) -> p g k", k=64),
                bufs["wb"].rearrange("p (g k) -> p g k", k=64),
                bufs["rin"].rearrange("p g -> p g ()").broadcast_to([128, ng, 64]), OP.mult),
                deps=[front_ev])

        def q_stage_round(bufs, quo_ev, qtmp_war):
            e6a = bld.op("s", lambda s: s.activation(bufs["qtmp"], bufs["q1"], AF.Identity,
                                                     bias=cRND[:], scale=1.0),
                         deps=[quo_ev, qtmp_war])
            e6b = bld.op("s", lambda s: s.activation(bufs["q1"], bufs["qtmp"], AF.Identity,
                                                     bias=cRNDn[:], scale=1.0), deps=[e6a])
            return e6b

        def q_stage_fin(width, bufs, round_ev, war_wh):
            ng = width // 64
            e7 = bld.op("v", lambda v: v.tensor_scalar(bufs["q1"], bufs["q1"], 1.0, -1.0,
                                                       OP.min, OP.max), deps=[round_ev])
            e8 = bld.op("g", lambda g: g.tensor_tensor(
                bufs["wh"].rearrange("p (g k) -> p g k", k=64),
                bufs["q1"].rearrange("p (g k) -> p g k", k=64),
                bufs["scb"].rearrange("p g -> p g ()").broadcast_to([128, ng, 64]), OP.mult),
                deps=[e7, war_wh])
            return e8

        NSL = 4
        wfq = [ar.af32(128, 2048) for _ in range(NSL)]
        qs = mk_qslots(2048, NSL)
        jobs = [("gu", i) for i in range(128)] + [("wp", i) for i in range(64)]
        NJ = len(jobs)

        def q_src(job):
            kind, idx = job
            if kind == "gu":
                return wgu[idx * 128:(idx + 1) * 128, :]
            r, cc = idx // 4, idx % 4
            return wp[r * 128:(r + 1) * 128, cc * 2048:(cc + 1) * 2048]

        def q_dst(job):
            kind, idx = job
            if kind == "gu":
                return wgu_q[idx * 128:(idx + 1) * 128, :]
            r, cc = idx // 4, idx % 4
            return wp_q[r * 128:(r + 1) * 128, cc * 2048:(cc + 1) * 2048]

        ld_ev = {}
        fr_ev = {}
        quo_ev = {}
        rnd_ev = {}
        st_ev = {}
        qtmp_last = [None]
        qn = [0]

        def prep_step():
            n = qn[0]
            if n >= NJ + 4:
                return False
            qn[0] += 1
            if n < NJ:
                sl = n % NSL
                ld_ev[n] = bld.dma("d", wfq[sl], q_src(jobs[n]),
                                   deps=[fr_ev.get(n - NSL)])
            if n >= 1 and n - 1 < NJ:
                k = n - 1
                fr_ev[k] = quant_front(wfq[k % NSL], 2048, qs[k % NSL], [ld_ev[k]])
            if n >= 2 and n - 2 < NJ:
                k = n - 2
                quo_ev[k] = q_stage_quo(2048, qs[k % NSL], fr_ev[k])
            if n >= 3 and n - 3 < NJ:
                k = n - 3
                rnd_ev[k] = q_stage_round(qs[k % NSL], quo_ev[k], qtmp_last[0])
                qtmp_last[0] = rnd_ev[k]
            if n >= 4 and n - 4 < NJ:
                k = n - 4
                e8 = q_stage_fin(2048, qs[k % NSL], rnd_ev[k], st_ev.get(k - NSL))
                st_ev[k] = bld.dma("g", q_dst(jobs[k]), qs[k % NSL]["wh"], deps=[e8])
            return True

        _lnb = dict(
            xb=[ar.af32(128, 2048) for _ in range(2)],
            sqs=ar.af32(128, 2048), sqacc=ar.af32(128, 2), mcol=ar.af32(128, 2),
            m2c=ar.af32(128, 1), vv=ar.af32(128, 1), rr=ar.af32(128, 1),
            xnt=[ar.abf(128, 2048) for _ in range(2)], xnasm=ar.abf(128, 16 * 512))

        def ln_pass(x_src, xn_dst, ntok, filler=None):
            xb = _lnb["xb"]
            sqs = _lnb["sqs"]
            sqacc = _lnb["sqacc"]
            mcol = _lnb["mcol"]
            m2c = _lnb["m2c"]
            vv = _lnb["vv"]
            rr = _lnb["rr"]
            xnt = _lnb["xnt"]
            xnasm = _lnb["xnasm"]
            prev_rd = [ln_pass.carry_rd[0], ln_pass.carry_rd[1]]
            prev_tr = [ln_pass.carry_tr[0], ln_pass.carry_tr[1]]
            asm_dma = [ln_pass.carry_asm]
            tevl = [ln_pass.carry_tev[0], ln_pass.carry_tev[1]]
            fr = {}

            def ln_front(i):
                sl = i % 2
                evl = bld.dma("d", xb[sl], x_src[i * 128:(i + 1) * 128, :], deps=[prev_rd[sl]])
                e1 = bld.op("s", lambda s, sl=sl: s.activation(sqs, xb[sl], AF.Square,
                                                               accum_out=sqacc[:, sl:sl + 1]),
                            deps=[evl, fr.get(("sq", i - 2))])
                e2 = bld.op("v", lambda v, sl=sl: v.tensor_reduce(mcol[:, sl:sl + 1], xb[sl],
                                                                  AX.X, OP.add),
                            deps=[evl, fr.get(("mc", i - 2))])
                fr[("e1", i)] = e1
                fr[("e2", i)] = e2

            def ln_back(i):
                sl = i % 2
                e1 = fr.pop(("e1", i))
                e2 = fr.pop(("e2", i))
                e3 = bld.op("v", lambda v, sl=sl: v.tensor_scalar(mcol[:, sl:sl + 1], mcol[:, sl:sl + 1],
                                                                  1.0 / 2048, None, OP.mult), deps=[e2])
                e4 = bld.op("v", lambda v, sl=sl: v.tensor_scalar(vv, sqacc[:, sl:sl + 1],
                                                                  1.0 / 2048, None, OP.mult), deps=[e1])
                e5 = bld.op("v", lambda v, sl=sl: v.tensor_tensor(m2c, mcol[:, sl:sl + 1],
                                                                  mcol[:, sl:sl + 1], OP.mult), deps=[e3])
                e6 = bld.op("v", lambda v: v.tensor_tensor(vv, vv, m2c, OP.subtract), deps=[e4, e5])
                e7 = bld.op("v", lambda v: v.tensor_scalar(vv, vv, 1e-5, None, OP.add), deps=[e6])
                e8 = bld.op("s", lambda s: s.activation(vv, vv, AF.Sqrt), deps=[e7])
                e9v = bld.op("v", lambda v: v.reciprocal(rr, vv), deps=[e8])
                e9 = bld.op("v", lambda v, sl=sl: v.tensor_scalar(
                    xnt[sl], xb[sl], mcol[:, sl:sl + 1], rr, OP.subtract, OP.mult),
                    deps=[e9v, e3, prev_tr[sl]])
                fr[("sq", i)] = e9
                fr[("mc", i)] = e9
                prev_rd[sl] = e9
                tcol = (i % 4) * 128
                evs16 = []
                for j in range(16):
                    tb = (i * 16 + j) % 2
                    evt = bld.op("t", lambda t, j=j, tb=tb, sl=sl: t.transpose(
                        psT[tb][:, (j % 8) * 128:(j % 8) * 128 + 128],
                        xnt[sl][:, j * 128:(j + 1) * 128], cEye[:]),
                        deps=[e9, tevl[tb]])
                    eng = "v" if j % 2 == 0 else "s"
                    def evac(e, j=j, tb=tb, tcol=tcol):
                        dstv = xnasm[:, j * 512 + tcol: j * 512 + tcol + 128]
                        srcv = psT[tb][:, (j % 8) * 128:(j % 8) * 128 + 128]
                        return e.tensor_copy(dstv, srcv) if j % 2 == 0 else e.copy(dstv, srcv)
                    eve = bld.op(eng, evac, deps=[evt, asm_dma[0]])
                    tevl[tb] = eve
                    evs16.append(eve)
                prev_tr[sl] = evs16[-1]
                if i % 4 == 3:
                    g0 = (i // 4) * 512
                    devs = []
                    for j in range(16):
                        devs.append(bld.dma("g", xn_dst[j * 128:(j + 1) * 128, g0:g0 + 512],
                                            xnasm[:, j * 512:(j + 1) * 512], deps=evs16))
                    asm_dma[0] = devs[-1]

            nt = ntok // 128
            for i in range(nt + 1):
                if i < nt:
                    ln_front(i)
                if i >= 1:
                    ln_back(i - 1)
                    if filler is not None:
                        for _ in range(6):
                            filler()
            ln_pass.carry_rd = prev_rd
            ln_pass.carry_tr = prev_tr
            ln_pass.carry_asm = asm_dma[0]
            ln_pass.carry_tev = tevl

        ln_pass.carry_rd = [None, None]
        ln_pass.carry_tr = [None, None]
        ln_pass.carry_asm = None
        ln_pass.carry_tev = [None, None]
        ln_pass(x_own, xn_own, TOK, prep_step)
        ln_pass(x_pre, xn_pre, PRE, prep_step)
        while prep_step():
            pass
        bld.barrier()
        ar.reset()

        pbank_ev = {i: None for i in range(6)}
        pbank_ev["psT0"] = None
        pbank_ev["psT1"] = None

        def proj_tm(xnb_ap, xnb_ev, wsrc, wwidth, wbufs, wbuf_ev, evac_fn):
            evs = []
            for ng in range(wwidth // 512):
                sl = ng % 2
                rr_ = wsrc[:, ng * 512:(ng + 1) * 512].rearrange("(dc p) n -> p dc n", p=128)
                weva = bld.dma("d", wbufs[sl][0].rearrange("p (dc n) -> p dc n", n=512),
                               rr_[:, 0:8, :], deps=[wbuf_ev[sl]])
                wevb = bld.dma("d", wbufs[sl][1].rearrange("p (dc n) -> p dc n", n=512),
                               rr_[:, 8:16, :], deps=[wbuf_ev[sl]])
                for tc in range(4):
                    bi = 4 + (ng * 4 + tc) % 2
                    bank = ps[bi]
                    def mmfn(t, tc=tc, sl=sl, bank=bank):
                        mm = None
                        for dc in range(16):
                            wv_ = wbufs[sl][dc // 8][:, (dc % 8) * 512:(dc % 8 + 1) * 512]
                            mm = t.matmul(bank[:],
                                          xnb_ap[:, dc * 512 + tc * 128: dc * 512 + tc * 128 + 128],
                                          wv_, start=(dc == 0), stop=(dc == 15))
                        return mm
                    evm = bld.op("t", mmfn, deps=[weva, wevb, xnb_ev, pbank_ev[bi]])
                    eve = evac_fn(tc, ng, bank, evm)
                    pbank_ev[bi] = eve
                    evs.append(eve)
                wbuf_ev[sl] = evs[-1]
            return evs

        def proj_fm(xnb_ap, xnb_ev, wsrc, fwidth, wbufs, wbuf_ev, evac_fn):
            evs = []
            for fh in range(fwidth // 512):
                sl = fh % 2
                rr_ = wsrc[:, fh * 512:(fh + 1) * 512].rearrange("(dc p) n -> p dc n", p=128)
                weva = bld.dma("d", wbufs[sl][0].rearrange("p (dc n) -> p dc n", n=512),
                               rr_[:, 0:8, :], deps=[wbuf_ev[sl]])
                wevb = bld.dma("d", wbufs[sl][1].rearrange("p (dc n) -> p dc n", n=512),
                               rr_[:, 8:16, :], deps=[wbuf_ev[sl]])
                for f128 in range(4):
                    bi = 4 + (fh * 4 + f128) % 2
                    bank = ps[bi]
                    def mmfn(t, f128=f128, sl=sl, bank=bank):
                        mm = None
                        for dc in range(16):
                            wv_ = wbufs[sl][dc // 8][:, (dc % 8) * 512 + f128 * 128:
                                                     (dc % 8) * 512 + f128 * 128 + 128]
                            mm = t.matmul(bank[:], wv_,
                                          xnb_ap[:, dc * 512:(dc + 1) * 512],
                                          start=(dc == 0), stop=(dc == 15))
                        return mm
                    evm = bld.op("t", mmfn, deps=[weva, wevb, xnb_ev, pbank_ev[bi]])
                    eve = evac_fn(fh, f128, bank, evm)
                    pbank_ev[bi] = eve
                    evs.append(eve)
                wbuf_ev[sl] = evs[-1]
            return evs

        ar.reset()
        xnb = ar.abf(128, 16 * 512)
        wbufs = [(ar.abf(128, 4096), ar.abf(128, 4096)) for _ in range(2)]
        a_tm = ar.abf(128, 4 * 1024)
        v_st = ar.abf(128, 4 * 2048)
        k_tm_st = ar.abf(128, 4 * 1024)
        stmp = [ar.af32(128, 512) for _ in range(2)]
        wbuf_ev = [None, None]
        xnb_last = [None]
        hk_ev = [None]
        hv_ev = [None]
        sprev_ev = [None]
        stmp_ev = [None, None]
        a_war = {tc: None for tc in range(4)}
        a_wr = {}
        v_wr = {}
        k_wr = {}

        def emit_prefix_st(st, xn_src, bias_ap, bias_dep):
            devs = []
            for dc in range(16):
                devs.append(bld.dma("d", xnb[:, dc * 512:(dc + 1) * 512],
                                    xn_src[dc * 128:(dc + 1) * 128, st * 512:(st + 1) * 512],
                                    deps=[xnb_last[0]]))
            xnb_ev = devs[-1]

            def evac_s(tc, ng, bank, evm):
                sl2 = (tc * 2 + ng) % 2
                e1 = bld.op("v", lambda v, sl2=sl2, bank=bank: v.tensor_scalar(
                    stmp[sl2], bank[:], 32.0, -32.0, OP.min, OP.max),
                    deps=[evm, stmp_ev[sl2]])
                e2 = bld.op("s", lambda s, tc=tc, ng=ng, sl2=sl2: s.activation(
                    a_tm[:, tc * 1024 + ng * 512: tc * 1024 + ng * 512 + 512],
                    stmp[sl2], AF.Exp, bias=bias_ap, scale=1.0),
                    deps=[e1, bias_dep, a_war[tc]])
                stmp_ev[sl2] = e2
                a_wr[(tc, ng)] = e2
                return e2
            proj_tm(xnb, xnb_ev, ws_b, 1024, wbufs, wbuf_ev, evac_s)

            def evac_v(tc, ng, bank, evm):
                e = bld.op("v", lambda v, tc=tc, ng=ng, bank=bank: v.tensor_copy(
                    v_st[:, tc * 2048 + ng * 512: tc * 2048 + ng * 512 + 512], bank[:]),
                    deps=[evm, v_wr.get((tc, ng))])
                v_wr[(tc, ng)] = e
                return e
            proj_tm(xnb, xnb_ev, wv_b, 2048, wbufs, wbuf_ev, evac_v)

            def evac_k(tc, ng, bank, evm):
                e = bld.op("v", lambda v, tc=tc, ng=ng, bank=bank: v.tensor_copy(
                    k_tm_st[:, tc * 1024 + ng * 512: tc * 1024 + ng * 512 + 512], bank[:]),
                    deps=[evm, k_wr.get((tc, ng))])
                k_wr[(tc, ng)] = e
                return e
            evk = proj_tm(xnb, xnb_ev, wk_b, 1024, wbufs, wbuf_ev, evac_k)
            xnb_last[0] = evk[-1]

            sa_mm = [None, None]
            hk_mm = {}
            hv_mm = {}
            for tc in range(4):
                adep = [a_wr[(tc, 0)], a_wr[(tc, 1)]]
                vdep = [v_wr[(tc, ng)] for ng in range(4)]
                kdep = [k_wr[(tc, 0)], k_wr[(tc, 1)]]
                for half in range(2):
                    e = bld.op("t", lambda t, tc=tc, half=half: t.matmul(
                        psT[half][:].bitcast(F32)[0:1, 0:512], cOnesColBf[:],
                        a_tm[:, tc * 1024 + half * 512: tc * 1024 + (half + 1) * 512],
                        start=(tc == 0), stop=(tc == 3)),
                        deps=adep + ([pbank_ev[f"psT{half}"]] if tc == 0 else []))
                    sa_mm[half] = e
                for half in range(2):
                    bank = ps[4 + half]
                    def mmhk(t, tc=tc, half=half, bank=bank):
                        mm = None
                        for hh in range(8):
                            h_ = half * 8 + hh
                            mm = t.matmul(bank[0:64, hh * 64:(hh + 1) * 64],
                                          k_tm_st[:, tc * 1024 + h_ * 64: tc * 1024 + (h_ + 1) * 64],
                                          a_tm[:, tc * 1024 + h_ * 64: tc * 1024 + (h_ + 1) * 64],
                                          start=(tc == 0 and hh == 0), stop=(tc == 3 and hh == 7),
                                          skip_group_check=True)
                        return mm
                    hk_mm[half] = bld.op("t", mmhk, deps=adep + kdep +
                                         ([pbank_ev[4 + half]] if tc == 0 else []))
                for qtr in range(4):
                    bank = ps[qtr]
                    def mmhv(t, tc=tc, qtr=qtr, bank=bank):
                        mm = None
                        for hh in range(4):
                            h_ = qtr * 4 + hh
                            mm = t.matmul(bank[0:64, hh * 128:(hh + 1) * 128],
                                          a_tm[:, tc * 1024 + h_ * 64: tc * 1024 + (h_ + 1) * 64],
                                          v_st[:, tc * 2048 + h_ * 128: tc * 2048 + (h_ + 1) * 128],
                                          start=(tc == 0 and hh == 0), stop=(tc == 3 and hh == 3),
                                          skip_group_check=True)
                        return mm
                    hv_mm[qtr] = bld.op("t", mmhv, deps=adep + vdep +
                                        ([pbank_ev[qtr]] if tc == 0 else []))
                a_war[tc] = None
            for half in range(2):
                e = bld.op("v", lambda v, half=half: v.tensor_tensor(
                    s_prev[0:1, half * 512:(half + 1) * 512],
                    s_prev[0:1, half * 512:(half + 1) * 512],
                    psT[half][:].bitcast(F32)[0:1, 0:512], OP.add),
                    deps=[sa_mm[half], sprev_ev[0]])
                pbank_ev[f"psT{half}"] = e
                sprev_ev[0] = e
            for half in range(2):
                e1 = bld.op("v", lambda v, half=half: v.tensor_tensor(
                    hk_sb[:, half * 512:(half + 1) * 512], hk_sb[:, half * 512:(half + 1) * 512],
                    ps[4 + half][0:64, :], OP.add), deps=[hk_mm[half], hk_ev[0]])
                pbank_ev[4 + half] = e1
                hk_ev[0] = e1
            for qtr in range(4):
                e1 = bld.op("v", lambda v, qtr=qtr: v.tensor_tensor(
                    hv_sb[:, qtr * 512:(qtr + 1) * 512], hv_sb[:, qtr * 512:(qtr + 1) * 512],
                    ps[qtr][0:64, :], OP.add), deps=[hv_mm[qtr], hv_ev[0]])
                pbank_ev[qtr] = e1
                hv_ev[0] = e1

        for st in range(st_pre):
            emit_prefix_st(st, xn_pre, cPBias[:], ev_pb)
        hkbf_wr = {0: bld.op("v", lambda v: v.tensor_copy(hk_bfs[0][:], hk_sb[:]),
                             deps=[hk_ev[0]]), 1: None}
        hvbf_wr = {0: bld.op("v", lambda v: v.tensor_copy(hv_bfs[0][:], hv_sb[:]),
                             deps=[hv_ev[0]]), 1: None}
        hkbf_rd = {0: None, 1: None}
        hvbf_rd = {0: None, 1: None}
        bld.barrier()
        ar.reset()

        xnb = ar.abf(128, 16 * 512)
        wbufs = [(ar.abf(128, 4096), ar.abf(128, 4096)) for _ in range(2)]
        a_tm = ar.abf(128, 4 * 1024)
        v_st = ar.abf(128, 4 * 2048)
        o_st = ar.abf(128, 4 * 2048)
        og_fm = ar.abf(128, 16 * 512)
        qh = [ar.abf(64, 512) for _ in range(16)]
        kh = [ar.abf(64, 512) for _ in range(16)]
        stmp = [ar.af32(128, 512) for _ in range(2)]
        srep_sb = ar.af32(128, 1024)
        esb = srep_sb
        sfull = ar.af32(128, 1024)
        sinv = ar.af32(128, 1024)
        kqc = ar.abf(128, 1024)
        okt = ar.af32(128, 1024)
        u_tm = ar.abf(128, 1024)
        ptc = kqc
        o2 = u_tm
        mx = ar.af32(128, 16)
        sm = ar.af32(128, 16)
        smr = ar.af32(128, 16)
        rr_st = ar.af32(128, 64)
        ogt = ar.abf(128, 512)
        sgt = stmp[0]
        sggt = kqc[:, 512:1024]
        afm = [ar.abf(64, 128) for _ in range(16)]
        ufm = [ar.abf(64, 128) for _ in range(16)]
        ktm = [ar.abf(128, 64) for _ in range(16)]
        attb = stmp

        ptbank_ev = {}
        wbuf_ev = [None, None]
        xnb_last = [None]
        stmp_ev = [None, None]
        a_war = {tc: None for tc in range(4)}
        a_wr, v_wr, q_wr, k_wr = {}, {}, {}, {}
        og_wr = {}
        attb_ev = [None, None]
        cb = {"pk0": None, "pk1": None, "okt": None, "u": None,
              "sfull": None, "sinv": None, "srep": None, "mx": None, "sm": None, "smr": None,
              "ogt": None}
        afm_ev = [None] * 16
        ufm_ev = [None] * 16
        ktm_ev = [None] * 16
        rr_war = [None]
        o_wr = {}

        def emit_main_st(st):
            devs = []
            for dc in range(16):
                devs.append(bld.dma("d", xnb[:, dc * 512:(dc + 1) * 512],
                                    xn_own[dc * 128:(dc + 1) * 128, st * 512:(st + 1) * 512],
                                    deps=[xnb_last[0]]))
            xnb_ev = devs[-1]

            def evac_q(fh, f128, bank, evm):
                h0 = fh * 8 + f128 * 2
                bld.op("v", lambda v, h0=h0, bank=bank: v.tensor_copy(qh[h0][:], bank[0:64, :]),
                       deps=[evm, q_wr.get(h0)])
                e2 = bld.op("v", lambda v, h0=h0, bank=bank: v.tensor_copy(qh[h0 + 1][:], bank[64:128, :]),
                            deps=[evm, q_wr.get(h0 + 1)])
                q_wr[h0] = e2
                q_wr[h0 + 1] = e2
                return e2
            proj_fm(xnb, xnb_ev, wq_b, 1024, wbufs, wbuf_ev, evac_q)

            def evac_k2(fh, f128, bank, evm):
                h0 = fh * 8 + f128 * 2
                bld.op("v", lambda v, h0=h0, bank=bank: v.tensor_copy(kh[h0][:], bank[0:64, :]),
                       deps=[evm, k_wr.get(h0)])
                e2 = bld.op("v", lambda v, h0=h0, bank=bank: v.tensor_copy(kh[h0 + 1][:], bank[64:128, :]),
                            deps=[evm, k_wr.get(h0 + 1)])
                k_wr[h0] = e2
                k_wr[h0 + 1] = e2
                return e2
            proj_fm(xnb, xnb_ev, wk_b, 1024, wbufs, wbuf_ev, evac_k2)

            def evac_s(tc, ng, bank, evm):
                sl2 = (tc * 2 + ng) % 2
                e1 = bld.op("v", lambda v, sl2=sl2, bank=bank: v.tensor_scalar(
                    stmp[sl2], bank[:], 32.0, -32.0, OP.min, OP.max),
                    deps=[evm, stmp_ev[sl2]])
                e2 = bld.op("s", lambda s, tc=tc, ng=ng, sl2=sl2: s.activation(
                    a_tm[:, tc * 1024 + ng * 512: tc * 1024 + ng * 512 + 512],
                    stmp[sl2], AF.Exp, bias=cBias32[:], scale=1.0),
                    deps=[e1, a_war[tc]])
                stmp_ev[sl2] = e2
                a_wr[(tc, ng)] = e2
                return e2
            proj_tm(xnb, xnb_ev, ws_b, 1024, wbufs, wbuf_ev, evac_s)

            def evac_v(tc, ng, bank, evm):
                e = bld.op("v", lambda v, tc=tc, ng=ng, bank=bank: v.tensor_copy(
                    v_st[:, tc * 2048 + ng * 512: tc * 2048 + ng * 512 + 512], bank[:]),
                    deps=[evm, v_wr.get((tc, ng))])
                v_wr[(tc, ng)] = e
                return e
            evg = proj_tm(xnb, xnb_ev, wv_b, 2048, wbufs, wbuf_ev, evac_v)
            xnb_last[0] = evg[-1]

            for tc in range(4):
                cidx = st * 4 + tc
                rp = cidx % 2
                wpar = (cidx + 1) % 2
                banksel = [0, 1, 4, 5]
                adep = [a_wr[(tc, 0)], a_wr[(tc, 1)]]
                vdep = [v_wr[(tc, ng)] for ng in range(4)]
                evS = []
                for half in range(2):
                    e = bld.op("t", lambda t, tc=tc, half=half: t.matmul(
                        ps[half][:], cU[:],
                        a_tm[:, tc * 1024 + half * 512: tc * 1024 + (half + 1) * 512],
                        start=True, stop=True),
                        deps=adep + [ev_u, pbank_ev[half]])
                    evS.append(e)
                evR = []
                for half in range(2):
                    e = bld.op("t", lambda t, half=half: t.matmul(
                        ps[2 + half][:], cOnes1[0:1, :], s_prev[0:1, half * 512:(half + 1) * 512],
                        start=True, stop=True),
                        deps=[sprev_ev[0], pbank_ev[2 + half]])
                    evR.append(e)
                sA = []
                for half in range(2):
                    e = bld.op("t", lambda t, tc=tc, half=half: t.matmul(
                        ps[4 + half][0:1, 0:512], cOnesColBf[:],
                        a_tm[:, tc * 1024 + half * 512: tc * 1024 + (half + 1) * 512],
                        start=True, stop=True),
                        deps=adep + [pbank_ev[4 + half]])
                    sA.append(e)
                for half in range(2):
                    e = bld.op("v", lambda v, half=half: v.tensor_tensor(
                        s_prev[0:1, half * 512:(half + 1) * 512],
                        s_prev[0:1, half * 512:(half + 1) * 512],
                        ps[4 + half][0:1, 0:512], OP.add),
                        deps=[sA[half], evR[0], evR[1], sprev_ev[0]])
                    pbank_ev[4 + half] = e
                    sprev_ev[0] = e
                a_war[tc] = None
                e1 = bld.op("v", lambda v: v.tensor_copy(srep_sb[:, 0:512], ps[2][:]),
                            deps=[evR[0], cb["srep"]])
                e2 = bld.op("v", lambda v: v.tensor_copy(srep_sb[:, 512:1024], ps[3][:]),
                            deps=[evR[1]])
                cb["srep"] = e2
                pbank_ev[2] = e1
                pbank_ev[3] = e2
                f1 = bld.op("v", lambda v: v.tensor_tensor(sfull[:, 0:512], ps[0][:],
                                                           srep_sb[:, 0:512], OP.add),
                            deps=[evS[0], e1, cb["sfull"]])
                f2 = bld.op("v", lambda v: v.tensor_tensor(sfull[:, 512:1024], ps[1][:],
                                                           srep_sb[:, 512:1024], OP.add),
                            deps=[evS[1], e2])
                cb["sfull"] = f2
                pbank_ev[0] = f1
                pbank_ev[1] = f2
                evsi = bld.op("v", lambda v: v.reciprocal(sinv[:], sfull[:]),
                              deps=[f1, f2, cb["sinv"]])
                cb["sinv"] = evsi
                kq_evac = {}
                for grp in range(4):
                    bi = 4 + grp % 2
                    bank = ps[bi]
                    def mmkq(t, tc=tc, grp=grp, bank=bank):
                        mm = None
                        for hh in range(4):
                            h_ = grp * 4 + hh
                            mm = t.matmul(bank[:, hh * 128:(hh + 1) * 128],
                                          kh[h_][:, tc * 128:(tc + 1) * 128],
                                          qh[h_][:, tc * 128:(tc + 1) * 128],
                                          start=True, stop=True)
                        return mm
                    hdeps = [k_wr[grp * 4], k_wr[grp * 4 + 2], q_wr[grp * 4], q_wr[grp * 4 + 2]]
                    e = bld.op("t", mmkq, deps=hdeps + [pbank_ev[bi]])
                    slot = grp % 2
                    ee = bld.op("v", lambda v, slot=slot, bank=bank: v.tensor_tensor(
                        kqc[:, slot * 512:(slot + 1) * 512], bank[:], cU4[:], OP.mult),
                        deps=[e, ev_u4, cb[f"pk{slot}"]])
                    pbank_ev[bi] = ee
                    cb[f"pk{grp % 2}"] = ee
                    kq_evac[grp] = ee
                ok_ev = []
                for half in range(2):
                    bank = ps[half]
                    def mmok(t, tc=tc, half=half, bank=bank):
                        mm = None
                        for hh in range(8):
                            h_ = half * 8 + hh
                            slot = (h_ // 4) % 2
                            mm = t.matmul(bank[:, hh * 64:(hh + 1) * 64],
                                          kqc[:, slot * 512 + (h_ % 4) * 128: slot * 512 + (h_ % 4 + 1) * 128],
                                          a_tm[:, tc * 1024 + h_ * 64: tc * 1024 + (h_ + 1) * 64],
                                          start=True, stop=False)
                            mm = t.matmul(bank[:, hh * 64:(hh + 1) * 64],
                                          qh[h_][:, tc * 128:(tc + 1) * 128],
                                          hk_bfs[rp][:, h_ * 64:(h_ + 1) * 64],
                                          start=False, stop=True)
                        return mm
                    e = bld.op("t", mmok, deps=[kq_evac[half * 2], kq_evac[half * 2 + 1],
                                                hkbf_wr[rp], pbank_ev[half]] + adep)
                    ok_ev.append(e)
                hkbf_rd[rp] = ok_ev[-1]
                g1 = bld.op("v", lambda v: v.scalar_tensor_tensor(
                    okt[:, 0:512], ps[0][:], SCALE, sinv[:, 0:512], OP.mult, OP.mult),
                    deps=[ok_ev[0], evsi, cb["okt"]])
                g2 = bld.op("v", lambda v: v.scalar_tensor_tensor(
                    okt[:, 512:1024], ps[1][:], SCALE, sinv[:, 512:1024], OP.mult, OP.mult),
                    deps=[ok_ev[1], evsi])
                cb["okt"] = g2
                pbank_ev[0] = g1
                pbank_ev[1] = g2
                m1 = bld.op("v", lambda v: v.tensor_reduce(
                    mx, okt.rearrange("p (h m) -> p h m", m=64), AX.X, OP.max, negate=True),
                    deps=[g1, g2, cb["mx"]])
                cb["mx"] = m1
                m2 = bld.op("v", lambda v: v.tensor_tensor(
                    okt.rearrange("p (h m) -> p h m", m=64),
                    okt.rearrange("p (h m) -> p h m", m=64),
                    mx.rearrange("p h -> p h ()").broadcast_to([128, 16, 64]), OP.add),
                    deps=[m1])
                m3 = bld.op("s", lambda s: s.activation(esb[:], okt[:], AF.Exp), deps=[m2])
                m4 = bld.op("v", lambda v: v.tensor_reduce(
                    sm, esb.rearrange("p (h m) -> p h m", m=64), AX.X, OP.add),
                    deps=[m3, cb["sm"]])
                cb["sm"] = m4
                m5 = bld.op("v", lambda v: v.reciprocal(smr, sm), deps=[m4, cb["smr"]])
                cb["smr"] = m5
                m6 = bld.op("v", lambda v: v.tensor_tensor(
                    esb.rearrange("p (h m) -> p h m", m=64),
                    esb.rearrange("p (h m) -> p h m", m=64),
                    smr.rearrange("p h -> p h ()").broadcast_to([128, 16, 64]), OP.mult),
                    deps=[m5])
                m7 = bld.op("v", lambda v: v.tensor_tensor(u_tm[:], esb[:], sinv[:], OP.mult),
                            deps=[m6, cb["u"]])
                cb["u"] = m7
                tr = {}

                def emit_tr(kind, tc=tc, adep=adep, m7=m7):
                    if kind == "a":
                        srcfn = lambda h_: a_tm[:, tc * 1024 + h_ * 64: tc * 1024 + (h_ + 1) * 64]
                        dstlist, evlist, osz, srcdep = afm, afm_ev, 128, adep
                    elif kind == "u":
                        srcfn = lambda h_: u_tm[:, h_ * 64:(h_ + 1) * 64]
                        dstlist, evlist, osz, srcdep = ufm, ufm_ev, 128, [m7]
                    else:
                        srcfn = lambda h_: kh[h_][:, tc * 128:(tc + 1) * 128]
                        dstlist, evlist, osz = ktm, ktm_ev, 64
                        srcdep = [k_wr[0], k_wr[2], k_wr[4], k_wr[6], k_wr[8], k_wr[10], k_wr[12], k_wr[14]]
                    oparts = 64 if kind != "k" else 128
                    for h_ in range(16):
                        tb = h_ % 2
                        slot = (h_ // 2) % 4
                        idap = cEye[0:64, 0:64] if kind == "k" else cEye[:]
                        def trf(t, h_=h_, tb=tb, slot=slot, srcfn=srcfn, oparts=oparts, osz=osz, idap=idap):
                            return t.transpose(psT[tb][0:oparts, slot * 256: slot * 256 + osz],
                                               srcfn(h_), idap)
                        evt = bld.op("t", trf, deps=srcdep + [ptbank_ev.get(tb)])
                        eng = "v" if h_ % 2 == 0 else "s"
                        def evf(e, h_=h_, tb=tb, slot=slot, dstlist=dstlist, oparts=oparts, osz=osz):
                            srcv = psT[tb][0:oparts, slot * 256: slot * 256 + osz]
                            dstv = dstlist[h_][:]
                            return e.tensor_copy(dstv, srcv) if h_ % 2 == 0 else e.copy(dstv, srcv)
                        eve = bld.op(eng, evf, deps=[evt, evlist[h_]])
                        evlist[h_] = eve
                        ptbank_ev[tb] = eve
                        tr[(kind, h_)] = eve

                emit_tr("a")
                emit_tr("k")

                hkmm = {}
                for half in range(2):
                    bi = 2 + half
                    bank = ps[bi]
                    def mmhk(t, tc=tc, half=half, bank=bank):
                        mm = None
                        for hh in range(8):
                            h_ = half * 8 + hh
                            mm = t.matmul(bank[0:64, hh * 64:(hh + 1) * 64],
                                          ktm[h_][:],
                                          a_tm[:, tc * 1024 + h_ * 64: tc * 1024 + (h_ + 1) * 64],
                                          start=True, stop=True)
                        return mm
                    hkmm[half] = bld.op("t", mmhk, deps=[tr[("k", half * 8 + hh)] for hh in range(8)] +
                                        adep + [pbank_ev[bi]])
                e_add = []
                for half in range(2):
                    e1_ = bld.op("v", lambda v, half=half: v.tensor_tensor(
                        hk_sb[:, half * 512:(half + 1) * 512], hk_sb[:, half * 512:(half + 1) * 512],
                        ps[2 + half][0:64, :], OP.add), deps=[hkmm[half], hk_ev[0]])
                    e_add.append(e1_)
                    pbank_ev[2 + half] = e1_
                hk_ev[0] = e_add[-1]
                hkbf_wr[wpar] = bld.op("v", lambda v: v.tensor_copy(hk_bfs[wpar][:], hk_sb[:]),
                                       deps=e_add + [hkbf_rd[wpar], hkbf_wr[wpar]])
                hvmm = {}
                for qtr in range(4):
                    bi = banksel[qtr]
                    bank = ps[bi]
                    def mmhv(t, tc=tc, qtr=qtr, bank=bank):
                        mm = None
                        for hh in range(4):
                            h_ = qtr * 4 + hh
                            mm = t.matmul(bank[0:64, hh * 128:(hh + 1) * 128],
                                          a_tm[:, tc * 1024 + h_ * 64: tc * 1024 + (h_ + 1) * 64],
                                          v_st[:, tc * 2048 + h_ * 128: tc * 2048 + (h_ + 1) * 128],
                                          start=True, stop=True)
                        return mm
                    hvmm[qtr] = bld.op("t", mmhv, deps=adep + vdep + [pbank_ev[bi]])
                e_add = []
                for qtr in range(4):
                    e1_ = bld.op("v", lambda v, qtr=qtr: v.tensor_tensor(
                        hv_sb[:, qtr * 512:(qtr + 1) * 512], hv_sb[:, qtr * 512:(qtr + 1) * 512],
                        ps[banksel[qtr]][0:64, :], OP.add), deps=[hvmm[qtr], hv_ev[0]])
                    e_add.append(e1_)
                    pbank_ev[banksel[qtr]] = e1_
                hv_ev[0] = e_add[-1]
                hvbf_wr[wpar] = bld.op("v", lambda v: v.tensor_copy(hv_bfs[wpar][:], hv_sb[:]),
                                       deps=e_add + [hvbf_rd[wpar], hvbf_wr[wpar]])

                emit_tr("u")
                pt_evac = {}
                ovi_ev = {}
                o_evs = [None] * 4

                def emit_pt(grp):
                    bi = 2 + grp % 2
                    bank = ps[bi]
                    def mmpt(t, grp=grp, bank=bank):
                        mm = None
                        for hh in range(4):
                            h_ = grp * 4 + hh
                            mm = t.matmul(bank[:, hh * 128:(hh + 1) * 128],
                                          afm[h_][:], ufm[h_][:], start=True, stop=True)
                        return mm
                    hdeps = [tr[("a", grp * 4 + hh)] for hh in range(4)] + \
                            [tr[("u", grp * 4 + hh)] for hh in range(4)]
                    e = bld.op("t", mmpt, deps=hdeps + [pbank_ev[bi]])
                    slot = grp % 2
                    ee = bld.op("v", lambda v, slot=slot, bank=bank: v.tensor_tensor(
                        ptc[:, slot * 512:(slot + 1) * 512], bank[:], cU4[:], OP.mult),
                        deps=[e, cb[f"pk{slot}"]])
                    pbank_ev[bi] = ee
                    cb[f"pk{slot}"] = ee
                    pt_evac[grp] = ee

                def emit_ov(qtr):
                    bi = banksel[qtr]
                    bank = ps[bi]
                    slot = qtr % 2
                    def mmov(t, tc=tc, qtr=qtr, bank=bank, slot=slot):
                        mm = None
                        for hh in range(4):
                            h_ = qtr * 4 + hh
                            mm = t.matmul(bank[:, hh * 128:(hh + 1) * 128],
                                          ptc[:, slot * 512 + hh * 128: slot * 512 + (hh + 1) * 128],
                                          v_st[:, tc * 2048 + h_ * 128: tc * 2048 + (h_ + 1) * 128],
                                          start=True, stop=True)
                        return mm
                    e = bld.op("t", mmov, deps=[pt_evac[qtr]] + vdep + [pbank_ev[bi]])
                    ovi_ev[qtr] = e
                    eo = bld.op("v", lambda v, tc=tc, qtr=qtr, bank=bank: v.tensor_copy(
                        o_st[:, tc * 2048 + qtr * 512: tc * 2048 + (qtr + 1) * 512], bank[:]),
                        deps=[e, o_wr.get((tc, qtr))])
                    pbank_ev[bi] = eo
                    o_evs[qtr] = eo

                emit_pt(0)
                emit_pt(1)
                emit_ov(0)
                emit_pt(2)
                emit_ov(1)
                emit_pt(3)
                emit_ov(2)
                emit_ov(3)
                for qtr in range(4):
                    bi = banksel[qtr]
                    bank = ps[bi]
                    def mmoj(t, qtr=qtr, bank=bank, rp=rp):
                        mm = None
                        for hh in range(4):
                            h_ = qtr * 4 + hh
                            mm = t.matmul(bank[:, hh * 128:(hh + 1) * 128],
                                          ufm[h_][:], hv_bfs[rp][:, h_ * 128:(h_ + 1) * 128],
                                          start=True, stop=True)
                        return mm
                    e = bld.op("t", mmoj, deps=[tr[("u", qtr * 4 + hh)] for hh in range(4)] +
                               [hvbf_wr[rp], o_evs[qtr], pbank_ev[bi]])
                    hvbf_rd[rp] = e
                    eo = bld.op("v", lambda v, tc=tc, qtr=qtr, bank=bank: v.tensor_tensor(
                        o_st[:, tc * 2048 + qtr * 512: tc * 2048 + (qtr + 1) * 512],
                        o_st[:, tc * 2048 + qtr * 512: tc * 2048 + (qtr + 1) * 512],
                        bank[:], OP.add), deps=[e, o_evs[qtr]])
                    pbank_ev[bi] = eo
                    o_wr[(tc, qtr)] = eo
                odep = [o_wr[(tc, qtr)] for qtr in range(4)]
                p1a = bld.op("v", lambda v, tc=tc: v.tensor_tensor(
                    o2[:], o_st[:, tc * 2048: tc * 2048 + 1024],
                    o_st[:, tc * 2048: tc * 2048 + 1024], OP.mult), deps=odep[0:2] + [cb["u"]])
                p2a = bld.op("v", lambda v, tc=tc: v.tensor_reduce(
                    rr_st[:, tc * 16: tc * 16 + 8], o2.rearrange("p (h d) -> p h d", d=128),
                    AX.X, OP.add), deps=[p1a, rr_war[0]])
                p1b = bld.op("v", lambda v, tc=tc: v.tensor_tensor(
                    o2[:], o_st[:, tc * 2048 + 1024: tc * 2048 + 2048],
                    o_st[:, tc * 2048 + 1024: tc * 2048 + 2048], OP.mult), deps=odep[2:4] + [p2a])
                cb["u"] = p1b
                p2b = bld.op("v", lambda v, tc=tc: v.tensor_reduce(
                    rr_st[:, tc * 16 + 8: tc * 16 + 16], o2.rearrange("p (h d) -> p h d", d=128),
                    AX.X, OP.add), deps=[p1b])
                p3 = bld.op("v", lambda v, tc=tc: v.tensor_scalar(
                    rr_st[:, tc * 16:(tc + 1) * 16], rr_st[:, tc * 16:(tc + 1) * 16],
                    1.0 / 128, 1e-5, OP.mult, OP.add), deps=[p2a, p2b])
                p4 = bld.op("s", lambda s, tc=tc: s.activation(
                    rr_st[:, tc * 16:(tc + 1) * 16], rr_st[:, tc * 16:(tc + 1) * 16], AF.Sqrt),
                    deps=[p3])
                p5 = bld.op("v", lambda v, tc=tc: v.reciprocal(
                    rr_st[:, tc * 16:(tc + 1) * 16], rr_st[:, tc * 16:(tc + 1) * 16]), deps=[p4])
                rr_war[0] = p5

            def evac_g(tc, ng, bank, evm):
                e1 = bld.op("s", lambda s, bank=bank: s.activation(sgt, bank[:], AF.Silu),
                            deps=[evm, stmp_ev[0]])
                e2 = bld.op("v", lambda v, ng=ng: v.tensor_tensor(
                    sggt, sgt, cGnw[:, ng * 512:(ng + 1) * 512], OP.mult),
                    deps=[e1, ev_g3, cb["pk1"]])
                stmp_ev[0] = e2
                cb["pk1"] = e2
                e3 = bld.op("v", lambda v, tc=tc, ng=ng: v.tensor_tensor(
                    ogt, o_st[:, tc * 2048 + ng * 512: tc * 2048 + ng * 512 + 512], sggt, OP.mult),
                    deps=[e2, o_wr[(tc, ng)], cb["ogt"]])
                cb["sggt"] = e3
                e4 = bld.op("v", lambda v, tc=tc, ng=ng: v.tensor_tensor(
                    ogt.rearrange("p (h d) -> p h d", d=128),
                    ogt.rearrange("p (h d) -> p h d", d=128),
                    rr_st[:, tc * 16 + ng * 4: tc * 16 + (ng + 1) * 4]
                    .rearrange("p h -> p h ()").broadcast_to([128, 4, 128]), OP.mult),
                    deps=[e3, rr_war[0]])
                last = e4
                for jj in range(4):
                    j = ng * 4 + jj
                    tb = jj % 2
                    def trf(t, jj=jj, tb=tb):
                        return t.transpose(psT[tb][:, 512 * (jj // 2): 512 * (jj // 2) + 128],
                                           ogt[:, jj * 128:(jj + 1) * 128], cEye[:])
                    evt = bld.op("t", trf, deps=[e4, ptbank_ev.get(tb)])
                    eng = "v" if jj % 2 == 0 else "s"
                    def evf(e, j=j, jj=jj, tb=tb, tc=tc):
                        srcv = psT[tb][:, 512 * (jj // 2): 512 * (jj // 2) + 128]
                        dstv = og_fm[:, j * 512 + tc * 128: j * 512 + tc * 128 + 128]
                        return e.tensor_copy(dstv, srcv) if jj % 2 == 0 else e.copy(dstv, srcv)
                    eve = bld.op(eng, evf, deps=[evt, og_wr.get((tc, j))])
                    og_wr[(tc, j)] = eve
                    ptbank_ev[tb] = eve
                    last = eve
                cb["ogt"] = last
                return last
            proj_tm(xnb, xnb_ev, wg_b, 2048, wbufs, wbuf_ev, evac_g)

            ogdep = [og_wr[(tc, j)] for tc in range(4) for j in range(16)]
            for ng in range(4):
                sl = ng % 2
                rr_ = wo_b[:, ng * 512:(ng + 1) * 512].rearrange("(dc p) n -> p dc n", p=128)
                weva = bld.dma("d", wbufs[sl][0].rearrange("p (dc n) -> p dc n", n=512),
                               rr_[:, 0:8, :], deps=[wbuf_ev[sl]])
                wevb = bld.dma("d", wbufs[sl][1].rearrange("p (dc n) -> p dc n", n=512),
                               rr_[:, 8:16, :], deps=[wbuf_ev[sl]])
                for tc in range(4):
                    bi = 4 + (ng * 4 + tc) % 2
                    bank = ps[bi]
                    def mmwo(t, tc=tc, sl=sl, bank=bank):
                        mm = None
                        for dc in range(16):
                            wv_ = wbufs[sl][dc // 8][:, (dc % 8) * 512:(dc % 8 + 1) * 512]
                            mm = t.matmul(bank[:],
                                          og_fm[:, dc * 512 + tc * 128: dc * 512 + tc * 128 + 128],
                                          wv_, start=(dc == 0), stop=(dc == 15))
                        return mm
                    evm = bld.op("t", mmwo, deps=[weva, wevb, pbank_ev[bi]] + ogdep)
                    asl = tc % 2
                    eve = bld.op("v", lambda v, bank=bank, asl=asl: v.tensor_copy(attb[asl], bank[:]),
                                 deps=[evm, attb_ev[asl]])
                    pbank_ev[bi] = eve
                    evd = bld.dma("g", att_scr[st * 512 + tc * 128: st * 512 + (tc + 1) * 128,
                                              ng * 512:(ng + 1) * 512], attb[asl], deps=[eve])
                    attb_ev[asl] = evd
                    stmp_ev[asl] = evd
                wbuf_ev[sl] = evm

        for st in range(st_own):
            emit_main_st(st)
        bld.barrier()
        ar.reset()

        wgb2 = [ar.abf(128, 16 * 1024) for _ in range(2)]
        wub2 = [ar.abf(128, 16 * 1024) for _ in range(2)]
        xnb2 = [ar.abf(128, 16 * 512) for _ in range(2)]
        hbuf = [ar.abf(128, 512) for _ in range(2)]
        sgb = [ar.af32(128, 512) for _ in range(2)]
        hsq = [ar.abf(128, 512) for _ in range(2)]
        rtmp = ar.af32(1, 512)
        ones_bf = ar.abf(128, 1)
        ev_ob = bld.op("v", lambda v: v.memset(ones_bf, 1.0))
        pbank_ev = {i: None for i in range(6)}
        pbank_ev["rms"] = None
        pbank_ev["rms2"] = None
        hbuf_ev = [None, None]
        sgb_ev = [None, None]
        hsq_ev = [None, None]
        rtmp_ev = [None]
        rmsacc_ev = [None]
        pend_rms = [None]
        xnb2_ev = [None, None]
        xnb2_dma = [None, None]
        wslab_ev = [None, None]
        wslab_dma = [None, None]
        def load_whalf(hs, slot, deps):
            evs = []
            for dc in range(16):
                evs.append(bld.op("d", (lambda out, in_: (lambda eng: eng.dma_start(
                    out=out, in_=in_, transpose=True)))(
                    wgb2[slot][:, dc * 1024:(dc + 1) * 1024],
                    wgu_q[hs * 1024:(hs + 1) * 1024, dc * 128:(dc + 1) * 128]),
                    deps=deps, dma=True))
                evs.append(bld.op("d", (lambda out, in_: (lambda eng: eng.dma_start(
                    out=out, in_=in_, transpose=True)))(
                    wub2[slot][:, dc * 1024:(dc + 1) * 1024],
                    wgu_q[HF + hs * 1024: HF + (hs + 1) * 1024, dc * 128:(dc + 1) * 128]),
                    deps=deps, dma=True))
            return evs[-1]

        def load_xnb2(st, slot, deps):
            evs = []
            for dc in range(16):
                evs.append(bld.dma("d", xnb2[slot][:, dc * 512:(dc + 1) * 512],
                                   xn_own[dc * 128:(dc + 1) * 128, st * 512:(st + 1) * 512],
                                   deps=deps))
            return evs[-1]

        wslab_dma[0] = load_whalf(0, 0, [None])
        for hs in range(8):
            wsl = hs % 2
            wev = wslab_dma[wsl]
            if hs + 1 < 8:
                wslab_dma[(hs + 1) % 2] = load_whalf(hs + 1, (hs + 1) % 2, [wslab_ev[(hs + 1) % 2]])
            for st in range(st_own):
                xsl = st % 2
                if hs == 0 and st == 0:
                    xnb2_dma[0] = load_xnb2(0, 0, [None])
                xnb_ev = xnb2_dma[xsl]
                if st + 1 < st_own:
                    xnb2_dma[(st + 1) % 2] = load_xnb2(st + 1, (st + 1) % 2,
                                                       [xnb2_ev[(st + 1) % 2]])
                elif hs + 1 < 8:
                    xnb2_dma[(st_own) % 2] = load_xnb2(0, (st_own) % 2, [xnb2_ev[(st_own) % 2]])
                last_eu = [None]
                for hc in range(8):
                    gb = ps[(hc % 3) * 2]
                    ub = ps[(hc % 3) * 2 + 1]
                    def mmg(t, hc=hc, gb=gb, wsl=wsl, xsl=xsl):
                        mm = None
                        for dc in range(16):
                            mm = t.matmul(gb[:],
                                          wgb2[wsl][:, dc * 1024 + hc * 128: dc * 1024 + hc * 128 + 128],
                                          xnb2[xsl][:, dc * 512:(dc + 1) * 512],
                                          start=(dc == 0), stop=(dc == 15))
                        return mm
                    def mmu(t, hc=hc, ub=ub, wsl=wsl, xsl=xsl):
                        mm = None
                        for dc in range(16):
                            mm = t.matmul(ub[:],
                                          wub2[wsl][:, dc * 1024 + hc * 128: dc * 1024 + hc * 128 + 128],
                                          xnb2[xsl][:, dc * 512:(dc + 1) * 512],
                                          start=(dc == 0), stop=(dc == 15))
                        return mm
                    eg_ = bld.op("t", mmg, deps=[wev, xnb_ev, pbank_ev[(hc % 3) * 2]])
                    eu_ = bld.op("t", mmu, deps=[wev, xnb_ev, pbank_ev[(hc % 3) * 2 + 1]])
                    last_eu[0] = eu_
                    if pend_rms[0] is not None:
                        fn_, pend_rms[0] = pend_rms[0], None
                        fn_()
                    sl = hc % 2
                    e1 = bld.op("s", lambda s, sl=sl, gb=gb: s.activation(sgb[sl], gb[:], AF.Silu),
                                deps=[eg_, sgb_ev[sl]])
                    pbank_ev[(hc % 3) * 2] = e1
                    e2 = bld.op("v", lambda v, sl=sl, ub=ub: v.tensor_tensor(
                        hbuf[sl], sgb[sl], ub[:], OP.mult), deps=[e1, eu_, hbuf_ev[sl]])
                    sgb_ev[sl] = e2
                    pbank_ev[(hc % 3) * 2 + 1] = e2
                    e3 = bld.op("s", lambda s, sl=sl: s.activation(hsq[sl], hbuf[sl], AF.Square),
                                deps=[e2, hsq_ev[sl]])

                    def emit_e4(hc=hc, sl=sl, e3=e3, st=st):
                        e4 = bld.op("t", lambda t, sl=sl, hc=hc: t.matmul(
                            psT[0][:].bitcast(F32)[0:1, :], ones_bf, hsq[sl],
                            start=(hc == 0), stop=(hc == 7), skip_group_check=True),
                            deps=[e3, ev_ob] + ([pbank_ev["rms"]] if hc == 0 else []))
                        hsq_ev[sl] = e4
                        if hc == 7:
                            r1 = bld.op("v", lambda v: v.tensor_copy(
                                rtmp, psT[0][:].bitcast(F32)[0:1, :]),
                                deps=[e4, rtmp_ev[0]])
                            pbank_ev["rms"] = r1
                            r2s = []
                            for q4 in range(4):
                                e = bld.op("t", lambda t, q4=q4: t.matmul(
                                    psT[1][:].bitcast(F32)[:, q4:q4 + 1],
                                    rtmp[0:1, q4 * 128:(q4 + 1) * 128],
                                    cOne11[0:1, :], start=True, stop=True,
                                    skip_group_check=True),
                                    deps=[r1] + ([pbank_ev["rms2"]] if q4 == 0 else []))
                                r2s.append(e)
                            rtmp_ev[0] = r2s[-1]
                            r3 = bld.op("v", lambda v, st=st: v.tensor_tensor(
                                rmsacc[:, st * 4:(st + 1) * 4], rmsacc[:, st * 4:(st + 1) * 4],
                                psT[1][:].bitcast(F32)[:, 0:4], OP.add), deps=r2s + [rmsacc_ev[0]])
                            rmsacc_ev[0] = r3
                            pbank_ev["rms2"] = r3
                    pend_rms[0] = emit_e4
                    evd = bld.dma("g", h_scr[hs * 1024 + hc * 128: hs * 1024 + (hc + 1) * 128,
                                             st * 512:(st + 1) * 512], hbuf[sl], deps=[e2])
                    hbuf_ev[sl] = evd
                xnb2_ev[xsl] = last_eu[0]
            wslab_ev[wsl] = last_eu[0]
        if pend_rms[0] is not None:
            fn_, pend_rms[0] = pend_rms[0], None
            fn_()
        bld.barrier()
        ar.reset()

        EPS32 = float(np.finfo(np.float32).eps)
        q1_ = bld.op("v", lambda v: v.tensor_scalar(rstdt[:], rmsacc[:], 1.0 / HF, EPS32,
                                                    OP.mult, OP.add))
        q2_ = bld.op("s", lambda s: s.activation(rstdt[:], rstdt[:], AF.Sqrt), deps=[q1_])
        q3_ = bld.op("v", lambda v: v.reciprocal(rstdt[:], rstdt[:]), deps=[q2_])

        wpb = ar.abf(128, 64 * 1024)
        hstr = [ar.abf(128, 512) for _ in range(16)]
        xb2 = [ar.af32(128, 1024) for _ in range(2)]
        ab2 = [ar.af32(128, 1024) for _ in range(2)]
        ob2 = [ar.af32(128, 1024) for _ in range(2)]
        p4banks = [p[:] for p in ps] + [psT[0][:].bitcast(F32), psT[1][:].bitcast(F32)]
        hstr_ev = [None] * 16
        xb2_ev = [None, None]
        ab2_ev = [None, None]
        ob2_ev = [None, None]
        wpb_ev = [None]
        bank_ev = [None] * 8
        for dh in range(2):
            wevs = []
            for hc in range(64):
                wevs.append(bld.op("d", (lambda out, in_: (lambda eng: eng.dma_start(
                    out=out, in_=in_, transpose=True)))(
                    wpb[:, hc * 1024:(hc + 1) * 1024],
                    wp_q[dh * 1024:(dh + 1) * 1024, hc * 128:(hc + 1) * 128]),
                    deps=[wpb_ev[0]], dma=True))
            wev = wevs[-1]
            for st in range(st_own):
                mm_last = None
                for wave in range(8):
                    wp_ = wave % 2
                    hevs = []
                    for sl in range(8):
                        hc = wave * 8 + sl
                        hevs.append(bld.dma("d", hstr[wp_ * 8 + sl],
                                            h_scr[hc * 128:(hc + 1) * 128, st * 512:(st + 1) * 512],
                                            deps=[hstr_ev[wp_ * 8 + sl]]))
                    for tc in range(4):
                        def mmp(t, tc=tc, wave=wave, wp_=wp_):
                            mm = None
                            for sl in range(8):
                                hc = wave * 8 + sl
                                for nh in range(2):
                                    bank = p4banks[tc * 2 + nh]
                                    mm = t.matmul(bank,
                                                  hstr[wp_ * 8 + sl][:, tc * 128:(tc + 1) * 128],
                                                  wpb[:, hc * 1024 + nh * 512: hc * 1024 + nh * 512 + 512],
                                                  start=(hc == 0), stop=(hc == 63))
                            return mm
                        mm_last = bld.op("t", mmp, deps=hevs + [wevs[wave * 8 + 7]] +
                                         ([bank_ev[tc * 2], bank_ev[tc * 2 + 1]] if wave == 0 else []))
                    for sl in range(8):
                        hstr_ev[wp_ * 8 + sl] = mm_last
                for tc in range(4):
                    row0 = st * 512 + tc * 128
                    sl2 = tc % 2
                    evx = bld.dma("d", xb2[sl2], x_own[row0:row0 + 128, dh * 1024:(dh + 1) * 1024],
                                  deps=[xb2_ev[sl2]])
                    eva = bld.dma("d", ab2[sl2], att_scr[row0:row0 + 128, dh * 1024:(dh + 1) * 1024],
                                  deps=[ab2_ev[sl2]])
                    b0 = p4banks[tc * 2]
                    b1 = p4banks[tc * 2 + 1]
                    e1 = bld.op("v", lambda v, sl2=sl2, b0=b0, st=st, tc=tc: v.scalar_tensor_tensor(
                        ob2[sl2][:, 0:512], b0,
                        rstdt[:, st * 4 + tc: st * 4 + tc + 1],
                        ab2[sl2][:, 0:512], OP.mult, OP.add),
                        deps=[mm_last, eva, q3_, ob2_ev[sl2]])
                    e2 = bld.op("v", lambda v, sl2=sl2, b1=b1, st=st, tc=tc: v.scalar_tensor_tensor(
                        ob2[sl2][:, 512:1024], b1,
                        rstdt[:, st * 4 + tc: st * 4 + tc + 1],
                        ab2[sl2][:, 512:1024], OP.mult, OP.add),
                        deps=[mm_last, eva, e1])
                    e3 = bld.op("v", lambda v, sl2=sl2: v.tensor_tensor(
                        ob2[sl2], ob2[sl2], xb2[sl2], OP.add), deps=[e1, e2, evx])
                    xb2_ev[sl2] = e3
                    ab2_ev[sl2] = e3
                    bank_ev[tc * 2] = e3
                    bank_ev[tc * 2 + 1] = e3
                    evd = bld.dma("g", out[row0:row0 + 128, dh * 1024:(dh + 1) * 1024],
                                  ob2[sl2], deps=[e3])
                    ob2_ev[sl2] = evd
            wpb_ev[0] = mm_last
        bld.waitop("v", [ob2_ev[0], ob2_ev[1], ("g", bld.cnt["g"]), ("d", bld.cnt["d"])])
        bld.op("v", lambda v: v.engine_nop())

        block = stack.enter_context(nc.Block())

        def mkbody(qe):
            def body(eng):
                for waits, fn, inc in bld.q[qe]:
                    for (se, v) in waits:
                        eng.wait_ge(sems[se], v)
                    if fn is not None:
                        instr = fn(eng)
                        instr.then_inc(sems[qe], inc)
            return body

        block.sync(mkbody("d"))
        block.tensor(mkbody("t"))
        block.vector(mkbody("v"))
        block.scalar(mkbody("s"))
        block.gpsimd(mkbody("g"))

    return nc


def _consts():
    u = np.triu(np.ones((128, 128), np.float32)).astype(BF)
    eye = np.eye(128, dtype=BF)
    return u, eye


def kernel(**inputs):
    ins = {k: np.asarray(v) for k, v in inputs.items()}
    x = np.ascontiguousarray(ins["L0_attn_hidden_states"].reshape(32768, 2048))
    u, eye = _consts()
    nc = build(TOK=4096, PRE=4096)
    in_maps = []
    for c in range(8):
        b, half = c // 2, c % 2
        in_maps.append({
            "x_own": np.ascontiguousarray(x[c * 4096:(c + 1) * 4096]),
            "x_pre": np.ascontiguousarray(x[b * 8192: b * 8192 + 4096]),
            "pbias": np.full((128, 1), -32.0 if half == 1 else -100000.0, np.float32),
            "wq": ins["L0_attn_Wq"], "wk": ins["L0_attn_Wk"], "wv": ins["L0_attn_Wv"],
            "wg": ins["L0_attn_Wg"], "ws": ins["L0_attn_Ws"], "wo": ins["L0_attn_Wo"],
            "wgu": ins["L0_ffn_w_gate_up"], "wp": ins["L0_ffn_w_proj"],
            "gnw": np.ascontiguousarray(ins["L0_attn_g_norm_weight"].reshape(1, 128)),
            "umask": u, "eye": eye,
        })
    res = run_bass_kernel_spmd(nc, in_maps, core_ids=list(range(8)))
    out = np.concatenate([np.asarray(r["out"]) for r in res.results], axis=0)
    return out.reshape(4, 8192, 2048)


# version 22 (final)
# speedup vs baseline: 1.1349x; 1.1045x over previous
import numpy as np
import ml_dtypes
from contextlib import ExitStack
import concourse.bass as bass
import concourse.mybir as mybir
from concourse.bass_utils import run_bass_kernel_spmd

BF = ml_dtypes.bfloat16
F32 = mybir.dt.float32
BF16 = mybir.dt.bfloat16
AF = mybir.ActivationFunctionType
OP = mybir.AluOpType
AX = mybir.AxisListType

D = 2048
H, DK, DV, M = 16, 64, 128, 64
HF = 8192
RND = 12582912.0
SCALE = 0.125


class Bld:
    ENGS = ("t", "v", "s", "g", "d")

    def __init__(self):
        self.q = {e: [] for e in self.ENGS}
        self.cnt = {e: 0 for e in self.ENGS}
        self.lastwait = {e: {} for e in self.ENGS}
        self.pend = {e: [] for e in self.ENGS}

    def _need(self, e, deps):
        need = {}
        for d_ in deps:
            if d_ is None:
                continue
            se, v = d_
            if se == e and v >= self.cnt[e]:
                v = self.cnt[e]
            if v > self.lastwait[e].get(se, 0) and v > need.get(se, 0):
                need[se] = v
        for se, v in need.items():
            self.lastwait[e][se] = v
        return sorted(need.items())

    def op(self, e, fn, deps=(), dma=False):
        deps = list(deps) + self.pend[e]
        self.pend[e] = []
        waits = self._need(e, deps)
        inc = 16 if dma else 1
        self.cnt[e] += inc
        self.q[e].append((waits, fn, inc))
        return (e, self.cnt[e])

    def dma(self, qe, out, in_, deps=()):
        def fn(eng, out=out, in_=in_):
            return eng.dma_start(out=out, in_=in_)
        return self.op(qe, fn, deps, dma=True)

    def waitop(self, e, deps):
        deps = list(deps) + self.pend[e]
        self.pend[e] = []
        waits = self._need(e, deps)
        self.q[e].append((waits, None, 0))

    def barrier(self):
        evs = [(e, self.cnt[e]) for e in self.ENGS if self.cnt[e] > 0]
        for e in self.ENGS:
            self.pend[e] = list({(se, v) for (se, v) in (self.pend[e] + evs)})


class Arena:
    def __init__(self, bf_ap, f32_ap, nbytes):
        self.bf = bf_ap
        self.f32 = f32_ap
        self.nbytes = nbytes
        self.off = 0

    def reset(self):
        self.off = 0

    def _take(self, nbytes):
        o = self.off
        self.off = o + ((nbytes + 3) // 4) * 4
        assert self.off <= self.nbytes, f"arena overflow {self.off} > {self.nbytes}"
        return o

    def abf(self, parts, n):
        o = self._take(2 * n)
        return self.bf[0:parts, o // 2: o // 2 + n]

    def af32(self, parts, n):
        o = self._take(4 * n)
        return self.f32[0:parts, o // 4: o // 4 + n]


def build(TOK=4096, PRE=4096, dbg=()):
    nc = bass.Bass("TRN2", target_bir_lowering=False, debug=False)

    def din(name, shape, dt):
        return nc.dram_tensor(name, shape, dt, kind="ExternalInput").ap()

    def dscr(name, shape, dt):
        kind = "ExternalOutput" if name in dbg else "Internal"
        return nc.dram_tensor(name, shape, dt, kind=kind).ap()

    x_own = din("x_own", [TOK, D], F32)
    x_pre = din("x_pre", [PRE, D], F32)
    pbias = din("pbias", [128, 1], F32)
    wq = din("wq", [D, 1024], F32)
    wk = din("wk", [D, 1024], F32)
    wv = din("wv", [D, 2048], F32)
    wg = din("wg", [D, 2048], F32)
    ws = din("ws", [D, 1024], F32)
    wo = din("wo", [2048, D], F32)
    wgu = din("wgu", [2 * HF, D], F32)
    wp = din("wp", [D, HF], F32)
    gnw = din("gnw", [1, 128], F32)
    umask_in = din("umask", [128, 128], BF16)
    eye_in = din("eye", [128, 128], BF16)

    out = nc.dram_tensor("out", [TOK, D], F32, kind="ExternalOutput").ap()

    wq_b = dscr("wq_b", [D, 1024], BF16)
    wk_b = dscr("wk_b", [D, 1024], BF16)
    ws_b = dscr("ws_b", [D, 1024], BF16)
    wv_b = dscr("wv_b", [D, 2048], BF16)
    wg_b = dscr("wg_b", [D, 2048], BF16)
    wo_b = dscr("wo_b", [2048, D], BF16)
    wgu_q = dscr("wgu_q", [2 * HF, D], BF16)
    wp_q = dscr("wp_q", [D, HF], BF16)
    xn_own = dscr("xn_own", [D, TOK], BF16)
    xn_pre = dscr("xn_pre", [D, PRE], BF16)
    h_scr = dscr("h_scr", [HF, TOK], BF16)
    att_scr = dscr("att_scr", [TOK, D], F32)

    bld = Bld()
    st_own = TOK // 512
    st_pre = PRE // 512

    with ExitStack() as stack:
        sb = stack.enter_context
        cU = sb(nc.sbuf_tensor("cU", [128, 128], BF16))
        cEye = sb(nc.sbuf_tensor("cEye", [128, 128], BF16))
        cU4 = sb(nc.sbuf_tensor("cU4", [128, 512], BF16))
        cOne11 = sb(nc.sbuf_tensor("cOne11", [1, 1], F32))
        cGnw = sb(nc.sbuf_tensor("cGnw", [128, 2048], BF16))
        cBias32 = sb(nc.sbuf_tensor("cBias32", [128, 1], F32))
        cOnesColBf = sb(nc.sbuf_tensor("cOnesColBf", [128, 1], BF16))
        cRND = sb(nc.sbuf_tensor("cRND", [128, 1], F32))
        cRNDn = sb(nc.sbuf_tensor("cRNDn", [128, 1], F32))
        cPBias = sb(nc.sbuf_tensor("cPBias", [128, 1], F32))
        hk_sb = sb(nc.sbuf_tensor("hk_sb", [64, 1024], F32))[:]
        hk_bfs = [sb(nc.sbuf_tensor(f"hk_bf{i}", [64, 1024], BF16)) for i in range(2)]
        hv_big = sb(nc.sbuf_tensor("hv_sb", [128, 2048], F32))
        hv_sb = hv_big[0:64, :]
        s_prev = hv_big[64:65, 0:1024]
        cOnes1 = hv_big[64:65, 1024:1152]
        gn_row = hv_big[64:65, 1152:1280]
        hv_bfs = [sb(nc.sbuf_tensor(f"hv_bf{i}", [64, 2048], BF16)) for i in range(2)]
        rmsacc = sb(nc.sbuf_tensor("rmsacc", [128, TOK // 128], F32))
        rstdt = sb(nc.sbuf_tensor("rstdt", [128, TOK // 128], F32))
        ARN = 178 * 1024 - 128
        arena_t = sb(nc.sbuf_tensor("arena", [128, ARN // 2], BF16))
        ar = Arena(arena_t[:], arena_t[:].bitcast(F32), ARN)
        ps = [sb(nc.psum_tensor(f"ps{i}", [128, 512], F32)) for i in range(6)]
        psT = [sb(nc.psum_tensor(f"psT{i}", [128, 1024], BF16)) for i in range(2)]
        sems = {e: sb(nc.semaphore(f"sem_{e}")) for e in Bld.ENGS}

        ev_u = bld.dma("d", cU[:], umask_in)
        ev_e = bld.dma("d", cEye[:], eye_in)
        ev_gn = bld.dma("d", gn_row, gnw)
        ev_pb = bld.dma("d", cPBias[:], pbias)
        bld.op("v", lambda v: v.memset(cOnes1[:], 1.0))
        bld.op("v", lambda v: v.memset(cOne11[:], 1.0))
        bld.op("v", lambda v: v.memset(cBias32[:], -32.0))
        bld.op("v", lambda v: v.memset(cOnesColBf[:], 1.0))
        bld.op("v", lambda v: v.memset(cRND[:], RND))
        bld.op("v", lambda v: v.memset(cRNDn[:], -RND))
        bld.op("v", lambda v: v.memset(s_prev[:], 0.0))
        bld.op("v", lambda v: v.memset(hk_sb[:], 0.0))
        bld.op("v", lambda v: v.memset(hk_bfs[0][:], 0.0))
        bld.op("v", lambda v: v.memset(hk_bfs[1][:], 0.0))
        bld.op("v", lambda v: v.memset(hv_sb[:], 0.0))
        bld.op("v", lambda v: v.memset(hv_bfs[0][:], 0.0))
        bld.op("v", lambda v: v.memset(hv_bfs[1][:], 0.0))
        bld.op("v", lambda v: v.memset(rmsacc[:], 0.0))
        ev_u4 = bld.op("v", lambda v: v.tensor_copy(
            cU4[:].rearrange("p (r c) -> p r c", c=128),
            cU[:].rearrange("p c -> p () c").broadcast_to([128, 4, 128])), deps=[ev_u])
        ev_g1 = bld.op("t", lambda t: t.matmul(ps[0][:, 0:128], cOnes1[0:1, :], gn_row,
                                               start=True, stop=True), deps=[ev_gn])
        gtmp = ar.abf(128, 128)
        ev_g2 = bld.op("v", lambda v: v.tensor_copy(gtmp, ps[0][:, 0:128]), deps=[ev_g1])
        ev_g3 = bld.op("v", lambda v: v.tensor_copy(
            cGnw[:].rearrange("p (r c) -> p r c", c=128),
            gtmp.rearrange("p c -> p () c").broadcast_to([128, 16, 128])), deps=[ev_g2])
        bld.barrier()
        ar.reset()

        wf = [ar.af32(128, 1024) for _ in range(2)]
        wbb = [ar.abf(128, 1024) for _ in range(2)]
        cast_jobs = []
        for (src, dst, width) in ((wq, wq_b, 1024), (wk, wk_b, 1024), (ws, ws_b, 1024),
                                  (wv, wv_b, 2048), (wg, wg_b, 2048), (wo, wo_b, 2048)):
            for r in range(16):
                for cc in range(width // 1024):
                    cast_jobs.append((src, dst, r, cc))
        prev_read = [None, None]
        prev_st = [None, None]
        for i, (src, dst, r, cc) in enumerate(cast_jobs):
            sl = i % 2
            evl = bld.dma("d", wf[sl], src[r * 128:(r + 1) * 128, cc * 1024:(cc + 1) * 1024],
                          deps=[prev_read[sl]])
            evc = bld.op("s", lambda s, sl=sl: s.copy(wbb[sl], wf[sl]), deps=[evl, prev_st[sl]])
            prev_read[sl] = evc
            evs_ = bld.dma("g", dst[r * 128:(r + 1) * 128, cc * 1024:(cc + 1) * 1024], wbb[sl],
                           deps=[evc])
            prev_st[sl] = evs_
        bld.barrier()
        ar.reset()

        def mk_qslots(width, nsl):
            out = []
            for _ in range(nsl):
                out.append(dict(
                    wb=ar.abf(128, width), sc=ar.af32(128, width // 64),
                    scb=ar.abf(128, width // 64), scf2=ar.af32(128, width // 64),
                    rin=ar.af32(128, width // 64), q1=ar.abf(128, width),
                    wh=ar.abf(128, width)))
            qtmp = ar.af32(128, width)
            for o in out:
                o["qtmp"] = qtmp
            return out

        def quant_front(wf32, width, bufs, deps):
            ng = width // 64
            e1 = bld.op("s", lambda s: s.copy(bufs["wb"], wf32), deps=deps)
            e2 = bld.op("v", lambda v: v.tensor_reduce(
                bufs["sc"], bufs["wb"].rearrange("p (g k) -> p g k", k=64), AX.X, OP.add,
                apply_absolute_value=True), deps=[e1])
            e3 = bld.op("v", lambda v: v.tensor_scalar(bufs["scb"], bufs["sc"], 1.0 / 64, 1e-8,
                                                       OP.mult, OP.max), deps=[e2])
            e3b = bld.op("s", lambda s: s.copy(bufs["scf2"], bufs["scb"]), deps=[e3])
            e4 = bld.op("v", lambda v: v.reciprocal(bufs["rin"], bufs["scf2"]), deps=[e3b])
            return e4

        def q_stage_quo(width, bufs, front_ev):
            ng = width // 64
            return bld.op("v", lambda v: v.tensor_tensor(
                bufs["q1"].rearrange("p (g k) -> p g k", k=64),
                bufs["wb"].rearrange("p (g k) -> p g k", k=64),
                bufs["rin"].rearrange("p g -> p g ()").broadcast_to([128, ng, 64]), OP.mult),
                deps=[front_ev])

        def q_stage_round(bufs, quo_ev, qtmp_war):
            e6a = bld.op("s", lambda s: s.activation(bufs["qtmp"], bufs["q1"], AF.Identity,
                                                     bias=cRND[:], scale=1.0),
                         deps=[quo_ev, qtmp_war])
            e6b = bld.op("s", lambda s: s.activation(bufs["q1"], bufs["qtmp"], AF.Identity,
                                                     bias=cRNDn[:], scale=1.0), deps=[e6a])
            return e6b

        def q_stage_fin(width, bufs, round_ev, war_wh):
            ng = width // 64
            e7 = bld.op("v", lambda v: v.tensor_scalar(bufs["q1"], bufs["q1"], 1.0, -1.0,
                                                       OP.min, OP.max), deps=[round_ev])
            e8 = bld.op("g", lambda g: g.tensor_tensor(
                bufs["wh"].rearrange("p (g k) -> p g k", k=64),
                bufs["q1"].rearrange("p (g k) -> p g k", k=64),
                bufs["scb"].rearrange("p g -> p g ()").broadcast_to([128, ng, 64]), OP.mult),
                deps=[e7, war_wh])
            return e8

        NSL = 4
        wfq = [ar.af32(128, 2048) for _ in range(NSL)]
        qs = mk_qslots(2048, NSL)
        jobs = [("gu", i) for i in range(128)] + [("wp", i) for i in range(64)]
        NJ = len(jobs)

        def q_src(job):
            kind, idx = job
            if kind == "gu":
                return wgu[idx * 128:(idx + 1) * 128, :]
            r, cc = idx // 4, idx % 4
            return wp[r * 128:(r + 1) * 128, cc * 2048:(cc + 1) * 2048]

        def q_dst(job):
            kind, idx = job
            if kind == "gu":
                return wgu_q[idx * 128:(idx + 1) * 128, :]
            r, cc = idx // 4, idx % 4
            return wp_q[r * 128:(r + 1) * 128, cc * 2048:(cc + 1) * 2048]

        ld_ev = {}
        fr_ev = {}
        quo_ev = {}
        rnd_ev = {}
        st_ev = {}
        qtmp_last = [None]
        qn = [0]

        def prep_step():
            n = qn[0]
            if n >= NJ + 4:
                return False
            qn[0] += 1
            if n < NJ:
                sl = n % NSL
                ld_ev[n] = bld.dma("d", wfq[sl], q_src(jobs[n]),
                                   deps=[fr_ev.get(n - NSL)])
            if n >= 1 and n - 1 < NJ:
                k = n - 1
                fr_ev[k] = quant_front(wfq[k % NSL], 2048, qs[k % NSL], [ld_ev[k]])
            if n >= 2 and n - 2 < NJ:
                k = n - 2
                quo_ev[k] = q_stage_quo(2048, qs[k % NSL], fr_ev[k])
            if n >= 3 and n - 3 < NJ:
                k = n - 3
                rnd_ev[k] = q_stage_round(qs[k % NSL], quo_ev[k], qtmp_last[0])
                qtmp_last[0] = rnd_ev[k]
            if n >= 4 and n - 4 < NJ:
                k = n - 4
                e8 = q_stage_fin(2048, qs[k % NSL], rnd_ev[k], st_ev.get(k - NSL))
                st_ev[k] = bld.dma("g", q_dst(jobs[k]), qs[k % NSL]["wh"], deps=[e8])
            return True

        _lnb = dict(
            xb=[ar.af32(128, 2048) for _ in range(2)],
            sqs=ar.af32(128, 2048), sqacc=ar.af32(128, 2), mcol=ar.af32(128, 2),
            m2c=ar.af32(128, 1), vv=ar.af32(128, 1), rr=ar.af32(128, 1),
            xnt=[ar.abf(128, 2048) for _ in range(2)], xnasm=ar.abf(128, 16 * 512))

        def ln_pass(x_src, xn_dst, ntok, filler=None):
            xb = _lnb["xb"]
            sqs = _lnb["sqs"]
            sqacc = _lnb["sqacc"]
            mcol = _lnb["mcol"]
            m2c = _lnb["m2c"]
            vv = _lnb["vv"]
            rr = _lnb["rr"]
            xnt = _lnb["xnt"]
            xnasm = _lnb["xnasm"]
            prev_rd = [ln_pass.carry_rd[0], ln_pass.carry_rd[1]]
            prev_tr = [ln_pass.carry_tr[0], ln_pass.carry_tr[1]]
            asm_dma = [ln_pass.carry_asm]
            tevl = [ln_pass.carry_tev[0], ln_pass.carry_tev[1]]
            fr = {}

            def ln_front(i):
                sl = i % 2
                evl = bld.dma("d", xb[sl], x_src[i * 128:(i + 1) * 128, :], deps=[prev_rd[sl]])
                e1 = bld.op("s", lambda s, sl=sl: s.activation(sqs, xb[sl], AF.Square,
                                                               accum_out=sqacc[:, sl:sl + 1]),
                            deps=[evl, fr.get(("sq", i - 2))])
                e2 = bld.op("v", lambda v, sl=sl: v.tensor_reduce(mcol[:, sl:sl + 1], xb[sl],
                                                                  AX.X, OP.add),
                            deps=[evl, fr.get(("mc", i - 2))])
                fr[("e1", i)] = e1
                fr[("e2", i)] = e2

            def ln_back(i):
                sl = i % 2
                e1 = fr.pop(("e1", i))
                e2 = fr.pop(("e2", i))
                e3 = bld.op("v", lambda v, sl=sl: v.tensor_scalar(mcol[:, sl:sl + 1], mcol[:, sl:sl + 1],
                                                                  1.0 / 2048, None, OP.mult), deps=[e2])
                e4 = bld.op("v", lambda v, sl=sl: v.tensor_scalar(vv, sqacc[:, sl:sl + 1],
                                                                  1.0 / 2048, None, OP.mult), deps=[e1])
                e5 = bld.op("v", lambda v, sl=sl: v.tensor_tensor(m2c, mcol[:, sl:sl + 1],
                                                                  mcol[:, sl:sl + 1], OP.mult), deps=[e3])
                e6 = bld.op("v", lambda v: v.tensor_tensor(vv, vv, m2c, OP.subtract), deps=[e4, e5])
                e7 = bld.op("v", lambda v: v.tensor_scalar(vv, vv, 1e-5, None, OP.add), deps=[e6])
                e8 = bld.op("s", lambda s: s.activation(vv, vv, AF.Sqrt), deps=[e7])
                e9v = bld.op("v", lambda v: v.reciprocal(rr, vv), deps=[e8])
                e9 = bld.op("v", lambda v, sl=sl: v.tensor_scalar(
                    xnt[sl], xb[sl], mcol[:, sl:sl + 1], rr, OP.subtract, OP.mult),
                    deps=[e9v, e3, prev_tr[sl]])
                fr[("sq", i)] = e9
                fr[("mc", i)] = e9
                prev_rd[sl] = e9
                tcol = (i % 4) * 128
                evs16 = []
                for j in range(16):
                    tb = (i * 16 + j) % 2
                    evt = bld.op("t", lambda t, j=j, tb=tb, sl=sl: t.transpose(
                        psT[tb][:, (j % 8) * 128:(j % 8) * 128 + 128],
                        xnt[sl][:, j * 128:(j + 1) * 128], cEye[:]),
                        deps=[e9, tevl[tb]])
                    eng = "v" if j % 2 == 0 else "s"
                    def evac(e, j=j, tb=tb, tcol=tcol):
                        dstv = xnasm[:, j * 512 + tcol: j * 512 + tcol + 128]
                        srcv = psT[tb][:, (j % 8) * 128:(j % 8) * 128 + 128]
                        return e.tensor_copy(dstv, srcv) if j % 2 == 0 else e.copy(dstv, srcv)
                    eve = bld.op(eng, evac, deps=[evt, asm_dma[0]])
                    tevl[tb] = eve
                    evs16.append(eve)
                prev_tr[sl] = evs16[-1]
                if i % 4 == 3:
                    g0 = (i // 4) * 512
                    devs = []
                    for j in range(16):
                        devs.append(bld.dma("g", xn_dst[j * 128:(j + 1) * 128, g0:g0 + 512],
                                            xnasm[:, j * 512:(j + 1) * 512], deps=evs16))
                    asm_dma[0] = devs[-1]

            nt = ntok // 128
            for i in range(nt + 1):
                if i < nt:
                    ln_front(i)
                if i >= 1:
                    ln_back(i - 1)
                    if filler is not None:
                        for _ in range(6):
                            filler()
            ln_pass.carry_rd = prev_rd
            ln_pass.carry_tr = prev_tr
            ln_pass.carry_asm = asm_dma[0]
            ln_pass.carry_tev = tevl

        ln_pass.carry_rd = [None, None]
        ln_pass.carry_tr = [None, None]
        ln_pass.carry_asm = None
        ln_pass.carry_tev = [None, None]
        ln_pass(x_own, xn_own, TOK, prep_step)
        ln_pass(x_pre, xn_pre, PRE, prep_step)
        while prep_step():
            pass
        bld.barrier()
        ar.reset()

        pbank_ev = {i: None for i in range(6)}
        pbank_ev["psT0"] = None
        pbank_ev["psT1"] = None

        def proj_tm(xnb_ap, xnb_ev, wsrc, wwidth, wbufs, wbuf_ev, evac_fn):
            evs = []
            for ng in range(wwidth // 512):
                sl = ng % 2
                rr_ = wsrc[:, ng * 512:(ng + 1) * 512].rearrange("(dc p) n -> p dc n", p=128)
                weva = bld.dma("d", wbufs[sl][0].rearrange("p (dc n) -> p dc n", n=512),
                               rr_[:, 0:8, :], deps=[wbuf_ev[sl]])
                wevb = bld.dma("d", wbufs[sl][1].rearrange("p (dc n) -> p dc n", n=512),
                               rr_[:, 8:16, :], deps=[wbuf_ev[sl]])
                for tc in range(4):
                    bi = 4 + (ng * 4 + tc) % 2
                    bank = ps[bi]
                    def mmfn(t, tc=tc, sl=sl, bank=bank):
                        mm = None
                        for dc in range(16):
                            wv_ = wbufs[sl][dc // 8][:, (dc % 8) * 512:(dc % 8 + 1) * 512]
                            mm = t.matmul(bank[:],
                                          xnb_ap[:, dc * 512 + tc * 128: dc * 512 + tc * 128 + 128],
                                          wv_, start=(dc == 0), stop=(dc == 15))
                        return mm
                    evm = bld.op("t", mmfn, deps=[weva, wevb, xnb_ev, pbank_ev[bi]])
                    eve = evac_fn(tc, ng, bank, evm)
                    pbank_ev[bi] = eve
                    evs.append(eve)
                wbuf_ev[sl] = evs[-1]
            return evs

        def proj_fm(xnb_ap, xnb_ev, wsrc, fwidth, wbufs, wbuf_ev, evac_fn):
            evs = []
            for fh in range(fwidth // 512):
                sl = fh % 2
                rr_ = wsrc[:, fh * 512:(fh + 1) * 512].rearrange("(dc p) n -> p dc n", p=128)
                weva = bld.dma("d", wbufs[sl][0].rearrange("p (dc n) -> p dc n", n=512),
                               rr_[:, 0:8, :], deps=[wbuf_ev[sl]])
                wevb = bld.dma("d", wbufs[sl][1].rearrange("p (dc n) -> p dc n", n=512),
                               rr_[:, 8:16, :], deps=[wbuf_ev[sl]])
                for f128 in range(4):
                    bi = 4 + (fh * 4 + f128) % 2
                    bank = ps[bi]
                    def mmfn(t, f128=f128, sl=sl, bank=bank):
                        mm = None
                        for dc in range(16):
                            wv_ = wbufs[sl][dc // 8][:, (dc % 8) * 512 + f128 * 128:
                                                     (dc % 8) * 512 + f128 * 128 + 128]
                            mm = t.matmul(bank[:], wv_,
                                          xnb_ap[:, dc * 512:(dc + 1) * 512],
                                          start=(dc == 0), stop=(dc == 15))
                        return mm
                    evm = bld.op("t", mmfn, deps=[weva, wevb, xnb_ev, pbank_ev[bi]])
                    eve = evac_fn(fh, f128, bank, evm)
                    pbank_ev[bi] = eve
                    evs.append(eve)
                wbuf_ev[sl] = evs[-1]
            return evs

        ar.reset()
        xnb = ar.abf(128, 16 * 512)
        wbufs = [(ar.abf(128, 4096), ar.abf(128, 4096)) for _ in range(2)]
        a_tm = ar.abf(128, 4 * 1024)
        v_st = ar.abf(128, 4 * 2048)
        k_tm_st = ar.abf(128, 4 * 1024)
        stmp = [ar.af32(128, 512) for _ in range(2)]
        wbuf_ev = [None, None]
        xnb_last = [None]
        hk_ev = [None]
        hv_ev = [None]
        sprev_ev = [None]
        stmp_ev = [None, None]
        a_war = {tc: None for tc in range(4)}
        a_wr = {}
        v_wr = {}
        k_wr = {}

        def emit_prefix_st(st, xn_src, bias_ap, bias_dep):
            devs = []
            for dc in range(16):
                devs.append(bld.dma("d", xnb[:, dc * 512:(dc + 1) * 512],
                                    xn_src[dc * 128:(dc + 1) * 128, st * 512:(st + 1) * 512],
                                    deps=[xnb_last[0]]))
            xnb_ev = devs[-1]

            def evac_s(tc, ng, bank, evm):
                sl2 = (tc * 2 + ng) % 2
                e1 = bld.op("v", lambda v, sl2=sl2, bank=bank: v.tensor_scalar(
                    stmp[sl2], bank[:], 32.0, -32.0, OP.min, OP.max),
                    deps=[evm, stmp_ev[sl2]])
                e2 = bld.op("s", lambda s, tc=tc, ng=ng, sl2=sl2: s.activation(
                    a_tm[:, tc * 1024 + ng * 512: tc * 1024 + ng * 512 + 512],
                    stmp[sl2], AF.Exp, bias=bias_ap, scale=1.0),
                    deps=[e1, bias_dep, a_war[tc]])
                stmp_ev[sl2] = e2
                a_wr[(tc, ng)] = e2
                return e2
            proj_tm(xnb, xnb_ev, ws_b, 1024, wbufs, wbuf_ev, evac_s)

            def evac_v(tc, ng, bank, evm):
                e = bld.op("v", lambda v, tc=tc, ng=ng, bank=bank: v.tensor_copy(
                    v_st[:, tc * 2048 + ng * 512: tc * 2048 + ng * 512 + 512], bank[:]),
                    deps=[evm, v_wr.get((tc, ng))])
                v_wr[(tc, ng)] = e
                return e
            proj_tm(xnb, xnb_ev, wv_b, 2048, wbufs, wbuf_ev, evac_v)

            def evac_k(tc, ng, bank, evm):
                e = bld.op("v", lambda v, tc=tc, ng=ng, bank=bank: v.tensor_copy(
                    k_tm_st[:, tc * 1024 + ng * 512: tc * 1024 + ng * 512 + 512], bank[:]),
                    deps=[evm, k_wr.get((tc, ng))])
                k_wr[(tc, ng)] = e
                return e
            evk = proj_tm(xnb, xnb_ev, wk_b, 1024, wbufs, wbuf_ev, evac_k)
            xnb_last[0] = evk[-1]

            sa_mm = [None, None]
            hk_mm = {}
            hv_mm = {}
            for tc in range(4):
                adep = [a_wr[(tc, 0)], a_wr[(tc, 1)]]
                vdep = [v_wr[(tc, ng)] for ng in range(4)]
                kdep = [k_wr[(tc, 0)], k_wr[(tc, 1)]]
                for half in range(2):
                    e = bld.op("t", lambda t, tc=tc, half=half: t.matmul(
                        psT[half][:].bitcast(F32)[0:1, 0:512], cOnesColBf[:],
                        a_tm[:, tc * 1024 + half * 512: tc * 1024 + (half + 1) * 512],
                        start=(tc == 0), stop=(tc == 3)),
                        deps=adep + ([pbank_ev[f"psT{half}"]] if tc == 0 else []))
                    sa_mm[half] = e
                for half in range(2):
                    bank = ps[4 + half]
                    def mmhk(t, tc=tc, half=half, bank=bank):
                        mm = None
                        for hh in range(8):
                            h_ = half * 8 + hh
                            mm = t.matmul(bank[0:64, hh * 64:(hh + 1) * 64],
                                          k_tm_st[:, tc * 1024 + h_ * 64: tc * 1024 + (h_ + 1) * 64],
                                          a_tm[:, tc * 1024 + h_ * 64: tc * 1024 + (h_ + 1) * 64],
                                          start=(tc == 0 and hh == 0), stop=(tc == 3 and hh == 7),
                                          skip_group_check=True)
                        return mm
                    hk_mm[half] = bld.op("t", mmhk, deps=adep + kdep +
                                         ([pbank_ev[4 + half]] if tc == 0 else []))
                for qtr in range(4):
                    bank = ps[qtr]
                    def mmhv(t, tc=tc, qtr=qtr, bank=bank):
                        mm = None
                        for hh in range(4):
                            h_ = qtr * 4 + hh
                            mm = t.matmul(bank[0:64, hh * 128:(hh + 1) * 128],
                                          a_tm[:, tc * 1024 + h_ * 64: tc * 1024 + (h_ + 1) * 64],
                                          v_st[:, tc * 2048 + h_ * 128: tc * 2048 + (h_ + 1) * 128],
                                          start=(tc == 0 and hh == 0), stop=(tc == 3 and hh == 3),
                                          skip_group_check=True)
                        return mm
                    hv_mm[qtr] = bld.op("t", mmhv, deps=adep + vdep +
                                        ([pbank_ev[qtr]] if tc == 0 else []))
                a_war[tc] = None
            for half in range(2):
                e = bld.op("v", lambda v, half=half: v.tensor_tensor(
                    s_prev[0:1, half * 512:(half + 1) * 512],
                    s_prev[0:1, half * 512:(half + 1) * 512],
                    psT[half][:].bitcast(F32)[0:1, 0:512], OP.add),
                    deps=[sa_mm[half], sprev_ev[0]])
                pbank_ev[f"psT{half}"] = e
                sprev_ev[0] = e
            for half in range(2):
                e1 = bld.op("v", lambda v, half=half: v.tensor_tensor(
                    hk_sb[:, half * 512:(half + 1) * 512], hk_sb[:, half * 512:(half + 1) * 512],
                    ps[4 + half][0:64, :], OP.add), deps=[hk_mm[half], hk_ev[0]])
                pbank_ev[4 + half] = e1
                hk_ev[0] = e1
            for qtr in range(4):
                e1 = bld.op("v", lambda v, qtr=qtr: v.tensor_tensor(
                    hv_sb[:, qtr * 512:(qtr + 1) * 512], hv_sb[:, qtr * 512:(qtr + 1) * 512],
                    ps[qtr][0:64, :], OP.add), deps=[hv_mm[qtr], hv_ev[0]])
                pbank_ev[qtr] = e1
                hv_ev[0] = e1

        for st in range(st_pre):
            emit_prefix_st(st, xn_pre, cPBias[:], ev_pb)
        hkbf_wr = {0: bld.op("v", lambda v: v.tensor_copy(hk_bfs[0][:], hk_sb[:]),
                             deps=[hk_ev[0]]), 1: None}
        hvbf_wr = {0: bld.op("v", lambda v: v.tensor_copy(hv_bfs[0][:], hv_sb[:]),
                             deps=[hv_ev[0]]), 1: None}
        hkbf_rd = {0: None, 1: None}
        hvbf_rd = {0: None, 1: None}
        bld.barrier()
        ar.reset()

        xnb = ar.abf(128, 16 * 512)
        wbufs = [(ar.abf(128, 4096), ar.abf(128, 4096)) for _ in range(2)]
        a_tm = ar.abf(128, 4 * 1024)
        v_st = ar.abf(128, 4 * 2048)
        o_st = ar.abf(128, 4 * 2048)
        og_fm = ar.abf(128, 16 * 512)
        qh = [ar.abf(64, 512) for _ in range(16)]
        kh = [ar.abf(64, 512) for _ in range(16)]
        stmp = [ar.af32(128, 512) for _ in range(2)]
        srep_sb = ar.af32(128, 1024)
        esb = srep_sb
        sfull = ar.af32(128, 1024)
        sinv = ar.af32(128, 1024)
        kqc = ar.abf(128, 1024)
        okt = ar.af32(128, 1024)
        u_tm = ar.abf(128, 1024)
        ptc = kqc
        o2 = u_tm
        mx = ar.af32(128, 16)
        sm = ar.af32(128, 16)
        smr = ar.af32(128, 16)
        rr_st = ar.af32(128, 64)
        ogt = ar.abf(128, 512)
        sgt = stmp[0]
        sggt = kqc[:, 512:1024]
        afm = [ar.abf(64, 128) for _ in range(16)]
        ufm = [ar.abf(64, 128) for _ in range(16)]
        ktm = [ar.abf(128, 64) for _ in range(16)]
        attb = stmp

        ptbank_ev = {}
        wbuf_ev = [None, None]
        xnb_last = [None]
        stmp_ev = [None, None]
        a_war = {tc: None for tc in range(4)}
        a_wr, v_wr, q_wr, k_wr = {}, {}, {}, {}
        og_wr = {}
        attb_ev = [None, None]
        cb = {"pk0": None, "pk1": None, "okt": None, "u": None,
              "sfull": None, "sinv": None, "srep": None, "mx": None, "sm": None, "smr": None,
              "ogt": None}
        afm_ev = [None] * 16
        ufm_ev = [None] * 16
        ktm_ev = [None] * 16
        rr_war = [None]
        o_wr = {}

        def emit_main_st(st):
            devs = []
            for dc in range(16):
                devs.append(bld.dma("d", xnb[:, dc * 512:(dc + 1) * 512],
                                    xn_own[dc * 128:(dc + 1) * 128, st * 512:(st + 1) * 512],
                                    deps=[xnb_last[0]]))
            xnb_ev = devs[-1]

            def evac_q(fh, f128, bank, evm):
                h0 = fh * 8 + f128 * 2
                bld.op("v", lambda v, h0=h0, bank=bank: v.tensor_copy(qh[h0][:], bank[0:64, :]),
                       deps=[evm, q_wr.get(h0)])
                e2 = bld.op("v", lambda v, h0=h0, bank=bank: v.tensor_copy(qh[h0 + 1][:], bank[64:128, :]),
                            deps=[evm, q_wr.get(h0 + 1)])
                q_wr[h0] = e2
                q_wr[h0 + 1] = e2
                return e2
            proj_fm(xnb, xnb_ev, wq_b, 1024, wbufs, wbuf_ev, evac_q)

            def evac_k2(fh, f128, bank, evm):
                h0 = fh * 8 + f128 * 2
                bld.op("v", lambda v, h0=h0, bank=bank: v.tensor_copy(kh[h0][:], bank[0:64, :]),
                       deps=[evm, k_wr.get(h0)])
                e2 = bld.op("v", lambda v, h0=h0, bank=bank: v.tensor_copy(kh[h0 + 1][:], bank[64:128, :]),
                            deps=[evm, k_wr.get(h0 + 1)])
                k_wr[h0] = e2
                k_wr[h0 + 1] = e2
                return e2
            proj_fm(xnb, xnb_ev, wk_b, 1024, wbufs, wbuf_ev, evac_k2)

            def evac_s(tc, ng, bank, evm):
                sl2 = (tc * 2 + ng) % 2
                e1 = bld.op("v", lambda v, sl2=sl2, bank=bank: v.tensor_scalar(
                    stmp[sl2], bank[:], 32.0, -32.0, OP.min, OP.max),
                    deps=[evm, stmp_ev[sl2]])
                e2 = bld.op("s", lambda s, tc=tc, ng=ng, sl2=sl2: s.activation(
                    a_tm[:, tc * 1024 + ng * 512: tc * 1024 + ng * 512 + 512],
                    stmp[sl2], AF.Exp, bias=cBias32[:], scale=1.0),
                    deps=[e1, a_war[tc]])
                stmp_ev[sl2] = e2
                a_wr[(tc, ng)] = e2
                return e2
            proj_tm(xnb, xnb_ev, ws_b, 1024, wbufs, wbuf_ev, evac_s)

            def evac_v(tc, ng, bank, evm):
                e = bld.op("v", lambda v, tc=tc, ng=ng, bank=bank: v.tensor_copy(
                    v_st[:, tc * 2048 + ng * 512: tc * 2048 + ng * 512 + 512], bank[:]),
                    deps=[evm, v_wr.get((tc, ng))])
                v_wr[(tc, ng)] = e
                return e
            evg = proj_tm(xnb, xnb_ev, wv_b, 2048, wbufs, wbuf_ev, evac_v)
            xnb_last[0] = evg[-1]

            for tc in range(4):
                cidx = st * 4 + tc
                rp = cidx % 2
                wpar = (cidx + 1) % 2
                banksel = [0, 1, 4, 5]
                adep = [a_wr[(tc, 0)], a_wr[(tc, 1)]]
                vdep = [v_wr[(tc, ng)] for ng in range(4)]
                evS = []
                for half in range(2):
                    e = bld.op("t", lambda t, tc=tc, half=half: t.matmul(
                        ps[half][:], cU[:],
                        a_tm[:, tc * 1024 + half * 512: tc * 1024 + (half + 1) * 512],
                        start=True, stop=True),
                        deps=adep + [ev_u, pbank_ev[half]])
                    evS.append(e)
                evR = []
                for half in range(2):
                    e = bld.op("t", lambda t, half=half: t.matmul(
                        ps[2 + half][:], cOnes1[0:1, :], s_prev[0:1, half * 512:(half + 1) * 512],
                        start=True, stop=True),
                        deps=[sprev_ev[0], pbank_ev[2 + half]])
                    evR.append(e)
                sA = []
                for half in range(2):
                    e = bld.op("t", lambda t, tc=tc, half=half: t.matmul(
                        ps[4 + half][0:1, 0:512], cOnesColBf[:],
                        a_tm[:, tc * 1024 + half * 512: tc * 1024 + (half + 1) * 512],
                        start=True, stop=True),
                        deps=adep + [pbank_ev[4 + half]])
                    sA.append(e)
                for half in range(2):
                    e = bld.op("v", lambda v, half=half: v.tensor_tensor(
                        s_prev[0:1, half * 512:(half + 1) * 512],
                        s_prev[0:1, half * 512:(half + 1) * 512],
                        ps[4 + half][0:1, 0:512], OP.add),
                        deps=[sA[half], evR[0], evR[1], sprev_ev[0]])
                    pbank_ev[4 + half] = e
                    sprev_ev[0] = e
                a_war[tc] = None
                e1 = bld.op("v", lambda v: v.tensor_copy(srep_sb[:, 0:512], ps[2][:]),
                            deps=[evR[0], cb["srep"]])
                e2 = bld.op("v", lambda v: v.tensor_copy(srep_sb[:, 512:1024], ps[3][:]),
                            deps=[evR[1]])
                cb["srep"] = e2
                pbank_ev[2] = e1
                pbank_ev[3] = e2
                f1 = bld.op("v", lambda v: v.tensor_tensor(sfull[:, 0:512], ps[0][:],
                                                           srep_sb[:, 0:512], OP.add),
                            deps=[evS[0], e1, cb["sfull"]])
                f2 = bld.op("v", lambda v: v.tensor_tensor(sfull[:, 512:1024], ps[1][:],
                                                           srep_sb[:, 512:1024], OP.add),
                            deps=[evS[1], e2])
                cb["sfull"] = f2
                pbank_ev[0] = f1
                pbank_ev[1] = f2
                evsi = bld.op("v", lambda v: v.reciprocal(sinv[:], sfull[:]),
                              deps=[f1, f2, cb["sinv"]])
                cb["sinv"] = evsi
                kq_evac = {}
                for grp in range(4):
                    bi = 4 + grp % 2
                    bank = ps[bi]
                    def mmkq(t, tc=tc, grp=grp, bank=bank):
                        mm = None
                        for hh in range(4):
                            h_ = grp * 4 + hh
                            mm = t.matmul(bank[:, hh * 128:(hh + 1) * 128],
                                          kh[h_][:, tc * 128:(tc + 1) * 128],
                                          qh[h_][:, tc * 128:(tc + 1) * 128],
                                          start=True, stop=True)
                        return mm
                    hdeps = [k_wr[grp * 4], k_wr[grp * 4 + 2], q_wr[grp * 4], q_wr[grp * 4 + 2]]
                    e = bld.op("t", mmkq, deps=hdeps + [pbank_ev[bi]])
                    slot = grp % 2
                    ee = bld.op("v", lambda v, slot=slot, bank=bank: v.tensor_tensor(
                        kqc[:, slot * 512:(slot + 1) * 512], bank[:], cU4[:], OP.mult),
                        deps=[e, ev_u4, cb[f"pk{slot}"]])
                    pbank_ev[bi] = ee
                    cb[f"pk{grp % 2}"] = ee
                    kq_evac[grp] = ee
                ok_ev = []
                for half in range(2):
                    bank = ps[half]
                    def mmok(t, tc=tc, half=half, bank=bank):
                        mm = None
                        for hh in range(8):
                            h_ = half * 8 + hh
                            slot = (h_ // 4) % 2
                            mm = t.matmul(bank[:, hh * 64:(hh + 1) * 64],
                                          kqc[:, slot * 512 + (h_ % 4) * 128: slot * 512 + (h_ % 4 + 1) * 128],
                                          a_tm[:, tc * 1024 + h_ * 64: tc * 1024 + (h_ + 1) * 64],
                                          start=True, stop=False)
                            mm = t.matmul(bank[:, hh * 64:(hh + 1) * 64],
                                          qh[h_][:, tc * 128:(tc + 1) * 128],
                                          hk_bfs[rp][:, h_ * 64:(h_ + 1) * 64],
                                          start=False, stop=True)
                        return mm
                    e = bld.op("t", mmok, deps=[kq_evac[half * 2], kq_evac[half * 2 + 1],
                                                hkbf_wr[rp], pbank_ev[half]] + adep)
                    ok_ev.append(e)
                hkbf_rd[rp] = ok_ev[-1]
                g1 = bld.op("v", lambda v: v.scalar_tensor_tensor(
                    okt[:, 0:512], ps[0][:], SCALE, sinv[:, 0:512], OP.mult, OP.mult),
                    deps=[ok_ev[0], evsi, cb["okt"]])
                g2 = bld.op("v", lambda v: v.scalar_tensor_tensor(
                    okt[:, 512:1024], ps[1][:], SCALE, sinv[:, 512:1024], OP.mult, OP.mult),
                    deps=[ok_ev[1], evsi])
                cb["okt"] = g2
                pbank_ev[0] = g1
                pbank_ev[1] = g2
                m1 = bld.op("v", lambda v: v.tensor_reduce(
                    mx, okt.rearrange("p (h m) -> p h m", m=64), AX.X, OP.max, negate=True),
                    deps=[g1, g2, cb["mx"]])
                cb["mx"] = m1
                m2 = bld.op("v", lambda v: v.tensor_tensor(
                    okt.rearrange("p (h m) -> p h m", m=64),
                    okt.rearrange("p (h m) -> p h m", m=64),
                    mx.rearrange("p h -> p h ()").broadcast_to([128, 16, 64]), OP.add),
                    deps=[m1])
                m3 = bld.op("s", lambda s: s.activation(esb[:], okt[:], AF.Exp), deps=[m2])
                m4 = bld.op("v", lambda v: v.tensor_reduce(
                    sm, esb.rearrange("p (h m) -> p h m", m=64), AX.X, OP.add),
                    deps=[m3, cb["sm"]])
                cb["sm"] = m4
                m5 = bld.op("v", lambda v: v.reciprocal(smr, sm), deps=[m4, cb["smr"]])
                cb["smr"] = m5
                m6 = bld.op("v", lambda v: v.tensor_tensor(
                    esb.rearrange("p (h m) -> p h m", m=64),
                    esb.rearrange("p (h m) -> p h m", m=64),
                    smr.rearrange("p h -> p h ()").broadcast_to([128, 16, 64]), OP.mult),
                    deps=[m5])
                m7 = bld.op("v", lambda v: v.tensor_tensor(u_tm[:], esb[:], sinv[:], OP.mult),
                            deps=[m6, cb["u"]])
                cb["u"] = m7
                tr = {}

                def emit_tr(kind, tc=tc, adep=adep, m7=m7):
                    if kind == "a":
                        srcfn = lambda h_: a_tm[:, tc * 1024 + h_ * 64: tc * 1024 + (h_ + 1) * 64]
                        dstlist, evlist, osz, srcdep = afm, afm_ev, 128, adep
                    elif kind == "u":
                        srcfn = lambda h_: u_tm[:, h_ * 64:(h_ + 1) * 64]
                        dstlist, evlist, osz, srcdep = ufm, ufm_ev, 128, [m7]
                    else:
                        srcfn = lambda h_: kh[h_][:, tc * 128:(tc + 1) * 128]
                        dstlist, evlist, osz = ktm, ktm_ev, 64
                        srcdep = [k_wr[0], k_wr[2], k_wr[4], k_wr[6], k_wr[8], k_wr[10], k_wr[12], k_wr[14]]
                    oparts = 64 if kind != "k" else 128
                    for h_ in range(16):
                        tb = h_ % 2
                        slot = (h_ // 2) % 4
                        idap = cEye[0:64, 0:64] if kind == "k" else cEye[:]
                        def trf(t, h_=h_, tb=tb, slot=slot, srcfn=srcfn, oparts=oparts, osz=osz, idap=idap):
                            return t.transpose(psT[tb][0:oparts, slot * 256: slot * 256 + osz],
                                               srcfn(h_), idap)
                        evt = bld.op("t", trf, deps=srcdep + [ptbank_ev.get(tb)])
                        eng = "v" if h_ % 2 == 0 else "s"
                        def evf(e, h_=h_, tb=tb, slot=slot, dstlist=dstlist, oparts=oparts, osz=osz):
                            srcv = psT[tb][0:oparts, slot * 256: slot * 256 + osz]
                            dstv = dstlist[h_][:]
                            return e.tensor_copy(dstv, srcv) if h_ % 2 == 0 else e.copy(dstv, srcv)
                        eve = bld.op(eng, evf, deps=[evt, evlist[h_]])
                        evlist[h_] = eve
                        ptbank_ev[tb] = eve
                        tr[(kind, h_)] = eve

                emit_tr("a")
                emit_tr("k")

                hkmm = {}
                for half in range(2):
                    bi = 2 + half
                    bank = ps[bi]
                    def mmhk(t, tc=tc, half=half, bank=bank):
                        mm = None
                        for hh in range(8):
                            h_ = half * 8 + hh
                            mm = t.matmul(bank[0:64, hh * 64:(hh + 1) * 64],
                                          ktm[h_][:],
                                          a_tm[:, tc * 1024 + h_ * 64: tc * 1024 + (h_ + 1) * 64],
                                          start=True, stop=True)
                        return mm
                    hkmm[half] = bld.op("t", mmhk, deps=[tr[("k", half * 8 + hh)] for hh in range(8)] +
                                        adep + [pbank_ev[bi]])
                e_add = []
                for half in range(2):
                    e1_ = bld.op("v", lambda v, half=half: v.tensor_tensor(
                        hk_sb[:, half * 512:(half + 1) * 512], hk_sb[:, half * 512:(half + 1) * 512],
                        ps[2 + half][0:64, :], OP.add), deps=[hkmm[half], hk_ev[0]])
                    e_add.append(e1_)
                    pbank_ev[2 + half] = e1_
                hk_ev[0] = e_add[-1]
                hkbf_wr[wpar] = bld.op("v", lambda v: v.tensor_copy(hk_bfs[wpar][:], hk_sb[:]),
                                       deps=e_add + [hkbf_rd[wpar], hkbf_wr[wpar]])
                hvmm = {}
                for qtr in range(4):
                    bi = banksel[qtr]
                    bank = ps[bi]
                    def mmhv(t, tc=tc, qtr=qtr, bank=bank):
                        mm = None
                        for hh in range(4):
                            h_ = qtr * 4 + hh
                            mm = t.matmul(bank[0:64, hh * 128:(hh + 1) * 128],
                                          a_tm[:, tc * 1024 + h_ * 64: tc * 1024 + (h_ + 1) * 64],
                                          v_st[:, tc * 2048 + h_ * 128: tc * 2048 + (h_ + 1) * 128],
                                          start=True, stop=True)
                        return mm
                    hvmm[qtr] = bld.op("t", mmhv, deps=adep + vdep + [pbank_ev[bi]])
                e_add = []
                for qtr in range(4):
                    e1_ = bld.op("v", lambda v, qtr=qtr: v.tensor_tensor(
                        hv_sb[:, qtr * 512:(qtr + 1) * 512], hv_sb[:, qtr * 512:(qtr + 1) * 512],
                        ps[banksel[qtr]][0:64, :], OP.add), deps=[hvmm[qtr], hv_ev[0]])
                    e_add.append(e1_)
                    pbank_ev[banksel[qtr]] = e1_
                hv_ev[0] = e_add[-1]
                hvbf_wr[wpar] = bld.op("v", lambda v: v.tensor_copy(hv_bfs[wpar][:], hv_sb[:]),
                                       deps=e_add + [hvbf_rd[wpar], hvbf_wr[wpar]])

                emit_tr("u")
                pt_evac = {}
                ovi_ev = {}
                o_evs = [None] * 4

                def emit_pt(grp):
                    bi = 2 + grp % 2
                    bank = ps[bi]
                    def mmpt(t, grp=grp, bank=bank):
                        mm = None
                        for hh in range(4):
                            h_ = grp * 4 + hh
                            mm = t.matmul(bank[:, hh * 128:(hh + 1) * 128],
                                          afm[h_][:], ufm[h_][:], start=True, stop=True)
                        return mm
                    hdeps = [tr[("a", grp * 4 + hh)] for hh in range(4)] + \
                            [tr[("u", grp * 4 + hh)] for hh in range(4)]
                    e = bld.op("t", mmpt, deps=hdeps + [pbank_ev[bi]])
                    slot = grp % 2
                    ee = bld.op("v", lambda v, slot=slot, bank=bank: v.tensor_tensor(
                        ptc[:, slot * 512:(slot + 1) * 512], bank[:], cU4[:], OP.mult),
                        deps=[e, cb[f"pk{slot}"]])
                    pbank_ev[bi] = ee
                    cb[f"pk{slot}"] = ee
                    pt_evac[grp] = ee

                def emit_ov(qtr):
                    bi = banksel[qtr]
                    bank = ps[bi]
                    slot = qtr % 2
                    def mmov(t, tc=tc, qtr=qtr, bank=bank, slot=slot):
                        mm = None
                        for hh in range(4):
                            h_ = qtr * 4 + hh
                            mm = t.matmul(bank[:, hh * 128:(hh + 1) * 128],
                                          ptc[:, slot * 512 + hh * 128: slot * 512 + (hh + 1) * 128],
                                          v_st[:, tc * 2048 + h_ * 128: tc * 2048 + (h_ + 1) * 128],
                                          start=True, stop=True)
                        return mm
                    e = bld.op("t", mmov, deps=[pt_evac[qtr]] + vdep + [pbank_ev[bi]])
                    ovi_ev[qtr] = e
                    eo = bld.op("v", lambda v, tc=tc, qtr=qtr, bank=bank: v.tensor_copy(
                        o_st[:, tc * 2048 + qtr * 512: tc * 2048 + (qtr + 1) * 512], bank[:]),
                        deps=[e, o_wr.get((tc, qtr))])
                    pbank_ev[bi] = eo
                    o_evs[qtr] = eo

                emit_pt(0)
                emit_pt(1)
                emit_ov(0)
                emit_pt(2)
                emit_ov(1)
                emit_pt(3)
                emit_ov(2)
                emit_ov(3)
                for qtr in range(4):
                    bi = banksel[qtr]
                    bank = ps[bi]
                    def mmoj(t, qtr=qtr, bank=bank, rp=rp):
                        mm = None
                        for hh in range(4):
                            h_ = qtr * 4 + hh
                            mm = t.matmul(bank[:, hh * 128:(hh + 1) * 128],
                                          ufm[h_][:], hv_bfs[rp][:, h_ * 128:(h_ + 1) * 128],
                                          start=True, stop=True)
                        return mm
                    e = bld.op("t", mmoj, deps=[tr[("u", qtr * 4 + hh)] for hh in range(4)] +
                               [hvbf_wr[rp], o_evs[qtr], pbank_ev[bi]])
                    hvbf_rd[rp] = e
                    eo = bld.op("v", lambda v, tc=tc, qtr=qtr, bank=bank: v.tensor_tensor(
                        o_st[:, tc * 2048 + qtr * 512: tc * 2048 + (qtr + 1) * 512],
                        o_st[:, tc * 2048 + qtr * 512: tc * 2048 + (qtr + 1) * 512],
                        bank[:], OP.add), deps=[e, o_evs[qtr]])
                    pbank_ev[bi] = eo
                    o_wr[(tc, qtr)] = eo
                odep = [o_wr[(tc, qtr)] for qtr in range(4)]
                p1a = bld.op("v", lambda v, tc=tc: v.tensor_tensor(
                    o2[:], o_st[:, tc * 2048: tc * 2048 + 1024],
                    o_st[:, tc * 2048: tc * 2048 + 1024], OP.mult), deps=odep[0:2] + [cb["u"]])
                p2a = bld.op("v", lambda v, tc=tc: v.tensor_reduce(
                    rr_st[:, tc * 16: tc * 16 + 8], o2.rearrange("p (h d) -> p h d", d=128),
                    AX.X, OP.add), deps=[p1a, rr_war[0]])
                p1b = bld.op("v", lambda v, tc=tc: v.tensor_tensor(
                    o2[:], o_st[:, tc * 2048 + 1024: tc * 2048 + 2048],
                    o_st[:, tc * 2048 + 1024: tc * 2048 + 2048], OP.mult), deps=odep[2:4] + [p2a])
                cb["u"] = p1b
                p2b = bld.op("v", lambda v, tc=tc: v.tensor_reduce(
                    rr_st[:, tc * 16 + 8: tc * 16 + 16], o2.rearrange("p (h d) -> p h d", d=128),
                    AX.X, OP.add), deps=[p1b])
                p3 = bld.op("v", lambda v, tc=tc: v.tensor_scalar(
                    rr_st[:, tc * 16:(tc + 1) * 16], rr_st[:, tc * 16:(tc + 1) * 16],
                    1.0 / 128, 1e-5, OP.mult, OP.add), deps=[p2a, p2b])
                p4 = bld.op("s", lambda s, tc=tc: s.activation(
                    rr_st[:, tc * 16:(tc + 1) * 16], rr_st[:, tc * 16:(tc + 1) * 16], AF.Sqrt),
                    deps=[p3])
                p5 = bld.op("v", lambda v, tc=tc: v.reciprocal(
                    rr_st[:, tc * 16:(tc + 1) * 16], rr_st[:, tc * 16:(tc + 1) * 16]), deps=[p4])
                rr_war[0] = p5

            def evac_g(tc, ng, bank, evm):
                e1 = bld.op("s", lambda s, bank=bank: s.activation(sgt, bank[:], AF.Silu),
                            deps=[evm, stmp_ev[0]])
                e2 = bld.op("v", lambda v, ng=ng: v.tensor_tensor(
                    sggt, sgt, cGnw[:, ng * 512:(ng + 1) * 512], OP.mult),
                    deps=[e1, ev_g3, cb["pk1"]])
                stmp_ev[0] = e2
                cb["pk1"] = e2
                e3 = bld.op("v", lambda v, tc=tc, ng=ng: v.tensor_tensor(
                    ogt, o_st[:, tc * 2048 + ng * 512: tc * 2048 + ng * 512 + 512], sggt, OP.mult),
                    deps=[e2, o_wr[(tc, ng)], cb["ogt"]])
                cb["sggt"] = e3
                e4 = bld.op("v", lambda v, tc=tc, ng=ng: v.tensor_tensor(
                    ogt.rearrange("p (h d) -> p h d", d=128),
                    ogt.rearrange("p (h d) -> p h d", d=128),
                    rr_st[:, tc * 16 + ng * 4: tc * 16 + (ng + 1) * 4]
                    .rearrange("p h -> p h ()").broadcast_to([128, 4, 128]), OP.mult),
                    deps=[e3, rr_war[0]])
                last = e4
                for jj in range(4):
                    j = ng * 4 + jj
                    tb = jj % 2
                    def trf(t, jj=jj, tb=tb):
                        return t.transpose(psT[tb][:, 512 * (jj // 2): 512 * (jj // 2) + 128],
                                           ogt[:, jj * 128:(jj + 1) * 128], cEye[:])
                    evt = bld.op("t", trf, deps=[e4, ptbank_ev.get(tb)])
                    eng = "v" if jj % 2 == 0 else "s"
                    def evf(e, j=j, jj=jj, tb=tb, tc=tc):
                        srcv = psT[tb][:, 512 * (jj // 2): 512 * (jj // 2) + 128]
                        dstv = og_fm[:, j * 512 + tc * 128: j * 512 + tc * 128 + 128]
                        return e.tensor_copy(dstv, srcv) if jj % 2 == 0 else e.copy(dstv, srcv)
                    eve = bld.op(eng, evf, deps=[evt, og_wr.get((tc, j))])
                    og_wr[(tc, j)] = eve
                    ptbank_ev[tb] = eve
                    last = eve
                cb["ogt"] = last
                return last
            proj_tm(xnb, xnb_ev, wg_b, 2048, wbufs, wbuf_ev, evac_g)

            ogdep = [og_wr[(tc, j)] for tc in range(4) for j in range(16)]
            for ng in range(4):
                sl = ng % 2
                rr_ = wo_b[:, ng * 512:(ng + 1) * 512].rearrange("(dc p) n -> p dc n", p=128)
                weva = bld.dma("d", wbufs[sl][0].rearrange("p (dc n) -> p dc n", n=512),
                               rr_[:, 0:8, :], deps=[wbuf_ev[sl]])
                wevb = bld.dma("d", wbufs[sl][1].rearrange("p (dc n) -> p dc n", n=512),
                               rr_[:, 8:16, :], deps=[wbuf_ev[sl]])
                for tc in range(4):
                    bi = 4 + (ng * 4 + tc) % 2
                    bank = ps[bi]
                    def mmwo(t, tc=tc, sl=sl, bank=bank):
                        mm = None
                        for dc in range(16):
                            wv_ = wbufs[sl][dc // 8][:, (dc % 8) * 512:(dc % 8 + 1) * 512]
                            mm = t.matmul(bank[:],
                                          og_fm[:, dc * 512 + tc * 128: dc * 512 + tc * 128 + 128],
                                          wv_, start=(dc == 0), stop=(dc == 15))
                        return mm
                    evm = bld.op("t", mmwo, deps=[weva, wevb, pbank_ev[bi]] + ogdep)
                    asl = tc % 2
                    eve = bld.op("v", lambda v, bank=bank, asl=asl: v.tensor_copy(attb[asl], bank[:]),
                                 deps=[evm, attb_ev[asl]])
                    pbank_ev[bi] = eve
                    evd = bld.dma("g", att_scr[st * 512 + tc * 128: st * 512 + (tc + 1) * 128,
                                              ng * 512:(ng + 1) * 512], attb[asl], deps=[eve])
                    attb_ev[asl] = evd
                    stmp_ev[asl] = evd
                wbuf_ev[sl] = evm

        for st in range(st_own):
            emit_main_st(st)
        bld.barrier()
        ar.reset()

        wgb2 = [ar.abf(128, 16 * 1024) for _ in range(2)]
        wub2 = [ar.abf(128, 16 * 1024) for _ in range(2)]
        xnb2 = [ar.abf(128, 16 * 512) for _ in range(2)]
        hbuf = [ar.abf(128, 512) for _ in range(2)]
        sgb = [ar.af32(128, 512) for _ in range(2)]
        hsq = [ar.abf(128, 512) for _ in range(2)]
        rtmp = ar.af32(1, 512)
        ones_bf = ar.abf(128, 1)
        ev_ob = bld.op("v", lambda v: v.memset(ones_bf, 1.0))
        pbank_ev = {i: None for i in range(6)}
        pbank_ev["rms"] = None
        pbank_ev["rms2"] = None
        hbuf_ev = [None, None]
        sgb_ev = [None, None]
        hsq_ev = [None, None]
        rtmp_ev = [None]
        rmsacc_ev = [None]
        pend_rms = [None]
        xnb2_ev = [None, None]
        xnb2_dma = [None, None]
        wslab_ev = [None, None]
        wslab_dma = [None, None]
        def load_whalf(hs, slot, deps):
            evs = []
            for dc in range(16):
                evs.append(bld.op("d", (lambda out, in_: (lambda eng: eng.dma_start(
                    out=out, in_=in_, transpose=True)))(
                    wgb2[slot][:, dc * 1024:(dc + 1) * 1024],
                    wgu_q[hs * 1024:(hs + 1) * 1024, dc * 128:(dc + 1) * 128]),
                    deps=deps, dma=True))
                evs.append(bld.op("d", (lambda out, in_: (lambda eng: eng.dma_start(
                    out=out, in_=in_, transpose=True)))(
                    wub2[slot][:, dc * 1024:(dc + 1) * 1024],
                    wgu_q[HF + hs * 1024: HF + (hs + 1) * 1024, dc * 128:(dc + 1) * 128]),
                    deps=deps, dma=True))
            return evs[-1]

        def load_xnb2(st, slot, deps):
            evs = []
            for dc in range(16):
                evs.append(bld.dma("d", xnb2[slot][:, dc * 512:(dc + 1) * 512],
                                   xn_own[dc * 128:(dc + 1) * 128, st * 512:(st + 1) * 512],
                                   deps=deps))
            return evs[-1]

        wslab_dma[0] = load_whalf(0, 0, [None])
        for hs in range(8):
            wsl = hs % 2
            wev = wslab_dma[wsl]
            if hs + 1 < 8:
                wslab_dma[(hs + 1) % 2] = load_whalf(hs + 1, (hs + 1) % 2, [wslab_ev[(hs + 1) % 2]])
            for st in range(st_own):
                xsl = st % 2
                if hs == 0 and st == 0:
                    xnb2_dma[0] = load_xnb2(0, 0, [None])
                xnb_ev = xnb2_dma[xsl]
                if st + 1 < st_own:
                    xnb2_dma[(st + 1) % 2] = load_xnb2(st + 1, (st + 1) % 2,
                                                       [xnb2_ev[(st + 1) % 2]])
                elif hs + 1 < 8:
                    xnb2_dma[(st_own) % 2] = load_xnb2(0, (st_own) % 2, [xnb2_ev[(st_own) % 2]])
                last_eu = [None]
                for hc in range(8):
                    gb = ps[(hc % 3) * 2]
                    ub = ps[(hc % 3) * 2 + 1]
                    def mmg(t, hc=hc, gb=gb, wsl=wsl, xsl=xsl):
                        mm = None
                        for dc in range(16):
                            mm = t.matmul(gb[:],
                                          wgb2[wsl][:, dc * 1024 + hc * 128: dc * 1024 + hc * 128 + 128],
                                          xnb2[xsl][:, dc * 512:(dc + 1) * 512],
                                          start=(dc == 0), stop=(dc == 15))
                        return mm
                    def mmu(t, hc=hc, ub=ub, wsl=wsl, xsl=xsl):
                        mm = None
                        for dc in range(16):
                            mm = t.matmul(ub[:],
                                          wub2[wsl][:, dc * 1024 + hc * 128: dc * 1024 + hc * 128 + 128],
                                          xnb2[xsl][:, dc * 512:(dc + 1) * 512],
                                          start=(dc == 0), stop=(dc == 15))
                        return mm
                    eg_ = bld.op("t", mmg, deps=[wev, xnb_ev, pbank_ev[(hc % 3) * 2]])
                    eu_ = bld.op("t", mmu, deps=[wev, xnb_ev, pbank_ev[(hc % 3) * 2 + 1]])
                    last_eu[0] = eu_
                    if pend_rms[0] is not None:
                        fn_, pend_rms[0] = pend_rms[0], None
                        fn_()
                    sl = hc % 2
                    e1 = bld.op("s", lambda s, sl=sl, gb=gb: s.activation(sgb[sl], gb[:], AF.Silu),
                                deps=[eg_, sgb_ev[sl]])
                    pbank_ev[(hc % 3) * 2] = e1
                    e2 = bld.op("v", lambda v, sl=sl, ub=ub: v.tensor_tensor(
                        hbuf[sl], sgb[sl], ub[:], OP.mult), deps=[e1, eu_, hbuf_ev[sl]])
                    sgb_ev[sl] = e2
                    pbank_ev[(hc % 3) * 2 + 1] = e2
                    e3 = bld.op("s", lambda s, sl=sl: s.activation(hsq[sl], hbuf[sl], AF.Square),
                                deps=[e2, hsq_ev[sl]])

                    def emit_e4(hc=hc, sl=sl, e3=e3, st=st):
                        e4 = bld.op("t", lambda t, sl=sl, hc=hc: t.matmul(
                            psT[0][:].bitcast(F32)[0:1, :], ones_bf, hsq[sl],
                            start=(hc == 0), stop=(hc == 7), skip_group_check=True),
                            deps=[e3, ev_ob] + ([pbank_ev["rms"]] if hc == 0 else []))
                        hsq_ev[sl] = e4
                        if hc == 7:
                            r1 = bld.op("v", lambda v: v.tensor_copy(
                                rtmp, psT[0][:].bitcast(F32)[0:1, :]),
                                deps=[e4, rtmp_ev[0]])
                            pbank_ev["rms"] = r1
                            r2s = []
                            for q4 in range(4):
                                e = bld.op("t", lambda t, q4=q4: t.matmul(
                                    psT[1][:].bitcast(F32)[:, q4:q4 + 1],
                                    rtmp[0:1, q4 * 128:(q4 + 1) * 128],
                                    cOne11[0:1, :], start=True, stop=True,
                                    skip_group_check=True),
                                    deps=[r1] + ([pbank_ev["rms2"]] if q4 == 0 else []))
                                r2s.append(e)
                            rtmp_ev[0] = r2s[-1]
                            r3 = bld.op("v", lambda v, st=st: v.tensor_tensor(
                                rmsacc[:, st * 4:(st + 1) * 4], rmsacc[:, st * 4:(st + 1) * 4],
                                psT[1][:].bitcast(F32)[:, 0:4], OP.add), deps=r2s + [rmsacc_ev[0]])
                            rmsacc_ev[0] = r3
                            pbank_ev["rms2"] = r3
                    pend_rms[0] = emit_e4
                    evd = bld.dma("g", h_scr[hs * 1024 + hc * 128: hs * 1024 + (hc + 1) * 128,
                                             st * 512:(st + 1) * 512], hbuf[sl], deps=[e2])
                    hbuf_ev[sl] = evd
                xnb2_ev[xsl] = last_eu[0]
            wslab_ev[wsl] = last_eu[0]
        if pend_rms[0] is not None:
            fn_, pend_rms[0] = pend_rms[0], None
            fn_()
        bld.barrier()
        ar.reset()

        EPS32 = float(np.finfo(np.float32).eps)
        q1_ = bld.op("v", lambda v: v.tensor_scalar(rstdt[:], rmsacc[:], 1.0 / HF, EPS32,
                                                    OP.mult, OP.add))
        q2_ = bld.op("s", lambda s: s.activation(rstdt[:], rstdt[:], AF.Sqrt), deps=[q1_])
        q3_ = bld.op("v", lambda v: v.reciprocal(rstdt[:], rstdt[:]), deps=[q2_])

        wpb = ar.abf(128, 64 * 1024)
        hstr = [ar.abf(128, 512) for _ in range(16)]
        xb2 = [ar.af32(128, 1024) for _ in range(2)]
        ab2 = [ar.af32(128, 1024) for _ in range(2)]
        ob2 = [ar.af32(128, 1024) for _ in range(2)]
        p4banks = [p[:] for p in ps] + [psT[0][:].bitcast(F32), psT[1][:].bitcast(F32)]
        hstr_ev = [None] * 16
        xb2_ev = [None, None]
        ab2_ev = [None, None]
        ob2_ev = [None, None]
        wpb_ev = [None]
        bank_ev = [None] * 8
        for dh in range(2):
            wevs = []
            for hc in range(64):
                wevs.append(bld.op("d", (lambda out, in_: (lambda eng: eng.dma_start(
                    out=out, in_=in_, transpose=True)))(
                    wpb[:, hc * 1024:(hc + 1) * 1024],
                    wp_q[dh * 1024:(dh + 1) * 1024, hc * 128:(hc + 1) * 128]),
                    deps=[wpb_ev[0]], dma=True))
            wev = wevs[-1]
            for st in range(st_own):
                mm_last = None
                for wave in range(8):
                    wp_ = wave % 2
                    hevs = []
                    for sl in range(8):
                        hc = wave * 8 + sl
                        hevs.append(bld.dma("d", hstr[wp_ * 8 + sl],
                                            h_scr[hc * 128:(hc + 1) * 128, st * 512:(st + 1) * 512],
                                            deps=[hstr_ev[wp_ * 8 + sl]]))
                    for tc in range(4):
                        def mmp(t, tc=tc, wave=wave, wp_=wp_):
                            mm = None
                            for sl in range(8):
                                hc = wave * 8 + sl
                                for nh in range(2):
                                    bank = p4banks[tc * 2 + nh]
                                    mm = t.matmul(bank,
                                                  hstr[wp_ * 8 + sl][:, tc * 128:(tc + 1) * 128],
                                                  wpb[:, hc * 1024 + nh * 512: hc * 1024 + nh * 512 + 512],
                                                  start=(hc == 0), stop=(hc == 63))
                            return mm
                        mm_last = bld.op("t", mmp, deps=hevs + [wevs[wave * 8 + 7]] +
                                         ([bank_ev[tc * 2], bank_ev[tc * 2 + 1]] if wave == 0 else []))
                    for sl in range(8):
                        hstr_ev[wp_ * 8 + sl] = mm_last
                for tc in range(4):
                    row0 = st * 512 + tc * 128
                    sl2 = tc % 2
                    evx = bld.dma("d", xb2[sl2], x_own[row0:row0 + 128, dh * 1024:(dh + 1) * 1024],
                                  deps=[xb2_ev[sl2]])
                    eva = bld.dma("d", ab2[sl2], att_scr[row0:row0 + 128, dh * 1024:(dh + 1) * 1024],
                                  deps=[ab2_ev[sl2]])
                    b0 = p4banks[tc * 2]
                    b1 = p4banks[tc * 2 + 1]
                    e1 = bld.op("v", lambda v, sl2=sl2, b0=b0, st=st, tc=tc: v.scalar_tensor_tensor(
                        ob2[sl2][:, 0:512], b0,
                        rstdt[:, st * 4 + tc: st * 4 + tc + 1],
                        ab2[sl2][:, 0:512], OP.mult, OP.add),
                        deps=[mm_last, eva, q3_, ob2_ev[sl2]])
                    e2 = bld.op("v", lambda v, sl2=sl2, b1=b1, st=st, tc=tc: v.scalar_tensor_tensor(
                        ob2[sl2][:, 512:1024], b1,
                        rstdt[:, st * 4 + tc: st * 4 + tc + 1],
                        ab2[sl2][:, 512:1024], OP.mult, OP.add),
                        deps=[mm_last, eva, e1])
                    e3 = bld.op("v", lambda v, sl2=sl2: v.tensor_tensor(
                        ob2[sl2], ob2[sl2], xb2[sl2], OP.add), deps=[e1, e2, evx])
                    xb2_ev[sl2] = e3
                    ab2_ev[sl2] = e3
                    bank_ev[tc * 2] = e3
                    bank_ev[tc * 2 + 1] = e3
                    evd = bld.dma("g", out[row0:row0 + 128, dh * 1024:(dh + 1) * 1024],
                                  ob2[sl2], deps=[e3])
                    ob2_ev[sl2] = evd
            wpb_ev[0] = mm_last
        bld.waitop("v", [ob2_ev[0], ob2_ev[1], ("g", bld.cnt["g"]), ("d", bld.cnt["d"])])
        bld.op("v", lambda v: v.engine_nop())

        block = stack.enter_context(nc.Block())

        def mkbody(qe):
            def body(eng):
                for waits, fn, inc in bld.q[qe]:
                    for (se, v) in waits:
                        eng.wait_ge(sems[se], v)
                    if fn is not None:
                        instr = fn(eng)
                        instr.then_inc(sems[qe], inc)
            return body

        block.sync(mkbody("d"))
        block.tensor(mkbody("t"))
        block.vector(mkbody("v"))
        block.scalar(mkbody("s"))
        block.gpsimd(mkbody("g"))

    return nc


def _consts():
    u = np.triu(np.ones((128, 128), np.float32)).astype(BF)
    eye = np.eye(128, dtype=BF)
    return u, eye


def kernel(**inputs):
    ins = {k: np.asarray(v) for k, v in inputs.items()}
    x = np.ascontiguousarray(ins["L0_attn_hidden_states"].reshape(32768, 2048))
    u, eye = _consts()
    nc = build(TOK=4096, PRE=4096)
    in_maps = []
    for c in range(8):
        b, half = c // 2, c % 2
        in_maps.append({
            "x_own": np.ascontiguousarray(x[c * 4096:(c + 1) * 4096]),
            "x_pre": np.ascontiguousarray(x[b * 8192: b * 8192 + 4096]),
            "pbias": np.full((128, 1), -32.0 if half == 1 else -100000.0, np.float32),
            "wq": ins["L0_attn_Wq"], "wk": ins["L0_attn_Wk"], "wv": ins["L0_attn_Wv"],
            "wg": ins["L0_attn_Wg"], "ws": ins["L0_attn_Ws"], "wo": ins["L0_attn_Wo"],
            "wgu": ins["L0_ffn_w_gate_up"], "wp": ins["L0_ffn_w_proj"],
            "gnw": np.ascontiguousarray(ins["L0_attn_g_norm_weight"].reshape(1, 128)),
            "umask": u, "eye": eye,
        })
    res = run_bass_kernel_spmd(nc, in_maps, core_ids=list(range(8)))
    out = np.concatenate([np.asarray(r["out"]) for r in res.results], axis=0)
    return out.reshape(4, 8192, 2048)
